# version 04
# baseline (speedup 1.0000x reference)
import contextlib

import numpy as np
import ml_dtypes

import concourse.bass as bass
import concourse.mybir as mybir
from concourse.bass_utils import run_bass_kernel_spmd
from concourse.alu_op_type import AluOpType as OP

F32 = mybir.dt.float32
BF16 = mybir.dt.bfloat16
I8 = mybir.dt.int8
AF = mybir.ActivationFunctionType
AX = mybir.AxisListType

C = 1024
DC = C // 128
DI = 2048
DIC = DI // 128
DS = 16
DTR = 64
NH = 16
HD = 64
LS = 256
QT = 256
NSUB = QT // 128
EPS_LN = 1e-5
EPS_RMS = 1e-6
EPS_Q = 1e-6
NEG = -1.0e30

CHAIN = ("ffn0a", "attn0", "ffn0b", "rwkva", "mamba", "rwkvb")


def _cdiv(a, b):
    return (a + b - 1) // b


class Pool:

    def __init__(self, nc, ctx, name, nbytes):
        assert nbytes % 64 == 0
        self.t = ctx.enter_context(nc.sbuf_tensor(name, [128, nbytes], I8))
        self.name = name
        self.nbytes = nbytes
        self.cur = 0
        self.marks = []

    def alloc(self, dtype, shape):
        esz = mybir.dt.size(dtype)
        n = 1
        for s in shape:
            n *= s
        nb = _cdiv(n * esz, 64) * 64
        assert self.cur + nb <= self.nbytes, (
            f"pool {self.name} overflow: {self.cur}+{nb}>{self.nbytes}")
        ap = self.t[:, self.cur:self.cur + n * esz].bitcast(dtype)
        self.cur += nb
        if len(shape) > 1:
            dims = " ".join(f"d{i}" for i in range(len(shape)))
            kw = {f"d{i}": s for i, s in enumerate(shape)}
            ap = ap.rearrange(f"p ({dims}) -> p {dims}", **kw)
        return ap

    def mark(self):
        self.marks.append(self.cur)

    def release(self):
        self.cur = self.marks.pop()


class Sem:
    def __init__(self, nc, ctx, name, step=1):
        self.h = ctx.enter_context(nc.semaphore(name))
        self.n = 0
        self.step = step

    def inc(self, instr):
        instr.then_inc(self.h, self.step)
        self.n += self.step
        return self.n


class Ctx:
    def load(self, dst, src):
        self.nc.sync.wait_ge(self.ld.h, self.ld.n)
        self.nc.sync.dma_start(out=dst, in_=src).then_inc(self.ld.h, 16)
        self.ld.n += 16
        return self.ld.n

    def loadw(self, dst, src):
        self.nc.scalar.wait_ge(self.lw.h, self.lw.n)
        self.nc.scalar.dma_start(out=dst, in_=src).then_inc(self.lw.h, 16)
        self.lw.n += 16
        return self.lw.n

    def store(self, dst, src):
        self.nc.sync.wait_ge(self.st.h, self.st.n)
        self.nc.sync.dma_start(out=dst, in_=src).then_inc(self.st.h, 16)
        self.st.n += 16
        return self.st.n

    def slot_wait(self, eng, key):
        rec = self.slots.get(key)
        if rec is not None:
            sem, tick = rec
            eng.wait_ge(sem.h, tick)

    def slot_set(self, key, sem, tick):
        self.slots[key] = (sem, tick)

    def vpub(self):
        return self.sv.inc(self.nc.vector.drain())

    def spub(self):
        return self.ss.inc(self.nc.scalar.drain())

    def tpub(self):
        return self.stn.inc(self.nc.tensor.drain())

    def gpub(self):
        return self.sg.inc(self.nc.gpsimd.drain())



def kernel(**inputs):
    T = inputs["L0_attn_x"].shape[1]
    n_cores = 8
    x = np.ascontiguousarray(np.asarray(inputs["L0_attn_x"], dtype=np.float32))

    def t(a):
        return np.ascontiguousarray(np.asarray(a, dtype=np.float32).T)

    def cols(v, n):
        v = np.asarray(v, dtype=np.float32).reshape(n, 128)
        return np.ascontiguousarray(v.T)

    common = host_inputs(inputs, t, cols)
    nc = build_program(T=T)
    in_maps = []
    for c in range(n_cores):
        m = dict(common)
        m["x0"] = np.ascontiguousarray(x[c])
        in_maps.append(m)
    res = run_bass_kernel_spmd(nc, in_maps, core_ids=list(range(n_cores)))
    out = np.stack([np.asarray(res.results[c]["y"]) for c in range(n_cores)], axis=0)
    return out.astype(np.float32)


def host_inputs(inputs, t, cols):
    common = {
        "wq_T": np.ascontiguousarray(np.asarray(inputs["L0_ffn_weight_q"]).T),
        "w_qkv": np.ascontiguousarray(np.asarray(inputs["L0_attn_w_qkv"], dtype=np.float32)),
        "w_proj": np.ascontiguousarray(np.asarray(inputs["L0_attn_w_proj"], dtype=np.float32)),
        "lnw_c": cols(inputs["L0_attn_ln_w"], DC),
        "wsc_c": cols(inputs["L0_ffn_weight_scale"], DC),
        "fbias_c": cols(inputs["L0_ffn_bias"], DC),
        "fbias_r": np.asarray(inputs["L0_ffn_bias"], np.float32).reshape(1, C),
        "gamma_r": np.asarray(inputs["L1_attn_gamma"], np.float32).reshape(1, C),
        "beta_r": np.asarray(inputs["L1_attn_beta"], np.float32).reshape(1, C),
        "outb_r": np.asarray(inputs["L1_attn_out_b"], np.float32).reshape(1, C),
        "wr_T": t(inputs["L1_ffn_W_r"]),
        "wk_T": t(inputs["L1_ffn_W_k"]),
        "wv_T": t(inputs["L1_ffn_W_v"]),
        "mur_c": cols(inputs["L1_ffn_mu_r"], DC),
        "muk_c": cols(inputs["L1_ffn_mu_k"], DC),
        "outw_T": t(inputs["L1_attn_out_w"]),
    }
    for d in ("f", "b"):
        common[f"inw_T_{d}"] = t(inputs[f"L1_attn_in_w_{d}"])
        xp = t(inputs[f"L1_attn_xproj_w_{d}"])
        xpp = np.zeros((DI, 128), np.float32)
        xpp[:, 0:DTR + DS] = xp[:, 0:DTR + DS]
        xpp[:, 96:96 + DS] = xp[:, DTR + DS:]
        common[f"xproj_T_{d}"] = xpp
        common[f"dtw_T_{d}"] = t(inputs[f"L1_attn_dt_w_{d}"])
        common[f"mout_{d}"] = np.ascontiguousarray(
            np.asarray(inputs[f"L1_attn_mout_w_{d}"], np.float32))
        common[f"convw_{d}"] = np.ascontiguousarray(
            np.asarray(inputs[f"L1_attn_conv_w_{d}"], np.float32).reshape(DI, 4)
            .reshape(DIC, 128, 4).transpose(1, 0, 2))
        common[f"convb_c_{d}"] = cols(inputs[f"L1_attn_conv_b_{d}"], DIC)
        common[f"dtb_c_{d}"] = cols(inputs[f"L1_attn_dt_b_{d}"], DIC)
        common[f"dd_c_{d}"] = cols(inputs[f"L1_attn_D_{d}"], DIC)
        common[f"alog_{d}"] = np.ascontiguousarray(
            np.asarray(inputs[f"L1_attn_A_log_{d}"], np.float32)
            .reshape(DIC, 128, DS).transpose(1, 0, 2).reshape(128, DIC * DS))

    ident = np.eye(128)
    prev = ident[:, ::-1]
    tri0 = np.where(np.arange(LS)[None, :] <= np.arange(128)[:, None], 0.0, NEG)
    tri1 = np.where(np.arange(LS)[None, :] <= 128 + np.arange(128)[:, None], 0.0, NEG)
    sel = np.zeros((DS, DS * 128), np.float32)
    for s in range(DS):
        sel[s, s * 128:(s + 1) * 128] = 1.0
    common.update({
        "ident_bf": ident.astype(ml_dtypes.bfloat16),
        "ident_f": ident.astype(np.float32),
        "prev_bf": prev.astype(ml_dtypes.bfloat16),
        "prev_f": prev.astype(np.float32),
        "ones_bf": np.ones((1, 128), ml_dtypes.bfloat16),
        "ones_col_bf": np.ones((128, 1), ml_dtypes.bfloat16),
        "ones_f": np.ones((1, 128), np.float32),
        "tri0": tri0.astype(np.float32),
        "tri1": tri1.astype(np.float32),
        "sel_bf": sel.astype(ml_dtypes.bfloat16),
    })
    return common



def build_program(T=8192, chain=CHAIN, taps=(), dbg_level=99):
    nc = bass.Bass("TRN2", target_bir_lowering=False)
    NT = T // QT
    cx = Ctx()
    cx.dbg = dbg_level
    cx.nc = nc
    cx.T = T
    cx.NT = NT
    cx.chain = chain
    cx.taps = set(taps)
    cx.tap_dram = {}
    cx.slots = {}

    dram = {}

    def din(name, shape, dt):
        dram[name] = nc.dram_tensor(name, shape, dt, kind="ExternalInput").ap()

    din("x0", [T, C], F32)
    din("wq_T", [C, C], I8)
    din("w_qkv", [C, 3 * C], F32)
    din("w_proj", [C, C], F32)
    din("lnw_c", [128, DC], F32)
    din("wsc_c", [128, DC], F32)
    din("fbias_c", [128, DC], F32)
    din("fbias_r", [1, C], F32)
    din("gamma_r", [1, C], F32)
    din("beta_r", [1, C], F32)
    din("outb_r", [1, C], F32)
    din("wr_T", [C, C], F32)
    din("wk_T", [C, C], F32)
    din("wv_T", [C, C], F32)
    din("mur_c", [128, DC], F32)
    din("muk_c", [128, DC], F32)
    din("outw_T", [DI, C], F32)
    for d in ("f", "b"):
        din(f"inw_T_{d}", [C, 4 * C], F32)
        din(f"xproj_T_{d}", [DI, 128], F32)
        din(f"dtw_T_{d}", [DTR, DI], F32)
        din(f"mout_{d}", [C, DI], F32)
        din(f"convw_{d}", [128, DIC, 4], F32)
        din(f"convb_c_{d}", [128, DIC], F32)
        din(f"dtb_c_{d}", [128, DIC], F32)
        din(f"dd_c_{d}", [128, DIC], F32)
        din(f"alog_{d}", [128, DIC * DS], F32)
    din("ident_bf", [128, 128], BF16)
    din("ident_f", [128, 128], F32)
    din("prev_bf", [128, 128], BF16)
    din("prev_f", [128, 128], F32)
    din("ones_bf", [1, 128], BF16)
    din("ones_col_bf", [128, 1], BF16)
    din("ones_f", [1, 128], F32)
    din("tri0", [128, LS], F32)
    din("tri1", [128, LS], F32)
    din("sel_bf", [DS, DS * 128], BF16)

    dram["y"] = nc.dram_tensor("y", [T, C], F32, kind="ExternalOutput").ap()
    dram["s_x3"] = nc.dram_tensor("s_x3", [T, C], F32, kind="Internal").ap()
    dram["s_h"] = nc.dram_tensor("s_h", [T, C], BF16, kind="Internal").ap()
    dram["s_p5a"] = nc.dram_tensor("s_p5a", [T, C], F32, kind="Internal").ap()
    dram["s_p5b"] = nc.dram_tensor("s_p5b", [T, C], F32, kind="Internal").ap()
    dram["s_x5"] = nc.dram_tensor("s_x5", [T, C], F32, kind="Internal").ap()
    dram["b_wqkv"] = nc.dram_tensor("b_wqkv", [C, 3 * C], BF16, kind="Internal").ap()
    dram["b_inw_f"] = nc.dram_tensor("b_inw_f", [C, 4 * C], BF16, kind="Internal").ap()
    dram["b_inw_b"] = nc.dram_tensor("b_inw_b", [C, 4 * C], BF16, kind="Internal").ap()
    dram["b_outw_f"] = nc.dram_tensor("b_outw_f", [C, C], BF16, kind="Internal").ap()
    dram["b_outw_b"] = nc.dram_tensor("b_outw_b", [C, C], BF16, kind="Internal").ap()
    dram["b_weff_f"] = nc.dram_tensor("b_weff_f", [DI, C], BF16, kind="Internal").ap()
    dram["b_weff_b"] = nc.dram_tensor("b_weff_b", [DI, C], BF16, kind="Internal").ap()
    for tname in cx.taps:
        dram["tap_" + tname] = nc.dram_tensor(
            "tap_" + tname, [T, C], F32, kind="ExternalOutput").ap()
    cx.dram = dram

    with contextlib.ExitStack() as ctx:
        cx.ctx = ctx
        cx.wpool = Pool(nc, ctx, "wpool", 68 * 1024)
        cx.fpool = Pool(nc, ctx, "fpool", 52 * 1024)
        cx.bpool = Pool(nc, ctx, "bpool", 60 * 1024)
        cx.cpool = Pool(nc, ctx, "cpool", 19 * 1024)

        ps = []
        for i in range(5):
            ps.append(ctx.enter_context(nc.psum_tensor(f"psA{i}", [128, 512], F32)))
        cx.ps = ps
        cx.ps_tb = ctx.enter_context(nc.psum_tensor("ps_tb", [128, 2048], BF16))
        cx.ps_tf = ctx.enter_context(nc.psum_tensor("ps_tf", [128, 512], F32))
        cx.tb_par = 0

        cx.ld = Sem(nc, ctx, "ld", 16)
        cx.lw = Sem(nc, ctx, "lw", 16)
        cx.st = Sem(nc, ctx, "st", 16)
        cx.sv = Sem(nc, ctx, "sv")
        cx.ss = Sem(nc, ctx, "ss")
        cx.stn = Sem(nc, ctx, "stn")
        cx.sg = Sem(nc, ctx, "sg")

        cx.block = ctx.enter_context(nc.Block())
        _emit(cx)

    return nc


def _barrier(cx):
    nc = cx.nc
    cx.vpub()
    cx.spub()
    cx.tpub()
    cx.gpub()
    ticks = [(s, s.n) for s in (cx.sv, cx.ss, cx.stn, cx.sg, cx.ld, cx.st,
                                cx.lw)]
    for eng in (nc.sync, nc.vector, nc.scalar, nc.tensor, nc.gpsimd):
        for sem, n in ticks:
            eng.wait_ge(sem.h, n)
    cx.slots = {}


def _emit(cx):
    _load_consts(cx)
    _weight_prep(cx)
    _barrier(cx)
    last = {"ffn0a", "attn0", "ffn0b"} & set(cx.chain)
    p1_out = "y" if not ({"rwkva", "mamba", "rwkvb"} & set(cx.chain)) else "s_x3"
    _pass1(cx, out=p1_out)
    if "rwkva" in cx.chain:
        _barrier(cx)
        p2_out = "y" if "mamba" not in cx.chain else None
        _pass2(cx, out=p2_out)
    if "mamba" in cx.chain:
        _barrier(cx)
        _mamba_pass(cx, "f")
        _barrier(cx)
        _mamba_pass(cx, "b", out=("y" if "rwkvb" not in cx.chain else "s_x5"))
    if "rwkvb" in cx.chain:
        _barrier(cx)
        _pass5(cx)
    cx.nc.sync.wait_ge(cx.st.h, cx.st.n)



def _load_consts(cx):
    nc, d = cx.nc, cx.dram
    cc = {}
    full = [
        ("ident_bf", BF16, [128]), ("ident_f", F32, [128]),
        ("prev_bf", BF16, [128]), ("prev_f", F32, [128]),
        ("tri0", F32, [LS]), ("tri1", F32, [LS]),
        ("lnw_c", F32, [DC]), ("wsc_c", F32, [DC]), ("fbias_c", F32, [DC]),
        ("mur_c", F32, [DC]), ("muk_c", F32, [DC]),
        ("convw_f", F32, [DIC, 4]), ("convw_b", F32, [DIC, 4]),
        ("convb_c_f", F32, [DIC]), ("convb_c_b", F32, [DIC]),
        ("dtb_c_f", F32, [DIC]), ("dtb_c_b", F32, [DIC]),
        ("dd_c_f", F32, [DIC]), ("dd_c_b", F32, [DIC]),
        ("alog_f", F32, [DIC * DS]), ("alog_b", F32, [DIC * DS]),
        ("ones_col_bf", BF16, [1]),
    ]
    for name, dt, shape in full:
        buf = cx.cpool.alloc(dt, shape)
        cx.load(buf, d[name][:, :] if len(d[name].shape) == 2 else d[name][:, :, :])
        cc[name] = buf
    rows = [("ones_bf", BF16, 1, 128), ("ones_f", F32, 1, 128),
            ("sel_bf", BF16, DS, DS * 128)]
    for name, dt, np_, nf in rows:
        buf = cx.cpool.alloc(dt, [nf])
        cx.load(buf[0:np_], d[name][:, :])
        cc[name] = buf
    cx.cc = cc

    nc.scalar.wait_ge(cx.ld.h, cx.ld.n)
    for dgn in ("f", "b"):
        apos = cx.cpool.alloc(F32, [DIC * DS])
        nc.scalar.activation(out=apos, in_=cc[f"alog_{dgn}"], func=AF.Exp)
        cc[f"apos_{dgn}"] = apos.rearrange("p (c s) -> p c s", c=DIC)
        ndtb = cx.cpool.alloc(F32, [DIC])
        nc.scalar.mul(out=ndtb, in_=cc[f"dtb_c_{dgn}"], mul=-1.0)
        cc[f"ndtb_{dgn}"] = ndtb
    cx.spub()


def _build_bcast(cx, row_dram_name, pool, scale=None):
    nc, d, cc = cx.nc, cx.dram, cx.cc
    bc = pool.alloc(F32, [C])
    cx.fpool.mark()
    rowstage = cx.fpool.alloc(F32, [C])
    tl = cx.load(rowstage[0:1, :], d[row_dram_name][:, :])
    nc.tensor.wait_ge(cx.ld.h, tl)
    for j in range(C // 512):
        cx.slot_wait(nc.tensor, "b4")
        nc.tensor.matmul(cx.ps[4][:, :], cc["ones_f"][0:1, :],
                         rowstage[0:1, j * 512:(j + 1) * 512],
                         start=True, stop=True)
        tt = cx.tpub()
        nc.vector.wait_ge(cx.stn.h, tt)
        nc.vector.tensor_copy(out=bc[:, j * 512:(j + 1) * 512], in_=cx.ps[4][:, :])
        cx.slot_set("b4", cx.sv, cx.vpub())
    if scale is not None:
        nc.vector.tensor_scalar(out=bc, in0=bc, scalar1=scale, scalar2=None,
                                op0=OP.mult)
        cx.vpub()
    nc.sync.wait_ge(cx.stn.h, cx.stn.n)
    cx.fpool.release()
    return bc


def _weight_prep(cx):
    nc, d = cx.nc, cx.dram
    cx.fpool.mark()
    cx.bpool.mark()
    fbuf = cx.fpool.alloc(F32, [4096])
    bb = cx.bpool.alloc(BF16, [4096])
    jobs = [(d[f"inw_T_{g}"], d[f"b_inw_{g}"], C, 4 * C) for g in ("f", "b")]
    jobs += [(d["outw_T"][0:C, :], d["b_outw_f"], C, C),
             (d["outw_T"][C:2 * C, :], d["b_outw_b"], C, C),
             (d["w_qkv"], d["b_wqkv"], C, 3 * C)]
    for src, dst, rows, width in jobs:
        for r in range(rows // 128):
            t0 = cx.load(fbuf[:, 0:width], src[r * 128:(r + 1) * 128, :])
            nc.vector.wait_ge(cx.ld.h, t0)
            nc.vector.tensor_copy(out=bb[:, 0:width], in_=fbuf[:, 0:width])
            tick = cx.vpub()
            nc.sync.wait_ge(cx.sv.h, tick)
            t1 = cx.store(dst[r * 128:(r + 1) * 128, :], bb[:, 0:width])
            nc.sync.wait_ge(cx.st.h, t1)
    cx.fpool.release()
    cx.bpool.release()

    cx.wpool.mark()
    cx.fpool.mark()
    mo_f32 = cx.fpool.alloc(F32, [DI])
    mo_bf = cx.wpool.alloc(BF16, [DC, DI])
    ow_bf = cx.wpool.alloc(BF16, [DC, C])
    wout = cx.wpool.alloc(BF16, [C])
    for dgn in ("f", "b"):
        for kc in range(DC):
            nc.sync.wait_ge(cx.sv.h, cx.sv.n)
            t0 = cx.load(mo_f32, d[f"mout_{dgn}"][kc * 128:(kc + 1) * 128, :])
            nc.vector.wait_ge(cx.ld.h, t0)
            nc.vector.tensor_copy(out=mo_bf[:, kc], in_=mo_f32)
            cx.vpub()
            t1 = cx.load(ow_bf[:, kc], d[f"b_outw_{dgn}"][kc * 128:(kc + 1) * 128, :])
        tick = cx.vpub()
        nc.tensor.wait_ge(cx.sv.h, tick)
        nc.tensor.wait_ge(cx.ld.h, cx.ld.n)
        for mi in range(DIC):
            for nj in range(2):
                key = f"b{nj % 2}"
                cx.slot_wait(nc.tensor, key)
                pslot = cx.ps[nj % 2][:, :]
                for kc in range(DC):
                    nc.tensor.matmul(pslot, mo_bf[:, kc, mi * 128:(mi + 1) * 128],
                                     ow_bf[:, kc, nj * 512:(nj + 1) * 512],
                                     start=(kc == 0), stop=(kc == DC - 1))
                tt = cx.tpub()
                nc.scalar.wait_ge(cx.stn.h, tt)
                nc.scalar.copy(out=wout[:, nj * 512:(nj + 1) * 512], in_=pslot)
                cx.slot_set(key, cx.ss, cx.spub())
            nc.sync.wait_ge(cx.ss.h, cx.ss.n)
            ts = cx.store(d[f"b_weff_{dgn}"][mi * 128:(mi + 1) * 128, :], wout)
            nc.scalar.wait_ge(cx.st.h, ts)
    cx.wpool.release()
    cx.fpool.release()



def _ln_tm(cx, x_tm, xn_out, nsub, width=C, eps=EPS_LN, ret_ab=False):
    nc = cx.nc
    f = cx.fpool
    f.mark()
    s_sum = f.alloc(F32, [nsub])
    s_sq = f.alloc(F32, [nsub])
    junk = f.alloc(F32, [width])
    mean = f.alloc(F32, [nsub])
    nmrs = f.alloc(F32, [nsub])
    rstd = f.alloc(F32, [nsub])
    tmp = f.alloc(F32, [nsub])
    r2 = f.alloc(F32, [nsub])
    nc.vector.drain()
    nc.vector.wait_ge(cx.ss.h, cx.ss.n)
    nc.scalar.wait_ge(cx.sv.h, cx.sv.n)
    for i in range(nsub):
        nc.vector.reduce_sum(out=s_sum[:, i:i + 1], in_=x_tm[:, i], axis=AX.X)
        if i:
            nc.scalar.drain()
        nc.scalar.activation(out=junk, in_=x_tm[:, i], func=AF.Square,
                             accum_out=s_sq[:, i:i + 1])
    nc.vector.drain()
    tsq = cx.spub()
    nc.vector.wait_ge(cx.ss.h, tsq)
    inv_w = 1.0 / width
    nc.vector.tensor_scalar(out=mean, in0=s_sum, scalar1=inv_w, scalar2=None,
                            op0=OP.mult)
    nc.vector.drain()
    nc.vector.scalar_tensor_tensor(out=tmp, in0=mean, scalar=-1.0, in1=mean,
                                   op0=OP.mult, op1=OP.mult)
    nc.vector.drain()
    nc.vector.scalar_tensor_tensor(out=tmp, in0=s_sq, scalar=inv_w, in1=tmp,
                                   op0=OP.mult, op1=OP.add)
    nc.vector.drain()
    nc.vector.tensor_scalar(out=tmp, in0=tmp, scalar1=eps, scalar2=None,
                            op0=OP.add)
    tick_v = cx.vpub()
    nc.scalar.wait_ge(cx.sv.h, tick_v)
    nc.scalar.activation(out=rstd, in_=tmp, func=AF.Sqrt)
    tick_s = cx.spub()
    nc.vector.wait_ge(cx.ss.h, tick_s)
    nc.vector.reciprocal(out=rstd, in_=rstd)
    nc.vector.drain()
    nc.vector.tensor_tensor(out=r2, in0=rstd, in1=rstd, op=OP.mult)
    nc.vector.drain()
    nc.vector.tensor_tensor(out=r2, in0=r2, in1=tmp, op=OP.mult)
    nc.vector.drain()
    nc.vector.tensor_scalar(out=r2, in0=r2, scalar1=-0.5, scalar2=1.5,
                            op0=OP.mult, op1=OP.add)
    nc.vector.drain()
    nc.vector.tensor_tensor(out=rstd, in0=rstd, in1=r2, op=OP.mult)
    nc.vector.drain()
    nc.vector.scalar_tensor_tensor(out=nmrs, in0=mean, scalar=-1.0, in1=rstd,
                                   op0=OP.mult, op1=OP.mult)
    nc.vector.drain()
    tick_s = None
    if xn_out is not None:
        for i in range(nsub):
            nc.vector.tensor_scalar(out=xn_out[:, i],
                                    in0=x_tm[:, i], scalar1=rstd[:, i:i + 1],
                                    scalar2=nmrs[:, i:i + 1],
                                    op0=OP.mult, op1=OP.add)
        nc.vector.drain()
        tick_s = cx.vpub()
    if ret_ab:
        tick_v = cx.vpub()
        return tick_s, tick_v, rstd, nmrs
    f.release()
    return tick_s


def _tp128(cx, src_ap, consume, bf=True, perm=None, ident_slice=None):
    nc = cx.nc
    idn = cx.cc[("prev_" if perm == "rev" else "ident_") + ("bf" if bf else "f")]
    if ident_slice is not None:
        idn = idn[ident_slice, ident_slice]
    out_w = src_ap.shape[0]
    if bf:
        par = cx.tb_par
        cx.tb_par ^= 1
        key = ("tb", par)
        psl = cx.ps_tb[:, par * 1024:par * 1024 + out_w]
    else:
        key = ("tf", 0)
        psl = cx.ps_tf[:, 0:out_w]
    cx.slot_wait(nc.tensor, key)
    tp = nc.tensor.transpose(psl[0:src_ap.shape[-1], :], src_ap, idn)
    tick = cx.stn.inc(tp)
    cx.slot_set(key, *consume(psl[0:src_ap.shape[-1], :], tick))


def _transpose_to_fm(cx, src_tm, dst_fm, nsub, src_bf=True, perm=None,
                     n_feat=C, sub_map=None):
    nc = cx.nc
    fc = n_feat // 128
    for i in range(nsub):
        di = sub_map[i] if sub_map else i
        for j in range(fc):

            def consume(psl, tick, di=di, j=j):
                nc.scalar.wait_ge(cx.stn.h, tick)
                cp = nc.scalar.copy(out=dst_fm[:, j, di * 128:(di + 1) * 128],
                                    in_=psl)
                return cx.ss, cx.ss.inc(cp)

            _tp128(cx, src_tm[:, i, j * 128:(j + 1) * 128], consume,
                   bf=src_bf, perm=perm)
    return cx.ss.n


def _fm_matmul(cx, w_view, rhs_fm, kcs, mi, pslot, pkey, extra=None):
    nc = cx.nc
    cx.slot_wait(nc.tensor, pkey)
    n = len(kcs)
    mm = None
    for ii, kc in enumerate(kcs):
        mm = nc.tensor.matmul(pslot, w_view[:, kc, mi * 128:(mi + 1) * 128],
                              rhs_fm[:, kc], start=(ii == 0),
                              stop=(extra is None and ii == n - 1))
    if extra is not None:
        lhsT, rhs = extra
        mm = nc.tensor.matmul(pslot, lhsT, rhs, start=False, stop=True)
    return cx.stn.inc(mm)


def _residual_add_fm_delta(cx, delta_fm, x_tm, nsub, scale, extra_tm=None):
    nc = cx.nc
    for i in range(nsub):
        for j in range(DC):

            def consume(psl, tick, i=i, j=j):
                nc.vector.wait_ge(cx.stn.h, tick)
                dst = x_tm[:, i, j * 128:(j + 1) * 128]
                op = nc.vector.scalar_tensor_tensor(out=dst, in0=psl,
                                                    scalar=scale, in1=dst,
                                                    op0=OP.mult, op1=OP.add)
                if extra_tm is not None:
                    nc.vector.drain()
                    op = nc.vector.tensor_tensor(
                        out=dst, in0=dst,
                        in1=extra_tm[:, i, j * 128:(j + 1) * 128], op=OP.add)
                return cx.sv, cx.sv.inc(op)

            _tp128(cx, delta_fm[:, j, i * 128:(i + 1) * 128], consume, bf=False)
    return cx.sv.n



def _quant_ffn(cx, bufs, x_tm, wq_bf):
    nc = cx.nc
    cc = cx.cc
    xn1, xq_tm, xq_fm, out_fm, scr = bufs
    amax, xsch, t2, frac = scr
    _ln_tm(cx, x_tm, xn1, NSUB)
    if cx.dbg < 2:
        return
    for i in range(NSUB):
        nc.vector.reduce_max(out=amax[:, i:i + 1], in_=xn1[:, i], axis=AX.X,
                             apply_absolute_value=True)
    nc.vector.drain()
    nc.vector.tensor_scalar(out=amax, in0=amax, scalar1=EPS_Q, scalar2=1.0 / 127.0,
                            op0=OP.max, op1=OP.mult)
    nc.vector.drain()
    nc.vector.tensor_scalar(out=xsch, in0=amax, scalar1=0.5, scalar2=None,
                            op0=OP.mult)
    nc.vector.drain()
    nc.vector.reciprocal(out=amax, in_=amax)
    nc.vector.drain()
    i32 = frac.bitcast(mybir.dt.int32)
    fb = frac
    for i in range(NSUB):
        nc.vector.tensor_scalar(out=t2, in0=xn1[:, i], scalar1=amax[:, i:i + 1],
                                scalar2=512.5, op0=OP.mult, op1=OP.add)
        nc.vector.drain()
        nc.vector.tensor_copy(out=i32, in_=t2)
        nc.vector.drain()
        nc.vector.tensor_copy(out=xn1[:, i], in_=i32)
        nc.vector.drain()
        nc.vector.tensor_tensor(out=fb, in0=xn1[:, i], in1=t2, op=OP.is_gt)
        nc.vector.drain()
        nc.vector.tensor_tensor(out=t2, in0=xn1[:, i], in1=fb, op=OP.subtract)
        nc.vector.drain()
        nc.vector.tensor_scalar(out=t2, in0=t2, scalar1=639.0,
                                scalar2=384.0, op0=OP.min, op1=OP.max)
        nc.vector.drain()
        nc.vector.tensor_scalar(out=xq_tm[:, i], in0=t2, scalar1=-512.0,
                                scalar2=None, op0=OP.add)
        nc.vector.drain()
    tick_q = cx.vpub()
    if cx.dbg < 3:
        return
    nc.tensor.wait_ge(cx.sv.h, tick_q)
    _transpose_to_fm(cx, xq_tm, xq_fm, NSUB, src_bf=True)
    if cx.dbg < 4:
        return
    nc.tensor.wait_ge(cx.ss.h, cx.ss.n)
    for mi in range(DC):
        pkey = f"b{mi % 2}"
        pslot = cx.ps[mi % 2][:, 0:QT]
        tt = _fm_matmul(cx, wq_bf, xq_fm, range(DC), mi, pslot, pkey)
        nc.vector.wait_ge(cx.stn.h, tt)
        op = nc.vector.tensor_scalar(out=out_fm[:, mi], in0=pslot,
                                     scalar1=cc["wsc_c"][:, mi:mi + 1],
                                     scalar2=None, op0=OP.mult)
        cx.slot_set(pkey, cx.sv, cx.sv.inc(op))
    nc.tensor.wait_ge(cx.sv.h, cx.sv.n)
    for i in range(NSUB):
        for j in range(DC):

            def consume(psl, tick, i=i, j=j):
                nc.vector.wait_ge(cx.stn.h, tick)
                dst = x_tm[:, i, j * 128:(j + 1) * 128]
                nc.vector.scalar_tensor_tensor(
                    out=frac[:, 0:128], in0=psl, scalar=xsch[:, i:i + 1],
                    in1=cx.cc["halfbias_bc"][:, j * 128:(j + 1) * 128],
                    op0=OP.mult, op1=OP.add)
                nc.vector.drain()
                op = nc.vector.tensor_tensor(out=dst, in0=dst,
                                             in1=frac[:, 0:128], op=OP.add)
                return cx.sv, cx.sv.inc(op)

            _tp128(cx, out_fm[:, j, i * 128:(i + 1) * 128], consume, bf=False)


def _attn(cx, bufs, x_tm, wqkv_get, wproj_bf):
    nc = cx.nc
    cc = cx.cc
    (xn2, xn2_fm, qf, kf, vf, sq, qh, kh, vtm, scsb, probs_all, pT, yatt,
     scr) = bufs
    row, rqb_q, rqb_k, mx, nb, sume, rsum, dproj = scr
    tick_ln = _ln_tm(cx, x_tm, xn2, NSUB)
    nc.tensor.wait_ge(cx.sv.h, tick_ln)
    tick = _transpose_to_fm(cx, xn2, xn2_fm, NSUB, src_bf=True)
    nc.tensor.wait_ge(cx.ss.h, tick)
    for phase in range(3):
        wv, tl, wkey = wqkv_get(phase)
        nc.tensor.wait_ge(cx.lw.h, tl)
        for mi in range(DC):
            pkey = f"b{mi % 2}"
            pslot = cx.ps[mi % 2][:, 0:QT]
            tt = _fm_matmul(cx, wv, xn2_fm, range(DC), mi, pslot, pkey)
            nc.scalar.wait_ge(cx.stn.h, tt)
            dst = (qf, kf, vf)[phase][:, mi]
            cp = nc.scalar.copy(out=dst, in_=pslot)
            cx.slot_set(pkey, cx.ss, cx.ss.inc(cp))
        cx.slot_set(wkey, cx.stn, cx.stn.n)
    for name, src, rqb in (("q", qf, rqb_q), ("k", kf, rqb_k)):
        nc.scalar.wait_ge(cx.ss.h, cx.ss.n)
        for j in range(DC):
            nc.scalar.activation(out=sq[:, j], in_=src[:, j], func=AF.Square)
        ts = cx.spub()
        nc.tensor.wait_ge(cx.ss.h, ts)
        cx.slot_wait(nc.tensor, "b2")
        for j in range(DC):
            nc.tensor.matmul(cx.ps[2][0:1, 0:QT], cc["ones_col_bf"][:, 0:1],
                             sq[:, j], start=(j == 0), stop=(j == DC - 1))
        tt = cx.tpub()
        nc.vector.wait_ge(cx.stn.h, tt)
        nc.vector.tensor_scalar(out=row[0:1, :], in0=cx.ps[2][0:1, 0:QT],
                                scalar1=1.0 / C, scalar2=EPS_RMS,
                                op0=OP.mult, op1=OP.add)
        cx.slot_set("b2", cx.sv, cx.vpub())
        nc.scalar.wait_ge(cx.sv.h, cx.sv.n)
        nc.scalar.activation(out=row[0:1, :], in_=row[0:1, :], func=AF.Sqrt)
        ts = cx.spub()
        nc.vector.wait_ge(cx.ss.h, ts)
        nc.vector.reciprocal(out=row[0:1, :], in_=row[0:1, :])
        tick_v = cx.vpub()
        nc.tensor.wait_ge(cx.sv.h, tick_v)
        cx.slot_wait(nc.tensor, "b4")
        nc.tensor.matmul(cx.ps[4][:, 0:QT], cc["ones_f"][0:1, :], row[0:1, 0:QT],
                         start=True, stop=True)
        tt = cx.tpub()
        nc.scalar.wait_ge(cx.stn.h, tt)
        nc.scalar.copy(out=rqb, in_=cx.ps[4][:, 0:QT])
        cx.slot_set("b4", cx.ss, cx.spub())
    nc.vector.wait_ge(cx.ss.h, cx.ss.n)
    for (src, dst, rqb) in ((qf, qh, rqb_q), (kf, kh, rqb_k)):
        for j in range(DC):
            nc.vector.scalar_tensor_tensor(out=dst[:, j], in0=src[:, j],
                                           scalar=cc["lnw_c"][:, j:j + 1],
                                           in1=rqb, op0=OP.mult, op1=OP.mult)
        nc.vector.drain()
    tick_qh = cx.vpub()
    nc.tensor.wait_ge(cx.ss.h, cx.ss.n)
    for h in range(NH):
        ch, r0 = h // 2, (h % 2) * 64
        for kc in range(NSUB):

            def consume(psl, tick, h=h, kc=kc):
                nc.scalar.wait_ge(cx.stn.h, tick)
                cp = nc.scalar.copy(out=vtm[:, h, kc * 64:(kc + 1) * 64],
                                    in_=psl)
                return cx.ss, cx.ss.inc(cp)

            _tp128(cx, vf[r0:r0 + 64, ch, kc * 128:(kc + 1) * 128], consume,
                   bf=True, ident_slice=slice(r0, r0 + 64))
    tick_vtm = cx.ss.n
    nc.tensor.wait_ge(cx.sv.h, tick_qh)
    nc.tensor.wait_ge(cx.ss.h, tick_vtm)
    cx.slot_set("b2", cx.sv, cx.sv.n)
    cx.slot_set("b3", cx.sv, cx.sv.n)
    cx.slot_set("b4", cx.ss, cx.ss.n)
    jobs = [(h, qs) for h in range(NH) for qs in range(NSUB)]
    for j, (h, qs) in enumerate(jobs):
        ch, r0 = h // 2, (h % 2) * 64
        jp = j % 4
        pkey = f"b{2 + (j % 2)}"
        cx.slot_wait(nc.tensor, pkey)
        pslot = cx.ps[2 + (j % 2)][:, 0:QT]
        nc.tensor.matmul(pslot, qh[r0:r0 + 64, ch, qs * 128:(qs + 1) * 128],
                         kh[r0:r0 + 64, ch, 0:QT], start=True, stop=True)
        tt = cx.tpub()
        nc.vector.wait_ge(cx.stn.h, tt)
        cx.slot_wait(nc.vector, ("scsb", jp))
        op = nc.vector.tensor_tensor(out=scsb[:, jp], in0=pslot,
                                     in1=cc["tri0" if qs == 0 else "tri1"],
                                     op=OP.add)
        cx.slot_set(pkey, cx.sv, cx.sv.inc(op))
        nc.vector.drain()
        nc.vector.reduce_max(out=mx[:, j:j + 1], in_=scsb[:, jp], axis=AX.X)
        nc.vector.drain()
        op = nc.vector.tensor_scalar(out=nb[:, j:j + 1], in0=mx[:, j:j + 1],
                                     scalar1=-0.125, scalar2=None, op0=OP.mult)
        tick_v = cx.sv.inc(op)
        nc.scalar.wait_ge(cx.sv.h, tick_v)
        op = nc.scalar.activation(out=probs_all[:, h, qs], in_=scsb[:, jp],
                                  func=AF.Exp, scale=0.125, bias=nb[:, j:j + 1],
                                  accum_out=sume[:, j:j + 1])
        ts = cx.ss.inc(op)
        cx.slot_set(("scsb", jp), cx.ss, ts)
        nc.vector.wait_ge(cx.ss.h, ts)
        nc.vector.reciprocal(out=rsum[:, j:j + 1], in_=sume[:, j:j + 1])
        nc.vector.drain()
        op = nc.vector.tensor_scalar(out=probs_all[:, h, qs],
                                     in0=probs_all[:, h, qs],
                                     scalar1=rsum[:, j:j + 1], scalar2=None,
                                     op0=OP.mult)
        cx.sv.inc(op)
    tick_p = cx.sv.n
    nc.tensor.wait_ge(cx.sv.h, tick_p)
    for h in range(NH):
        ch, r0 = h // 2, (h % 2) * 64
        for qs in range(NSUB):
            for kc in range(NSUB):

                def consume(psl, tick, kc=kc, qs=qs):
                    nc.scalar.wait_ge(cx.stn.h, tick)
                    cp = nc.scalar.copy(out=pT[:, kc, qs * 128:(qs + 1) * 128],
                                        in_=psl)
                    return cx.ss, cx.ss.inc(cp)

                _tp128(cx, probs_all[:, h, qs, kc * 128:(kc + 1) * 128],
                       consume, bf=True)
        ts = cx.ss.n
        nc.tensor.wait_ge(cx.ss.h, ts)
        pkey = "b4"
        cx.slot_wait(nc.tensor, pkey)
        pslot = cx.ps[4][0:64, 0:QT]
        mm = None
        for kc in range(NSUB):
            mm = nc.tensor.matmul(pslot, vtm[:, h, kc * 64:(kc + 1) * 64],
                                  pT[:, kc, 0:QT], start=(kc == 0),
                                  stop=(kc == NSUB - 1))
        tt = cx.stn.inc(mm)
        nc.scalar.wait_ge(cx.stn.h, tt)
        cp = nc.scalar.copy(out=yatt[r0:r0 + 64, ch], in_=pslot)
        cx.slot_set(pkey, cx.ss, cx.ss.inc(cp))
    nc.tensor.wait_ge(cx.ss.h, cx.ss.n)
    for mi in range(DC):
        pkey = f"b{mi % 2}"
        pslot = cx.ps[mi % 2][:, 0:QT]
        tt = _fm_matmul(cx, wproj_bf, yatt, range(DC), mi, pslot, pkey)
        nc.scalar.wait_ge(cx.stn.h, tt)
        cp = nc.scalar.copy(out=dproj[:, mi], in_=pslot)
        cx.slot_set(pkey, cx.ss, cx.ss.inc(cp))
    nc.tensor.wait_ge(cx.ss.h, cx.ss.n)
    _residual_add_fm_delta(cx, dproj, x_tm, NSUB, 1.0)




def _pass1(cx, out="s_x3"):
    nc, d, cc = cx.nc, cx.dram, cx.cc
    NT = cx.NT
    cx.wpool.mark()
    cx.fpool.mark()
    cx.bpool.mark()
    wq_bf = cx.wpool.alloc(BF16, [DC, C])
    wproj_bf = cx.wpool.alloc(BF16, [DC, C])
    wqkv_sl = [cx.wpool.alloc(BF16, [DC, C]) for _ in range(2)]
    cx.cc["halfbias_bc"] = _build_bcast(cx, "fbias_r", cx.wpool, scale=0.5)
    stage = cx.fpool.alloc(F32, [C])
    stage_i8 = cx.bpool.alloc(I8, [C])
    for kc in range(DC):
        t0 = cx.load(stage_i8, d["wq_T"][kc * 128:(kc + 1) * 128, :])
        nc.vector.wait_ge(cx.ld.h, t0)
        nc.vector.tensor_copy(out=wq_bf[:, kc], in_=stage_i8)
        nc.vector.drain()
        t0 = cx.load(stage, d["w_proj"][kc * 128:(kc + 1) * 128, :])
        nc.vector.wait_ge(cx.ld.h, t0)
        nc.vector.tensor_copy(out=wproj_bf[:, kc], in_=stage)
        tick_w = cx.vpub()
        nc.sync.wait_ge(cx.sv.h, tick_w)

    def wqkv_get(phase):
        slot = phase % 2
        key = ("wqkv", slot)
        cx.slot_wait(nc.scalar, key)
        tl = cx.loadw(wqkv_sl[slot],
                      d["b_wqkv"][:, phase * C:(phase + 1) * C]
                      .rearrange("(a p) c -> p a c", p=128))
        return wqkv_sl[slot], tl, key

    xa = [cx.fpool.alloc(F32, [NSUB, C])]
    xn1 = cx.fpool.alloc(F32, [NSUB, C])
    xq_tm = cx.bpool.alloc(BF16, [NSUB, C])
    xq_fm = cx.bpool.alloc(BF16, [DC, QT])
    out_fm = cx.fpool.alloc(F32, [DC, QT])
    scrq = (cx.fpool.alloc(F32, [NSUB]), cx.fpool.alloc(F32, [NSUB]),
            cx.fpool.alloc(F32, [C]), cx.fpool.alloc(F32, [C]))
    xn2 = cx.bpool.alloc(BF16, [NSUB, C])
    xn2_fm = cx.bpool.alloc(BF16, [DC, QT])
    qf = cx.bpool.alloc(BF16, [DC, QT])
    kf = cx.bpool.alloc(BF16, [DC, QT])
    vf = cx.bpool.alloc(BF16, [DC, QT])
    sq = xn2_fm
    vtm = cx.bpool.alloc(BF16, [NH, NSUB * 64])
    scsb = cx.fpool.alloc(F32, [4, QT])
    probs_all = cx.bpool.alloc(BF16, [NH, NSUB, QT])
    pT = cx.bpool.alloc(BF16, [NSUB, QT])
    yatt = cx.bpool.alloc(BF16, [DC, QT])
    NJ = NH * NSUB
    scra = (cx.fpool.alloc(F32, [QT]), cx.fpool.alloc(F32, [QT]),
            cx.fpool.alloc(F32, [QT]), cx.fpool.alloc(F32, [NJ]),
            cx.fpool.alloc(F32, [NJ]), cx.fpool.alloc(F32, [NJ]),
            cx.fpool.alloc(F32, [NJ]), out_fm)
    qb1 = (xn1, xq_tm, xq_fm, out_fm, scrq)
    ab = (xn2, xn2_fm, qf, kf, vf, sq, qf, kf, vtm, scsb, probs_all, pT, yatt,
          scra)

    tile_end = {}
    for k in range(NT):
        p = 0
        if k >= 1:
            for sem, tick in tile_end[k - 1]:
                for eng in (nc.vector, nc.scalar, nc.tensor):
                    eng.wait_ge(sem.h, tick)
        if k >= 1:
            for sem, tick in tile_end[k - 1]:
                nc.sync.wait_ge(sem.h, tick)
        tk = cx.load(xa[p], d["x0"][k * QT:(k + 1) * QT, :]
                     .rearrange("(a p) c -> p a c", p=128))
        for eng in (nc.vector, nc.scalar, nc.tensor):
            eng.wait_ge(cx.ld.h, tk)
        if "ffn0a" in cx.chain:
            _quant_ffn(cx, qb1, xa[p], wq_bf)
        if "attn0" in cx.chain:
            _attn(cx, ab, xa[p], wqkv_get, wproj_bf)
        if "ffn0b" in cx.chain:
            _quant_ffn(cx, qb1, xa[p], wq_bf)
        tick_v = cx.vpub()
        nc.sync.wait_ge(cx.sv.h, tick_v)
        cx.store(d[out][k * QT:(k + 1) * QT, :]
                 .rearrange("(a p) c -> p a c", p=128), xa[p])
        tile_end[k] = [(cx.sv, cx.sv.n), (cx.ss, cx.ss.n), (cx.stn, cx.stn.n),
                       (cx.st, cx.st.n)]
    cx.wpool.release()
    cx.fpool.release()
    cx.bpool.release()



def _rwkv_delta(cx, bufs, x_tm, weights, carry, first):
    nc, cc = cx.nc, cx.cc
    (xn4, xf_buf, xr, rbf, kbuf, dproj) = bufs
    wr_bf, wk_bf, wv_bf, omr, omk = weights
    tick_ln = _ln_tm(cx, x_tm, xn4, NSUB)
    nc.vector.wait_ge(cx.ss.h, cx.ss.n)
    if first:
        nc.vector.memset(xf_buf[:, :, 0:1], 0.0)
    else:
        nc.vector.tensor_copy(out=xf_buf[:, :, 0:1], in_=carry)
    tcar = cx.vpub()
    nc.tensor.wait_ge(cx.sv.h, tick_ln)
    tick = _transpose_to_fm(cx, xn4, xf_buf.rearrange("p a b -> p a b")
                            [:, :, 1:QT + 1], NSUB)
    nc.vector.wait_ge(cx.ss.h, tick)
    nc.vector.tensor_copy(out=carry, in_=xf_buf[:, :, QT:QT + 1])
    nc.vector.drain()
    for phase in range(2):
        mu = cc["mur_c"] if phase == 0 else cc["muk_c"]
        om = omr if phase == 0 else omk
        if phase == 1:
            nc.vector.wait_ge(cx.stn.h, cx.stn.n)
        for j in range(DC):
            nc.vector.tensor_scalar(out=xr[:, j], in0=xf_buf[:, j, 0:QT],
                                    scalar1=om[:, j:j + 1], scalar2=None,
                                    op0=OP.mult)
            nc.vector.drain()
            nc.vector.scalar_tensor_tensor(out=xr[:, j],
                                           in0=xf_buf[:, j, 1:QT + 1],
                                           scalar=mu[:, j:j + 1], in1=xr[:, j],
                                           op0=OP.mult, op1=OP.add)
            nc.vector.drain()
        tick_mix = cx.vpub()
        nc.tensor.wait_ge(cx.sv.h, tick_mix)
        w = wr_bf if phase == 0 else wk_bf
        for mi in range(DC):
            pkey = f"b{mi % 2}"
            pslot = cx.ps[mi % 2][:, 0:QT]
            tt = _fm_matmul(cx, w, xr, range(DC), mi, pslot, pkey)
            nc.scalar.wait_ge(cx.stn.h, tt)
            if phase == 0:
                cp = nc.scalar.activation(out=rbf[:, mi], in_=pslot,
                                          func=AF.Sigmoid)
            else:
                cp = nc.scalar.activation(out=kbuf[:, mi], in_=pslot,
                                          func=AF.Relu)
            cx.slot_set(pkey, cx.ss, cx.ss.inc(cp))
    nc.vector.wait_ge(cx.ss.h, cx.ss.n)
    for j in range(DC):
        nc.vector.tensor_tensor(out=kbuf[:, j], in0=kbuf[:, j], in1=kbuf[:, j],
                                op=OP.mult)
    nc.vector.drain()
    tk2 = cx.vpub()
    nc.tensor.wait_ge(cx.sv.h, tk2)
    for mi in range(DC):
        pkey = f"b{mi % 2}"
        pslot = cx.ps[mi % 2][:, 0:QT]
        tt = _fm_matmul(cx, wv_bf, kbuf, range(DC), mi, pslot, pkey)
        nc.vector.wait_ge(cx.stn.h, tt)
        op = nc.vector.tensor_tensor(out=dproj[:, mi], in0=rbf[:, mi],
                                     in1=pslot, op=OP.mult)
        cx.slot_set(pkey, cx.sv, cx.sv.inc(op))
    nc.tensor.wait_ge(cx.sv.h, cx.sv.n)


def _rwkv_weights(cx):
    nc, d = cx.nc, cx.dram
    wr_bf = cx.wpool.alloc(BF16, [DC, C])
    wk_bf = cx.wpool.alloc(BF16, [DC, C])
    wv_bf = cx.wpool.alloc(BF16, [DC, C])
    stage = cx.fpool.alloc(F32, [C])
    for name, dst in (("wr_T", wr_bf), ("wk_T", wk_bf), ("wv_T", wv_bf)):
        for kc in range(DC):
            t0 = cx.load(stage, d[name][kc * 128:(kc + 1) * 128, :])
            nc.vector.wait_ge(cx.ld.h, t0)
            nc.vector.tensor_copy(out=dst[:, kc], in_=stage)
            tw = cx.vpub()
            nc.sync.wait_ge(cx.sv.h, tw)
    omr = cx.wpool.alloc(F32, [DC])
    omk = cx.wpool.alloc(F32, [DC])
    nc.vector.wait_ge(cx.ld.h, cx.ld.n)
    nc.vector.tensor_scalar(out=omr, in0=cx.cc["mur_c"], scalar1=-1.0,
                            scalar2=1.0, op0=OP.mult, op1=OP.add)
    nc.vector.drain()
    nc.vector.tensor_scalar(out=omk, in0=cx.cc["muk_c"], scalar1=-1.0,
                            scalar2=1.0, op0=OP.mult, op1=OP.add)
    cx.vpub()
    return (wr_bf, wk_bf, wv_bf, omr, omk)


def _pass2(cx, out=None):
    nc, d, cc = cx.nc, cx.dram, cx.cc
    NT = cx.NT
    cx.wpool.mark()
    cx.fpool.mark()
    cx.bpool.mark()
    weights = _rwkv_weights(cx)
    cc["gamma_bc"] = _build_bcast(cx, "gamma_r", cx.wpool)
    cc["beta_bc"] = _build_bcast(cx, "beta_r", cx.wpool)
    cc["outb_bc"] = _build_bcast(cx, "outb_r", cx.wpool)

    xa = cx.fpool.alloc(F32, [NSUB, C])
    a_tm = cx.fpool.alloc(F32, [NSUB, C])
    h1 = cx.fpool.alloc(F32, [NSUB, C])
    dproj = cx.fpool.alloc(F32, [DC, QT])
    xn4 = cx.bpool.alloc(BF16, [NSUB, C])
    xf_buf = cx.bpool.alloc(BF16, [DC, QT + 1])
    xr = cx.bpool.alloc(BF16, [DC, QT])
    rbf = cx.bpool.alloc(BF16, [DC, QT])
    kbuf = cx.bpool.alloc(BF16, [DC, QT])
    h_tm = cx.bpool.alloc(BF16, [NSUB, C])
    carry = cx.bpool.alloc(BF16, [DC, 1])
    rw_bufs = (xn4, xf_buf, xr, rbf, kbuf, dproj)

    tile_end = {}
    for k in range(NT):
        if k >= 1:
            for sem, tick in tile_end[k - 1]:
                for eng in (nc.vector, nc.scalar, nc.tensor):
                    eng.wait_ge(sem.h, tick)
                nc.sync.wait_ge(sem.h, tick)
        tk = cx.load(xa, d["s_x3"][k * QT:(k + 1) * QT, :]
                     .rearrange("(a p) c -> p a c", p=128))
        for eng in (nc.vector, nc.scalar, nc.tensor):
            eng.wait_ge(cx.ld.h, tk)
        _rwkv_delta(cx, rw_bufs, xa, weights, carry, first=(k == 0))
        _residual_add_fm_delta(cx, dproj, xa, NSUB, 0.5)
        _ln_tm(cx, xa, a_tm, NSUB)
        _ln_tm(cx, a_tm, h1, NSUB)
        for i in range(NSUB):
            nc.vector.tensor_tensor(out=h1[:, i], in0=h1[:, i],
                                    in1=cc["gamma_bc"], op=OP.mult)
            nc.vector.drain()
            nc.vector.tensor_tensor(out=h_tm[:, i], in0=h1[:, i],
                                    in1=cc["beta_bc"], op=OP.add)
            nc.vector.drain()
        for i in range(NSUB):
            nc.vector.tensor_tensor(out=h1[:, i], in0=xa[:, i], in1=a_tm[:, i],
                                    op=OP.add)
            nc.vector.drain()
            nc.vector.tensor_tensor(out=h1[:, i], in0=h1[:, i],
                                    in1=cc["outb_bc"], op=OP.add)
            nc.vector.drain()
        tick_v = cx.vpub()
        nc.sync.wait_ge(cx.sv.h, tick_v)
        cx.store(d["s_h"][k * QT:(k + 1) * QT, :]
                 .rearrange("(a p) c -> p a c", p=128), h_tm)
        cx.store(d["s_p5a"][k * QT:(k + 1) * QT, :]
                 .rearrange("(a p) c -> p a c", p=128), h1)
        if out:
            cx.store(d[out][k * QT:(k + 1) * QT, :]
                     .rearrange("(a p) c -> p a c", p=128), xa)
        tile_end[k] = [(cx.sv, cx.sv.n), (cx.ss, cx.ss.n), (cx.stn, cx.stn.n),
                       (cx.st, cx.st.n)]
    cx.wpool.release()
    cx.fpool.release()
    cx.bpool.release()


def _mamba_pass(cx, dgn, out=None):
    nc, d, cc = cx.nc, cx.dram, cx.cc
    NT = cx.NT
    rev = dgn == "b"
    src_p5 = "s_p5a" if dgn == "f" else "s_p5b"
    dst_p5 = "s_p5b" if dgn == "f" else (out or "s_x5")
    cx.wpool.mark()
    cx.fpool.mark()
    cx.bpool.mark()
    inw_sl = [cx.wpool.alloc(BF16, [DC, C]) for _ in range(2)]
    weff_sl = [cx.wpool.alloc(BF16, [DIC, 256]) for _ in range(2)]
    xproj_bf = cx.wpool.alloc(BF16, [DIC, 128])
    dtw_bf = cx.wpool.alloc(BF16, [DI])
    cx.fpool.mark()
    stage = cx.fpool.alloc(F32, [DI])
    for kc in range(DIC):
        t0 = cx.load(stage[:, 0:128], d[f"xproj_T_{dgn}"][kc * 128:(kc + 1) * 128, :])
        nc.vector.wait_ge(cx.ld.h, t0)
        nc.vector.tensor_copy(out=xproj_bf[:, kc], in_=stage[:, 0:128])
        tw = cx.vpub()
        nc.sync.wait_ge(cx.sv.h, tw)
    t0 = cx.load(stage[0:DTR, :], d[f"dtw_T_{dgn}"][:, :])
    nc.vector.wait_ge(cx.ld.h, t0)
    nc.vector.tensor_copy(out=dtw_bf[0:DTR, :], in_=stage[0:DTR, :])
    cx.vpub()
    nc.sync.wait_ge(cx.sv.h, cx.sv.n)
    cx.fpool.release()

    h_tm = cx.fpool.alloc(F32, [0]) if False else cx.bpool.alloc(BF16, [NSUB, C])
    h_fm = cx.bpool.alloc(BF16, [DC, QT])
    xi_buf = cx.bpool.alloc(BF16, [DIC, QT + 3])
    zs_fm = cx.bpool.alloc(BF16, [DIC, QT])
    xc_fm = cx.bpool.alloc(BF16, [DIC, QT])
    rows_bf = cx.bpool.alloc(BF16, [QT])
    bnr_bf = cx.bpool.alloc(BF16, [QT])
    ccr_bf = cx.bpool.alloc(BF16, [QT])
    du_bf = cx.bpool.alloc(BF16, [QT])
    bb_all = cx.bpool.alloc(BF16, [DS, QT])
    cb_all = cx.bpool.alloc(BF16, [DS, QT])
    w_half = [cx.bpool.alloc(BF16, [DS // 2, QT]) for _ in range(2)]
    hall = xi_buf.rearrange("p a b -> p (a b)")[:, 0:DS * QT] \
        .rearrange("p (s q) -> p s q", s=DS)
    yp_fm = zs_fm

    dn_fm = cx.fpool.alloc(F32, [DIC, QT])
    e_half = [cx.fpool.alloc(F32, [DS // 2, QT]) for _ in range(2)]
    sgbuf = cx.fpool.alloc(F32, [QT])
    convacc2 = [cx.fpool.alloc(F32, [QT]) for _ in range(2)]
    cgtmp = cx.fpool.alloc(F32, [QT])
    t1f = cx.fpool.alloc(F32, [QT])
    ys_f = cx.fpool.alloc(F32, [QT])
    fix8 = cx.fpool.alloc(F32, [DS // 2])
    xdbl = cx.fpool.alloc(F32, [QT])
    p5_tm = cx.fpool.alloc(F32, [NSUB, C])
    state = cx.fpool.alloc(F32, [DIC, DS])
    carry3 = cx.bpool.alloc(BF16, [DIC, 3])
    pevac = [cx.fpool.alloc(F32, [QT]) for _ in range(2)]

    nc.vector.wait_ge(cx.ld.h, cx.ld.n)
    nc.vector.memset(state.rearrange("p a b -> p (a b)"), 0.0)
    nc.vector.drain()
    nc.vector.memset(carry3.rearrange("p a b -> p (a b)"), 0.0)
    tick0 = cx.vpub()
    nc.sync.wait_ge(cx.sv.h, tick0)

    sub_map = list(range(NSUB - 1, -1, -1)) if rev else None
    perm = "rev" if rev else None

    def stream_w(dram_ap, slot_bufs, qi, key_pfx, view):
        slot = qi % 2
        key = (key_pfx, slot)
        cx.slot_wait(nc.scalar, key)
        tl = cx.loadw(slot_bufs[slot] if view is None else view(slot_bufs[slot]),
                      dram_ap)
        return slot_bufs[slot], tl, key

    tile_end = {}
    for kk in range(NT):
        k = NT - 1 - kk if rev else kk
        if kk >= 1:
            for sem, tick in tile_end[kk - 1]:
                for eng in (nc.vector, nc.scalar, nc.tensor, nc.gpsimd):
                    eng.wait_ge(sem.h, tick)
                nc.sync.wait_ge(sem.h, tick)
        tk = cx.load(h_tm, d["s_h"][k * QT:(k + 1) * QT, :]
                     .rearrange("(a p) c -> p a c", p=128))
        for eng in (nc.vector, nc.scalar, nc.tensor):
            eng.wait_ge(cx.ld.h, tk)
        _transpose_to_fm(cx, h_tm, h_fm, NSUB, src_bf=True, perm=perm,
                         sub_map=sub_map)
        nc.vector.wait_ge(cx.ss.h, cx.ss.n)
        nc.vector.tensor_copy(out=xi_buf[:, :, 0:3], in_=carry3[:, :, :])
        tcar = cx.vpub()
        nc.tensor.wait_ge(cx.sv.h, tcar)
        for qg in range(4):
            wv, tl, wkey = stream_w(
                d[f"b_inw_{dgn}"][:, qg * C:(qg + 1) * C]
                .rearrange("(a p) c -> p a c", p=128), inw_sl, qg, "inw", None)
            nc.tensor.wait_ge(cx.lw.h, tl)
            for mi in range(8):
                mg = qg * 8 + mi
                pkey = f"b{mi % 2}"
                pslot = cx.ps[mi % 2][:, 0:QT]
                tt = _fm_matmul(cx, wv, h_fm, range(DC), mi, pslot, pkey)
                nc.scalar.wait_ge(cx.stn.h, tt)
                if mg < DIC:
                    cp = nc.scalar.copy(out=xi_buf[:, mg, 3:QT + 3], in_=pslot)
                else:
                    cp = nc.scalar.activation(out=zs_fm[:, mg - DIC], in_=pslot,
                                              func=AF.Silu)
                cx.slot_set(pkey, cx.ss, cx.ss.inc(cp))
            cx.slot_set(wkey, cx.stn, cx.stn.n)
        cw = cc[f"convw_{dgn}"]
        nc.gpsimd.wait_ge(cx.ss.h, cx.ss.n)
        for c in range(DIC):
            ca = convacc2[c % 2]
            nc.gpsimd.wait_ge(cx.ss.h, cx.ss.n)
            nc.gpsimd.tensor_tensor(out=ca, in0=xi_buf[:, c, 0:QT],
                                    in1=cw[:, c, 0:1].broadcast_to([128, QT]),
                                    op=OP.mult)
            nc.gpsimd.drain()
            for j in range(1, 4):
                nc.gpsimd.tensor_tensor(
                    out=cgtmp, in0=xi_buf[:, c, j:j + QT],
                    in1=cw[:, c, j:j + 1].broadcast_to([128, QT]), op=OP.mult)
                nc.gpsimd.drain()
                nc.gpsimd.tensor_tensor(out=ca, in0=ca, in1=cgtmp, op=OP.add)
                nc.gpsimd.drain()
            tcv = cx.sg.inc(nc.gpsimd.drain())
            nc.scalar.wait_ge(cx.sg.h, tcv)
            cp = nc.scalar.activation(out=xc_fm[:, c], in_=ca, func=AF.Silu,
                                      bias=cc[f"convb_c_{dgn}"][:, c:c + 1])
            cx.ss.inc(cp)
        nc.gpsimd.tensor_copy(out=carry3[:, :, :], in_=xi_buf[:, :, QT:QT + 3])
        tcar2 = cx.sg.inc(nc.gpsimd.drain())
        nc.vector.wait_ge(cx.sg.h, tcar2)
        cx.slot_wait(nc.tensor, "b2")
        nc.tensor.wait_ge(cx.ss.h, cx.ss.n)
        for kc in range(DIC):
            nc.tensor.matmul(cx.ps[2][:, 0:QT], xproj_bf[:, kc],
                             xc_fm[:, kc], start=(kc == 0), stop=(kc == DIC - 1))
        tt = cx.tpub()
        nc.scalar.wait_ge(cx.stn.h, tt)
        nc.scalar.copy(out=xdbl[:, :], in_=cx.ps[2][:, 0:QT])
        tsd = cx.spub()
        cx.slot_set("b2", cx.ss, tsd)
        nc.vector.wait_ge(cx.ss.h, tsd)
        nc.vector.tensor_copy(out=rows_bf[0:DTR, :], in_=xdbl[0:DTR, :])
        nc.vector.drain()
        nc.vector.tensor_scalar(out=bnr_bf[0:DS, :],
                                in0=xdbl[DTR:DTR + DS, :], scalar1=-1.0,
                                scalar2=None, op0=OP.mult)
        nc.vector.drain()
        nc.vector.tensor_copy(out=ccr_bf[0:DS, :], in_=xdbl[96:96 + DS, :])
        trows = cx.vpub()
        nc.tensor.wait_ge(cx.sv.h, trows)
        for mc in range(DIC):
            pkey = f"b{mc % 2}"
            pslot = cx.ps[mc % 2][:, 0:QT]
            cx.slot_wait(nc.tensor, pkey)
            nc.tensor.matmul(pslot, dtw_bf[0:DTR, mc * 128:(mc + 1) * 128],
                             rows_bf[0:DTR, 0:QT], start=True, stop=True)
            tt = cx.tpub()
            nc.scalar.wait_ge(cx.stn.h, tt)
            nc.scalar.activation(out=sgbuf, in_=pslot, func=AF.Sigmoid,
                                 scale=-1.0, bias=cc[f"ndtb_{dgn}"][:, mc:mc + 1])
            nc.scalar.drain()
            nc.scalar.activation(out=dn_fm[:, mc], in_=sgbuf, func=AF.Ln)
            cx.slot_set(pkey, cx.ss, cx.spub())
        nc.tensor.wait_ge(cx.ss.h, cx.ss.n)
        for s in range(DS):
            for (rsrc, dst) in ((bnr_bf, bb_all), (ccr_bf, cb_all)):
                cx.slot_wait(nc.tensor, "b3")
                nc.tensor.matmul(cx.ps[3][:, 0:QT],
                                 cc["sel_bf"][0:DS, s * 128:(s + 1) * 128],
                                 rsrc[0:DS, 0:QT],
                                 start=True, stop=True)
                tt = cx.tpub()
                nc.vector.wait_ge(cx.stn.h, tt)
                nc.vector.tensor_copy(out=dst[:, s], in_=cx.ps[3][:, 0:QT])
                cx.slot_set("b3", cx.sv, cx.vpub())
        SH = DS // 2
        half_i = 0
        e_cons = {}
        nc.scalar.wait_ge(cx.sv.h, cx.sv.n)
        for c in range(DIC):
            nc.vector.wait_ge(cx.ss.h, cx.ss.n)
            du = nc.vector.tensor_tensor(out=du_bf, in0=dn_fm[:, c],
                                         in1=xc_fm[:, c], op=OP.mult)
            nc.vector.drain()
            for hf in range(2):
                par = half_i % 2
                eb = e_half[par]
                wb = w_half[par]
                s0 = hf * SH
                if par in e_cons:
                    nc.scalar.wait_ge(cx.sv.h, e_cons[par])
                for j in range(SH):
                    ei = nc.scalar.activation(
                        out=eb[:, j], in_=dn_fm[:, c], func=AF.Exp,
                        scale=cc[f"apos_{dgn}"][:, c, s0 + j:s0 + j + 1])
                tse = cx.ss.inc(ei)
                nc.vector.wait_ge(cx.ss.h, tse)
                wi = nc.vector.tensor_tensor(
                    out=wb[:, :, :],
                    in0=du_bf[:].unsqueeze(1).broadcast_to([128, SH, QT]),
                    in1=bb_all[:, s0:s0 + SH, :], op=OP.mult)
                nc.vector.drain()
                nc.vector.tensor_tensor(out=fix8, in0=eb[:, :, 0],
                                        in1=state[:, c, s0:s0 + SH],
                                        op=OP.mult)
                nc.vector.drain()
                nc.vector.tensor_tensor(out=wb[:, :, 0], in0=wb[:, :, 0],
                                        in1=fix8, op=OP.add)
                nc.vector.drain()
                nc.vector.memset(eb[:, :, 0:1], 0.0)
                nc.vector.drain()
                sc = nc.vector.tensor_tensor_scan(
                    out=hall[:, s0:s0 + SH, :].rearrange("p a b -> p (a b)"),
                    data0=eb[:, :, :].rearrange("p a b -> p (a b)"),
                    data1=wb[:, :, :].rearrange("p a b -> p (a b)"),
                    initial=0.0, op0=OP.mult, op1=OP.add)
                e_cons[par] = cx.sv.inc(sc)
                nc.vector.drain()
                nc.vector.tensor_copy(out=state[:, c, s0:s0 + SH],
                                      in_=hall[:, s0:s0 + SH, QT - 1])
                nc.vector.drain()
                half_i += 1
            hc = nc.vector.tensor_tensor(
                out=hall[:, :, :].rearrange("p a b -> p (a b)"),
                in0=hall[:, :, :].rearrange("p a b -> p (a b)"),
                in1=cb_all[:, :, :].rearrange("p a b -> p (a b)"), op=OP.mult)
            nc.vector.drain()
            stride = 1
            while stride < DS // 2:
                nw = DS // (2 * stride)
                nc.vector.tensor_tensor(
                    out=hall[:, 0:nw, :].rearrange("p a b -> p (a b)"),
                    in0=hall[:, 0:nw, :].rearrange("p a b -> p (a b)"),
                    in1=hall[:, nw:2 * nw, :].rearrange("p a b -> p (a b)"),
                    op=OP.add)
                nc.vector.drain()
                stride *= 2
            nc.vector.tensor_tensor(out=ys_f, in0=hall[:, 0, :],
                                    in1=hall[:, 1, :], op=OP.add)
            nc.vector.drain()
            nc.vector.scalar_tensor_tensor(out=t1f, in0=xc_fm[:, c],
                                           scalar=cc[f"dd_c_{dgn}"][:, c:c + 1],
                                           in1=ys_f, op0=OP.mult, op1=OP.add)
            nc.vector.drain()
            gi = nc.vector.tensor_tensor(out=yp_fm[:, c], in0=t1f,
                                         in1=zs_fm[:, c], op=OP.mult)
            cx.sv.inc(gi)
        tick_scan = cx.sv.n
        tkp = cx.load(p5_tm, d[src_p5][k * QT:(k + 1) * QT, :]
                      .rearrange("(a p) c -> p a c", p=128))
        nc.tensor.wait_ge(cx.sv.h, tick_scan)
        nc.vector.wait_ge(cx.ld.h, tkp)
        for qg in range(4):
            wv, tl, wkey = stream_w(
                d[f"b_weff_{dgn}"][:, qg * 256:(qg + 1) * 256]
                .rearrange("(a p) c -> p a c", p=128), weff_sl, qg, "weff",
                None)
            nc.tensor.wait_ge(cx.lw.h, tl)
            for mi in range(2):
                mg = qg * 2 + mi
                pkey = f"b{mi % 2}"
                pslot = cx.ps[mi % 2][:, 0:QT]
                tt = _fm_matmul(cx, wv, yp_fm, range(DIC), mi, pslot, pkey)
                nc.scalar.wait_ge(cx.stn.h, tt)
                nc.scalar.copy(out=pevac[mi % 2], in_=pslot)
                tse = cx.spub()
                cx.slot_set(pkey, cx.ss, tse)
                for i in range(NSUB):

                    def consume(psl, tick, i=i, mg=mg):
                        nc.vector.wait_ge(cx.stn.h, tick)
                        di = (sub_map[i] if sub_map else i)
                        dst = p5_tm[:, di, mg * 128:(mg + 1) * 128]
                        op = nc.vector.tensor_tensor(out=dst, in0=dst, in1=psl,
                                                     op=OP.add)
                        return cx.sv, cx.sv.inc(op)

                    nc.tensor.wait_ge(cx.ss.h, tse)
                    _tp128(cx, pevac[mi % 2][:, i * 128:(i + 1) * 128],
                           consume, bf=False, perm=perm)
            cx.slot_set(wkey, cx.stn, cx.stn.n)
        tick_v = cx.vpub()
        nc.sync.wait_ge(cx.sv.h, tick_v)
        cx.store(d[dst_p5][k * QT:(k + 1) * QT, :]
                 .rearrange("(a p) c -> p a c", p=128), p5_tm)
        tile_end[kk] = [(cx.sv, cx.sv.n), (cx.ss, cx.ss.n), (cx.stn, cx.stn.n),
                        (cx.st, cx.st.n), (cx.sg, cx.sg.n)]
    cx.wpool.release()
    cx.fpool.release()
    cx.bpool.release()


def _pass5(cx):
    nc, d, cc = cx.nc, cx.dram, cx.cc
    NT = cx.NT
    cx.wpool.mark()
    cx.fpool.mark()
    cx.bpool.mark()
    weights = _rwkv_weights(cx)
    xa = cx.fpool.alloc(F32, [NSUB, C])
    dproj = cx.fpool.alloc(F32, [DC, QT])
    xn4 = cx.bpool.alloc(BF16, [NSUB, C])
    xf_buf = cx.bpool.alloc(BF16, [DC, QT + 1])
    xr = cx.bpool.alloc(BF16, [DC, QT])
    rbf = cx.bpool.alloc(BF16, [DC, QT])
    kbuf = cx.bpool.alloc(BF16, [DC, QT])
    carry = cx.bpool.alloc(BF16, [DC, 1])
    rw_bufs = (xn4, xf_buf, xr, rbf, kbuf, dproj)

    tile_end = {}
    for k in range(NT):
        if k >= 1:
            for sem, tick in tile_end[k - 1]:
                for eng in (nc.vector, nc.scalar, nc.tensor):
                    eng.wait_ge(sem.h, tick)
                nc.sync.wait_ge(sem.h, tick)
        tk = cx.load(xa, d["s_x5"][k * QT:(k + 1) * QT, :]
                     .rearrange("(a p) c -> p a c", p=128))
        for eng in (nc.vector, nc.scalar, nc.tensor):
            eng.wait_ge(cx.ld.h, tk)
        _rwkv_delta(cx, rw_bufs, xa, weights, carry, first=(k == 0))
        _residual_add_fm_delta(cx, dproj, xa, NSUB, 0.5)
        tick_v = cx.vpub()
        nc.sync.wait_ge(cx.sv.h, tick_v)
        cx.store(d["y"][k * QT:(k + 1) * QT, :]
                 .rearrange("(a p) c -> p a c", p=128), xa)
        tile_end[k] = [(cx.sv, cx.sv.n), (cx.ss, cx.ss.n), (cx.stn, cx.stn.n),
                       (cx.st, cx.st.n)]
    cx.wpool.release()
    cx.fpool.release()
    cx.bpool.release()


# version 05
# speedup vs baseline: 1.2516x; 1.2516x over previous
import contextlib

import numpy as np
import ml_dtypes

import concourse.bass as bass
import concourse.mybir as mybir
from concourse.bass_utils import run_bass_kernel_spmd
from concourse.alu_op_type import AluOpType as OP

F32 = mybir.dt.float32
BF16 = mybir.dt.bfloat16
I8 = mybir.dt.int8
AF = mybir.ActivationFunctionType
AX = mybir.AxisListType

C = 1024
DC = C // 128
DI = 2048
DIC = DI // 128
DS = 16
DTR = 64
NH = 16
HD = 64
LS = 256
QT = 256
NSUB = QT // 128
EPS_LN = 1e-5
EPS_RMS = 1e-6
EPS_Q = 1e-6
NEG = -1.0e30

CHAIN = ("ffn0a", "attn0", "ffn0b", "rwkva", "mamba", "rwkvb")


def _cdiv(a, b):
    return (a + b - 1) // b


class Pool:

    def __init__(self, nc, ctx, name, nbytes):
        assert nbytes % 64 == 0
        self.t = ctx.enter_context(nc.sbuf_tensor(name, [128, nbytes], I8))
        self.name = name
        self.nbytes = nbytes
        self.cur = 0
        self.marks = []

    def alloc(self, dtype, shape):
        esz = mybir.dt.size(dtype)
        n = 1
        for s in shape:
            n *= s
        nb = _cdiv(n * esz, 64) * 64
        assert self.cur + nb <= self.nbytes, (
            f"pool {self.name} overflow: {self.cur}+{nb}>{self.nbytes}")
        ap = self.t[:, self.cur:self.cur + n * esz].bitcast(dtype)
        self.cur += nb
        if len(shape) > 1:
            dims = " ".join(f"d{i}" for i in range(len(shape)))
            kw = {f"d{i}": s for i, s in enumerate(shape)}
            ap = ap.rearrange(f"p ({dims}) -> p {dims}", **kw)
        return ap

    def mark(self):
        self.marks.append(self.cur)

    def release(self):
        self.cur = self.marks.pop()


class Sem:
    def __init__(self, nc, ctx, name, step=1):
        self.h = ctx.enter_context(nc.semaphore(name))
        self.n = 0
        self.step = step

    def inc(self, instr):
        instr.then_inc(self.h, self.step)
        self.n += self.step
        return self.n


class Ctx:
    def load(self, dst, src):
        self.nc.sync.wait_ge(self.ld.h, self.ld.n)
        self.nc.sync.dma_start(out=dst, in_=src).then_inc(self.ld.h, 16)
        self.ld.n += 16
        return self.ld.n

    def loadw(self, dst, src):
        self.nc.scalar.wait_ge(self.lw.h, self.lw.n)
        self.nc.scalar.dma_start(out=dst, in_=src).then_inc(self.lw.h, 16)
        self.lw.n += 16
        return self.lw.n

    def store(self, dst, src):
        self.nc.sync.wait_ge(self.st.h, self.st.n)
        self.nc.sync.dma_start(out=dst, in_=src).then_inc(self.st.h, 16)
        self.st.n += 16
        return self.st.n

    def slot_wait(self, eng, key):
        rec = self.slots.get(key)
        if rec is not None:
            sem, tick = rec
            eng.wait_ge(sem.h, tick)

    def slot_set(self, key, sem, tick):
        self.slots[key] = (sem, tick)

    def vpub(self):
        return self.sv.inc(self.nc.vector.drain())

    def spub(self):
        return self.ss.inc(self.nc.scalar.drain())

    def tpub(self):
        return self.stn.inc(self.nc.tensor.drain())

    def gpub(self):
        return self.sg.inc(self.nc.gpsimd.drain())



def kernel(**inputs):
    T = inputs["L0_attn_x"].shape[1]
    n_cores = 8
    x = np.ascontiguousarray(np.asarray(inputs["L0_attn_x"], dtype=np.float32))

    def t(a):
        return np.ascontiguousarray(np.asarray(a, dtype=np.float32).T)

    def cols(v, n):
        v = np.asarray(v, dtype=np.float32).reshape(n, 128)
        return np.ascontiguousarray(v.T)

    common = host_inputs(inputs, t, cols)
    nc = build_program(T=T)
    in_maps = []
    for c in range(n_cores):
        m = dict(common)
        m["x0"] = np.ascontiguousarray(x[c])
        in_maps.append(m)
    res = run_bass_kernel_spmd(nc, in_maps, core_ids=list(range(n_cores)))
    out = np.stack([np.asarray(res.results[c]["y"]) for c in range(n_cores)], axis=0)
    return out.astype(np.float32)


def host_inputs(inputs, t, cols):
    common = {
        "wq_T": np.ascontiguousarray(np.asarray(inputs["L0_ffn_weight_q"]).T),
        "w_qkv": np.ascontiguousarray(np.asarray(inputs["L0_attn_w_qkv"], dtype=np.float32)),
        "w_proj": np.ascontiguousarray(np.asarray(inputs["L0_attn_w_proj"], dtype=np.float32)),
        "lnw_c": cols(inputs["L0_attn_ln_w"], DC),
        "wsc_c": cols(inputs["L0_ffn_weight_scale"], DC),
        "fbias_c": cols(inputs["L0_ffn_bias"], DC),
        "fbias_r": np.asarray(inputs["L0_ffn_bias"], np.float32).reshape(1, C),
        "gamma_r": np.asarray(inputs["L1_attn_gamma"], np.float32).reshape(1, C),
        "beta_r": np.asarray(inputs["L1_attn_beta"], np.float32).reshape(1, C),
        "outb_r": np.asarray(inputs["L1_attn_out_b"], np.float32).reshape(1, C),
        "wr_T": t(inputs["L1_ffn_W_r"]),
        "wk_T": t(inputs["L1_ffn_W_k"]),
        "wv_T": t(inputs["L1_ffn_W_v"]),
        "mur_c": cols(inputs["L1_ffn_mu_r"], DC),
        "muk_c": cols(inputs["L1_ffn_mu_k"], DC),
        "outw_T": t(inputs["L1_attn_out_w"]),
    }
    for d in ("f", "b"):
        common[f"inw_T_{d}"] = t(inputs[f"L1_attn_in_w_{d}"])
        xp = t(inputs[f"L1_attn_xproj_w_{d}"])
        xpp = np.zeros((DI, 128), np.float32)
        xpp[:, 0:DTR + DS] = xp[:, 0:DTR + DS]
        xpp[:, 96:96 + DS] = xp[:, DTR + DS:]
        common[f"xproj_T_{d}"] = xpp
        common[f"dtw_T_{d}"] = t(inputs[f"L1_attn_dt_w_{d}"])
        common[f"mout_{d}"] = np.ascontiguousarray(
            np.asarray(inputs[f"L1_attn_mout_w_{d}"], np.float32))
        common[f"convw_{d}"] = np.ascontiguousarray(
            np.asarray(inputs[f"L1_attn_conv_w_{d}"], np.float32).reshape(DI, 4)
            .reshape(DIC, 128, 4).transpose(1, 0, 2))
        common[f"convb_c_{d}"] = cols(inputs[f"L1_attn_conv_b_{d}"], DIC)
        common[f"dtb_c_{d}"] = cols(inputs[f"L1_attn_dt_b_{d}"], DIC)
        common[f"dd_c_{d}"] = cols(inputs[f"L1_attn_D_{d}"], DIC)
        common[f"alog_{d}"] = np.ascontiguousarray(
            np.asarray(inputs[f"L1_attn_A_log_{d}"], np.float32)
            .reshape(DIC, 128, DS).transpose(1, 0, 2).reshape(128, DIC * DS))

    ident = np.eye(128)
    prev = ident[:, ::-1]
    tri0 = np.where(np.arange(LS)[None, :] <= np.arange(128)[:, None], 0.0, NEG)
    tri1 = np.where(np.arange(LS)[None, :] <= 128 + np.arange(128)[:, None], 0.0, NEG)
    sel = np.zeros((DS, DS * 128), np.float32)
    for s in range(DS):
        sel[s, s * 128:(s + 1) * 128] = 1.0
    common.update({
        "ident_bf": ident.astype(ml_dtypes.bfloat16),
        "ident_f": ident.astype(np.float32),
        "prev_bf": prev.astype(ml_dtypes.bfloat16),
        "prev_f": prev.astype(np.float32),
        "ones_bf": np.ones((1, 128), ml_dtypes.bfloat16),
        "ones_col_bf": np.ones((128, 1), ml_dtypes.bfloat16),
        "ones_f": np.ones((1, 128), np.float32),
        "tri0": tri0.astype(np.float32),
        "tri1": tri1.astype(np.float32),
        "sel_bf": sel.astype(ml_dtypes.bfloat16),
    })
    return common



def build_program(T=8192, chain=CHAIN, taps=(), dbg_level=99):
    nc = bass.Bass("TRN2", target_bir_lowering=False)
    NT = T // QT
    cx = Ctx()
    cx.dbg = dbg_level
    cx.nc = nc
    cx.T = T
    cx.NT = NT
    cx.chain = chain
    cx.taps = set(taps)
    cx.tap_dram = {}
    cx.slots = {}

    dram = {}

    def din(name, shape, dt):
        dram[name] = nc.dram_tensor(name, shape, dt, kind="ExternalInput").ap()

    din("x0", [T, C], F32)
    din("wq_T", [C, C], I8)
    din("w_qkv", [C, 3 * C], F32)
    din("w_proj", [C, C], F32)
    din("lnw_c", [128, DC], F32)
    din("wsc_c", [128, DC], F32)
    din("fbias_c", [128, DC], F32)
    din("fbias_r", [1, C], F32)
    din("gamma_r", [1, C], F32)
    din("beta_r", [1, C], F32)
    din("outb_r", [1, C], F32)
    din("wr_T", [C, C], F32)
    din("wk_T", [C, C], F32)
    din("wv_T", [C, C], F32)
    din("mur_c", [128, DC], F32)
    din("muk_c", [128, DC], F32)
    din("outw_T", [DI, C], F32)
    for d in ("f", "b"):
        din(f"inw_T_{d}", [C, 4 * C], F32)
        din(f"xproj_T_{d}", [DI, 128], F32)
        din(f"dtw_T_{d}", [DTR, DI], F32)
        din(f"mout_{d}", [C, DI], F32)
        din(f"convw_{d}", [128, DIC, 4], F32)
        din(f"convb_c_{d}", [128, DIC], F32)
        din(f"dtb_c_{d}", [128, DIC], F32)
        din(f"dd_c_{d}", [128, DIC], F32)
        din(f"alog_{d}", [128, DIC * DS], F32)
    din("ident_bf", [128, 128], BF16)
    din("ident_f", [128, 128], F32)
    din("prev_bf", [128, 128], BF16)
    din("prev_f", [128, 128], F32)
    din("ones_bf", [1, 128], BF16)
    din("ones_col_bf", [128, 1], BF16)
    din("ones_f", [1, 128], F32)
    din("tri0", [128, LS], F32)
    din("tri1", [128, LS], F32)
    din("sel_bf", [DS, DS * 128], BF16)

    dram["y"] = nc.dram_tensor("y", [T, C], F32, kind="ExternalOutput").ap()
    dram["s_x3"] = nc.dram_tensor("s_x3", [T, C], F32, kind="Internal").ap()
    dram["s_h"] = nc.dram_tensor("s_h", [T, C], BF16, kind="Internal").ap()
    dram["s_p5a"] = nc.dram_tensor("s_p5a", [T, C], F32, kind="Internal").ap()
    dram["s_p5b"] = nc.dram_tensor("s_p5b", [T, C], F32, kind="Internal").ap()
    dram["s_x5"] = nc.dram_tensor("s_x5", [T, C], F32, kind="Internal").ap()
    dram["b_wqkv"] = nc.dram_tensor("b_wqkv", [C, 3 * C], BF16, kind="Internal").ap()
    dram["b_inw_f"] = nc.dram_tensor("b_inw_f", [C, 4 * C], BF16, kind="Internal").ap()
    dram["b_inw_b"] = nc.dram_tensor("b_inw_b", [C, 4 * C], BF16, kind="Internal").ap()
    dram["b_outw_f"] = nc.dram_tensor("b_outw_f", [C, C], BF16, kind="Internal").ap()
    dram["b_outw_b"] = nc.dram_tensor("b_outw_b", [C, C], BF16, kind="Internal").ap()
    dram["b_weff_f"] = nc.dram_tensor("b_weff_f", [DI, C], BF16, kind="Internal").ap()
    dram["b_weff_b"] = nc.dram_tensor("b_weff_b", [DI, C], BF16, kind="Internal").ap()
    for tname in cx.taps:
        dram["tap_" + tname] = nc.dram_tensor(
            "tap_" + tname, [T, C], F32, kind="ExternalOutput").ap()
    cx.dram = dram

    with contextlib.ExitStack() as ctx:
        cx.ctx = ctx
        cx.wpool = Pool(nc, ctx, "wpool", 68 * 1024)
        cx.fpool = Pool(nc, ctx, "fpool", 52 * 1024)
        cx.bpool = Pool(nc, ctx, "bpool", 60 * 1024)
        cx.cpool = Pool(nc, ctx, "cpool", 19 * 1024)

        ps = []
        for i in range(5):
            ps.append(ctx.enter_context(nc.psum_tensor(f"psA{i}", [128, 512], F32)))
        cx.ps = ps
        cx.ps_tb = ctx.enter_context(nc.psum_tensor("ps_tb", [128, 2048], BF16))
        cx.ps_tf = ctx.enter_context(nc.psum_tensor("ps_tf", [128, 512], F32))
        cx.tb_par = 0

        cx.ld = Sem(nc, ctx, "ld", 16)
        cx.lw = Sem(nc, ctx, "lw", 16)
        cx.st = Sem(nc, ctx, "st", 16)
        cx.sv = Sem(nc, ctx, "sv")
        cx.ss = Sem(nc, ctx, "ss")
        cx.stn = Sem(nc, ctx, "stn")
        cx.sg = Sem(nc, ctx, "sg")

        cx.block = ctx.enter_context(nc.Block())
        _emit(cx)

    return nc


def _barrier(cx):
    nc = cx.nc
    cx.vpub()
    cx.spub()
    cx.tpub()
    cx.gpub()
    ticks = [(s, s.n) for s in (cx.sv, cx.ss, cx.stn, cx.sg, cx.ld, cx.st,
                                cx.lw)]
    for eng in (nc.sync, nc.vector, nc.scalar, nc.tensor, nc.gpsimd):
        for sem, n in ticks:
            eng.wait_ge(sem.h, n)
    cx.slots = {}


def _emit(cx):
    _load_consts(cx)
    _weight_prep(cx)
    _barrier(cx)
    last = {"ffn0a", "attn0", "ffn0b"} & set(cx.chain)
    p1_out = "y" if not ({"rwkva", "mamba", "rwkvb"} & set(cx.chain)) else "s_x3"
    _pass1(cx, out=p1_out)
    if "rwkva" in cx.chain:
        _barrier(cx)
        p2_out = "y" if "mamba" not in cx.chain else None
        _pass2(cx, out=p2_out)
    if "mamba" in cx.chain:
        _barrier(cx)
        _mamba_pass(cx, "f")
        _barrier(cx)
        _mamba_pass(cx, "b", out=("y" if "rwkvb" not in cx.chain else "s_x5"))
    if "rwkvb" in cx.chain:
        _barrier(cx)
        _pass5(cx)
    cx.nc.sync.wait_ge(cx.st.h, cx.st.n)



def _load_consts(cx):
    nc, d = cx.nc, cx.dram
    cc = {}
    full = [
        ("ident_bf", BF16, [128]), ("ident_f", F32, [128]),
        ("prev_bf", BF16, [128]), ("prev_f", F32, [128]),
        ("tri0", F32, [LS]), ("tri1", F32, [LS]),
        ("lnw_c", F32, [DC]), ("wsc_c", F32, [DC]), ("fbias_c", F32, [DC]),
        ("mur_c", F32, [DC]), ("muk_c", F32, [DC]),
        ("convw_f", F32, [DIC, 4]), ("convw_b", F32, [DIC, 4]),
        ("convb_c_f", F32, [DIC]), ("convb_c_b", F32, [DIC]),
        ("dtb_c_f", F32, [DIC]), ("dtb_c_b", F32, [DIC]),
        ("dd_c_f", F32, [DIC]), ("dd_c_b", F32, [DIC]),
        ("alog_f", F32, [DIC * DS]), ("alog_b", F32, [DIC * DS]),
        ("ones_col_bf", BF16, [1]),
    ]
    for name, dt, shape in full:
        buf = cx.cpool.alloc(dt, shape)
        cx.load(buf, d[name][:, :] if len(d[name].shape) == 2 else d[name][:, :, :])
        cc[name] = buf
    rows = [("ones_bf", BF16, 1, 128), ("ones_f", F32, 1, 128),
            ("sel_bf", BF16, DS, DS * 128)]
    for name, dt, np_, nf in rows:
        buf = cx.cpool.alloc(dt, [nf])
        cx.load(buf[0:np_], d[name][:, :])
        cc[name] = buf
    cx.cc = cc

    nc.scalar.wait_ge(cx.ld.h, cx.ld.n)
    for dgn in ("f", "b"):
        apos = cx.cpool.alloc(F32, [DIC * DS])
        nc.scalar.activation(out=apos, in_=cc[f"alog_{dgn}"], func=AF.Exp)
        cc[f"apos_{dgn}"] = apos.rearrange("p (c s) -> p c s", c=DIC)
        ndtb = cx.cpool.alloc(F32, [DIC])
        nc.scalar.mul(out=ndtb, in_=cc[f"dtb_c_{dgn}"], mul=-1.0)
        cc[f"ndtb_{dgn}"] = ndtb
    cx.spub()


def _build_bcast(cx, row_dram_name, pool, scale=None):
    nc, d, cc = cx.nc, cx.dram, cx.cc
    bc = pool.alloc(F32, [C])
    cx.fpool.mark()
    rowstage = cx.fpool.alloc(F32, [C])
    tl = cx.load(rowstage[0:1, :], d[row_dram_name][:, :])
    nc.tensor.wait_ge(cx.ld.h, tl)
    for j in range(C // 512):
        cx.slot_wait(nc.tensor, "b4")
        nc.tensor.matmul(cx.ps[4][:, :], cc["ones_f"][0:1, :],
                         rowstage[0:1, j * 512:(j + 1) * 512],
                         start=True, stop=True)
        tt = cx.tpub()
        nc.vector.wait_ge(cx.stn.h, tt)
        nc.vector.tensor_copy(out=bc[:, j * 512:(j + 1) * 512], in_=cx.ps[4][:, :])
        cx.slot_set("b4", cx.sv, cx.vpub())
    if scale is not None:
        nc.vector.tensor_scalar(out=bc, in0=bc, scalar1=scale, scalar2=None,
                                op0=OP.mult)
        cx.vpub()
    nc.sync.wait_ge(cx.stn.h, cx.stn.n)
    cx.fpool.release()
    return bc


def _weight_prep(cx):
    nc, d = cx.nc, cx.dram
    cx.fpool.mark()
    cx.bpool.mark()
    fbuf = cx.fpool.alloc(F32, [4096])
    bb = cx.bpool.alloc(BF16, [4096])
    jobs = [(d[f"inw_T_{g}"], d[f"b_inw_{g}"], C, 4 * C) for g in ("f", "b")]
    jobs += [(d["outw_T"][0:C, :], d["b_outw_f"], C, C),
             (d["outw_T"][C:2 * C, :], d["b_outw_b"], C, C),
             (d["w_qkv"], d["b_wqkv"], C, 3 * C)]
    for src, dst, rows, width in jobs:
        for r in range(rows // 128):
            t0 = cx.load(fbuf[:, 0:width], src[r * 128:(r + 1) * 128, :])
            nc.vector.wait_ge(cx.ld.h, t0)
            nc.vector.tensor_copy(out=bb[:, 0:width], in_=fbuf[:, 0:width])
            tick = cx.vpub()
            nc.sync.wait_ge(cx.sv.h, tick)
            t1 = cx.store(dst[r * 128:(r + 1) * 128, :], bb[:, 0:width])
            nc.sync.wait_ge(cx.st.h, t1)
    cx.fpool.release()
    cx.bpool.release()

    cx.wpool.mark()
    cx.fpool.mark()
    mo_f32 = cx.fpool.alloc(F32, [DI])
    mo_bf = cx.wpool.alloc(BF16, [DC, DI])
    ow_bf = cx.wpool.alloc(BF16, [DC, C])
    wout = cx.wpool.alloc(BF16, [C])
    for dgn in ("f", "b"):
        for kc in range(DC):
            nc.sync.wait_ge(cx.sv.h, cx.sv.n)
            t0 = cx.load(mo_f32, d[f"mout_{dgn}"][kc * 128:(kc + 1) * 128, :])
            nc.vector.wait_ge(cx.ld.h, t0)
            nc.vector.tensor_copy(out=mo_bf[:, kc], in_=mo_f32)
            cx.vpub()
            t1 = cx.load(ow_bf[:, kc], d[f"b_outw_{dgn}"][kc * 128:(kc + 1) * 128, :])
        tick = cx.vpub()
        nc.tensor.wait_ge(cx.sv.h, tick)
        nc.tensor.wait_ge(cx.ld.h, cx.ld.n)
        for mi in range(DIC):
            for nj in range(2):
                key = f"b{nj % 2}"
                cx.slot_wait(nc.tensor, key)
                pslot = cx.ps[nj % 2][:, :]
                for kc in range(DC):
                    nc.tensor.matmul(pslot, mo_bf[:, kc, mi * 128:(mi + 1) * 128],
                                     ow_bf[:, kc, nj * 512:(nj + 1) * 512],
                                     start=(kc == 0), stop=(kc == DC - 1))
                tt = cx.tpub()
                nc.scalar.wait_ge(cx.stn.h, tt)
                nc.scalar.copy(out=wout[:, nj * 512:(nj + 1) * 512], in_=pslot)
                cx.slot_set(key, cx.ss, cx.spub())
            nc.sync.wait_ge(cx.ss.h, cx.ss.n)
            ts = cx.store(d[f"b_weff_{dgn}"][mi * 128:(mi + 1) * 128, :], wout)
            nc.scalar.wait_ge(cx.st.h, ts)
    cx.wpool.release()
    cx.fpool.release()



def _ln_tm(cx, x_tm, xn_out, nsub, width=C, eps=EPS_LN, ret_ab=False):
    nc = cx.nc
    f = cx.fpool
    f.mark()
    s_sum = f.alloc(F32, [nsub])
    s_sq = f.alloc(F32, [nsub])
    junk = f.alloc(F32, [width])
    mean = f.alloc(F32, [nsub])
    nmrs = f.alloc(F32, [nsub])
    rstd = f.alloc(F32, [nsub])
    tmp = f.alloc(F32, [nsub])
    r2 = f.alloc(F32, [nsub])
    nc.vector.drain()
    nc.vector.wait_ge(cx.ss.h, cx.ss.n)
    nc.scalar.wait_ge(cx.sv.h, cx.sv.n)
    for i in range(nsub):
        nc.vector.reduce_sum(out=s_sum[:, i:i + 1], in_=x_tm[:, i], axis=AX.X)
        if i:
            nc.scalar.drain()
        nc.scalar.activation(out=junk, in_=x_tm[:, i], func=AF.Square,
                             accum_out=s_sq[:, i:i + 1])
    nc.vector.drain()
    tsq = cx.spub()
    nc.vector.wait_ge(cx.ss.h, tsq)
    inv_w = 1.0 / width
    nc.vector.tensor_scalar(out=mean, in0=s_sum, scalar1=inv_w, scalar2=None,
                            op0=OP.mult)
    nc.vector.drain()
    nc.vector.scalar_tensor_tensor(out=tmp, in0=mean, scalar=-1.0, in1=mean,
                                   op0=OP.mult, op1=OP.mult)
    nc.vector.drain()
    nc.vector.scalar_tensor_tensor(out=tmp, in0=s_sq, scalar=inv_w, in1=tmp,
                                   op0=OP.mult, op1=OP.add)
    nc.vector.drain()
    op = nc.vector.tensor_scalar(out=tmp, in0=tmp, scalar1=eps, scalar2=None,
                                 op0=OP.add)
    tick_v = cx.sv.inc(op)
    nc.scalar.wait_ge(cx.sv.h, tick_v)
    nc.scalar.activation(out=r2, in_=tmp, func=AF.Ln)
    nc.scalar.drain()
    op = nc.scalar.activation(out=rstd, in_=r2, func=AF.Exp, scale=-0.5)
    tick_s = cx.ss.inc(op)
    nc.vector.wait_ge(cx.ss.h, tick_s)
    nc.vector.scalar_tensor_tensor(out=nmrs, in0=mean, scalar=-1.0, in1=rstd,
                                   op0=OP.mult, op1=OP.mult)
    nc.vector.drain()
    tick_s = None
    if xn_out is not None:
        for i in range(nsub):
            nc.vector.tensor_scalar(out=xn_out[:, i],
                                    in0=x_tm[:, i], scalar1=rstd[:, i:i + 1],
                                    scalar2=nmrs[:, i:i + 1],
                                    op0=OP.mult, op1=OP.add)
        nc.vector.drain()
        tick_s = cx.vpub()
    if ret_ab:
        tick_v = cx.vpub()
        return tick_s, tick_v, rstd, nmrs
    f.release()
    return tick_s


def _tp128(cx, src_ap, consume, bf=True, perm=None, ident_slice=None):
    nc = cx.nc
    idn = cx.cc[("prev_" if perm == "rev" else "ident_") + ("bf" if bf else "f")]
    if ident_slice is not None:
        idn = idn[ident_slice, ident_slice]
    out_w = src_ap.shape[0]
    if bf:
        par = cx.tb_par
        cx.tb_par ^= 1
        key = ("tb", par)
        psl = cx.ps_tb[:, par * 1024:par * 1024 + out_w]
    else:
        key = ("tf", 0)
        psl = cx.ps_tf[:, 0:out_w]
    cx.slot_wait(nc.tensor, key)
    tp = nc.tensor.transpose(psl[0:src_ap.shape[-1], :], src_ap, idn)
    tick = cx.stn.inc(tp)
    cx.slot_set(key, *consume(psl[0:src_ap.shape[-1], :], tick))


def _transpose_to_fm(cx, src_tm, dst_fm, nsub, src_bf=True, perm=None,
                     n_feat=C, sub_map=None):
    nc = cx.nc
    fc = n_feat // 128
    for i in range(nsub):
        di = sub_map[i] if sub_map else i
        for j in range(fc):

            def consume(psl, tick, di=di, j=j):
                nc.scalar.wait_ge(cx.stn.h, tick)
                cp = nc.scalar.copy(out=dst_fm[:, j, di * 128:(di + 1) * 128],
                                    in_=psl)
                return cx.ss, cx.ss.inc(cp)

            _tp128(cx, src_tm[:, i, j * 128:(j + 1) * 128], consume,
                   bf=src_bf, perm=perm)
    return cx.ss.n


def _fm_matmul(cx, w_view, rhs_fm, kcs, mi, pslot, pkey, extra=None):
    nc = cx.nc
    cx.slot_wait(nc.tensor, pkey)
    n = len(kcs)
    mm = None
    for ii, kc in enumerate(kcs):
        mm = nc.tensor.matmul(pslot, w_view[:, kc, mi * 128:(mi + 1) * 128],
                              rhs_fm[:, kc], start=(ii == 0),
                              stop=(extra is None and ii == n - 1))
    if extra is not None:
        lhsT, rhs = extra
        mm = nc.tensor.matmul(pslot, lhsT, rhs, start=False, stop=True)
    return cx.stn.inc(mm)


def _residual_add_fm_delta(cx, delta_fm, x_tm, nsub, scale, extra_tm=None):
    nc = cx.nc
    for i in range(nsub):
        for j in range(DC):

            def consume(psl, tick, i=i, j=j):
                nc.vector.wait_ge(cx.stn.h, tick)
                dst = x_tm[:, i, j * 128:(j + 1) * 128]
                op = nc.vector.scalar_tensor_tensor(out=dst, in0=psl,
                                                    scalar=scale, in1=dst,
                                                    op0=OP.mult, op1=OP.add)
                if extra_tm is not None:
                    nc.vector.drain()
                    op = nc.vector.tensor_tensor(
                        out=dst, in0=dst,
                        in1=extra_tm[:, i, j * 128:(j + 1) * 128], op=OP.add)
                return cx.sv, cx.sv.inc(op)

            _tp128(cx, delta_fm[:, j, i * 128:(i + 1) * 128], consume, bf=False)
    return cx.sv.n



def _quant_ffn(cx, bufs, x_tm, wq_bf):
    nc = cx.nc
    cc = cx.cc
    xn1, xq_tm, xq_fm, out_fm, scr = bufs
    amax, xsch, t2, frac = scr
    _ln_tm(cx, x_tm, xn1, NSUB)
    if cx.dbg < 2:
        return
    for i in range(NSUB):
        nc.vector.reduce_max(out=amax[:, i:i + 1], in_=xn1[:, i], axis=AX.X,
                             apply_absolute_value=True)
    nc.vector.drain()
    nc.vector.tensor_scalar(out=amax, in0=amax, scalar1=EPS_Q, scalar2=1.0 / 127.0,
                            op0=OP.max, op1=OP.mult)
    nc.vector.drain()
    nc.vector.tensor_scalar(out=xsch, in0=amax, scalar1=0.5, scalar2=None,
                            op0=OP.mult)
    nc.vector.drain()
    nc.vector.reciprocal(out=amax, in_=amax)
    nc.vector.drain()
    i32 = frac.bitcast(mybir.dt.int32)
    fb = frac
    for i in range(NSUB):
        nc.vector.tensor_scalar(out=t2, in0=xn1[:, i], scalar1=amax[:, i:i + 1],
                                scalar2=512.5, op0=OP.mult, op1=OP.add)
        nc.vector.drain()
        nc.vector.tensor_copy(out=i32, in_=t2)
        nc.vector.drain()
        nc.vector.tensor_copy(out=xn1[:, i], in_=i32)
        nc.vector.drain()
        nc.vector.tensor_tensor(out=fb, in0=xn1[:, i], in1=t2, op=OP.is_gt)
        nc.vector.drain()
        nc.vector.tensor_tensor(out=t2, in0=xn1[:, i], in1=fb, op=OP.subtract)
        nc.vector.drain()
        nc.vector.tensor_scalar(out=t2, in0=t2, scalar1=639.0,
                                scalar2=384.0, op0=OP.min, op1=OP.max)
        nc.vector.drain()
        nc.vector.tensor_scalar(out=xq_tm[:, i], in0=t2, scalar1=-512.0,
                                scalar2=None, op0=OP.add)
        nc.vector.drain()
    tick_q = cx.vpub()
    if cx.dbg < 3:
        return
    nc.tensor.wait_ge(cx.sv.h, tick_q)
    _transpose_to_fm(cx, xq_tm, xq_fm, NSUB, src_bf=True)
    if cx.dbg < 4:
        return
    nc.tensor.wait_ge(cx.ss.h, cx.ss.n)
    for mi in range(DC):
        pkey = f"b{mi % 2}"
        pslot = cx.ps[mi % 2][:, 0:QT]
        tt = _fm_matmul(cx, wq_bf, xq_fm, range(DC), mi, pslot, pkey)
        nc.vector.wait_ge(cx.stn.h, tt)
        op = nc.vector.tensor_scalar(out=out_fm[:, mi], in0=pslot,
                                     scalar1=cc["wsc_c"][:, mi:mi + 1],
                                     scalar2=None, op0=OP.mult)
        cx.slot_set(pkey, cx.sv, cx.sv.inc(op))
    nc.tensor.wait_ge(cx.sv.h, cx.sv.n)
    for i in range(NSUB):
        for j in range(DC):

            def consume(psl, tick, i=i, j=j):
                nc.vector.wait_ge(cx.stn.h, tick)
                dst = x_tm[:, i, j * 128:(j + 1) * 128]
                nc.vector.scalar_tensor_tensor(
                    out=frac[:, 0:128], in0=psl, scalar=xsch[:, i:i + 1],
                    in1=cx.cc["halfbias_bc"][:, j * 128:(j + 1) * 128],
                    op0=OP.mult, op1=OP.add)
                nc.vector.drain()
                op = nc.vector.tensor_tensor(out=dst, in0=dst,
                                             in1=frac[:, 0:128], op=OP.add)
                return cx.sv, cx.sv.inc(op)

            _tp128(cx, out_fm[:, j, i * 128:(i + 1) * 128], consume, bf=False)


def _attn(cx, bufs, x_tm, wqkv_get, wproj_bf):
    nc = cx.nc
    cc = cx.cc
    (xn2, xn2_fm, qf, kf, vf, sq, qh, kh, vtm, scsb, probs_all, pT, yatt,
     scr) = bufs
    row, rqb_q, rqb_k, mx, nb, sume, rsum, dproj = scr
    tick_ln = _ln_tm(cx, x_tm, xn2, NSUB)
    nc.tensor.wait_ge(cx.sv.h, tick_ln)
    tick = _transpose_to_fm(cx, xn2, xn2_fm, NSUB, src_bf=True)
    nc.tensor.wait_ge(cx.ss.h, tick)
    for phase in range(3):
        wv, tl, wkey = wqkv_get(phase)
        nc.tensor.wait_ge(cx.ld.h, tl)
        for mi in range(DC):
            pkey = f"b{mi % 2}"
            pslot = cx.ps[mi % 2][:, 0:QT]
            tt = _fm_matmul(cx, wv, xn2_fm, range(DC), mi, pslot, pkey)
            nc.scalar.wait_ge(cx.stn.h, tt)
            dst = (qf, kf, vf)[phase][:, mi]
            cp = nc.scalar.copy(out=dst, in_=pslot)
            cx.slot_set(pkey, cx.ss, cx.ss.inc(cp))
        cx.slot_set(wkey, cx.stn, cx.stn.n)
    for name, src, rqb in (("q", qf, rqb_q), ("k", kf, rqb_k)):
        nc.scalar.wait_ge(cx.ss.h, cx.ss.n)
        for j in range(DC):
            nc.scalar.activation(out=sq[:, j], in_=src[:, j], func=AF.Square)
        ts = cx.spub()
        nc.tensor.wait_ge(cx.ss.h, ts)
        cx.slot_wait(nc.tensor, "b2")
        for j in range(DC):
            nc.tensor.matmul(cx.ps[2][0:1, 0:QT], cc["ones_col_bf"][:, 0:1],
                             sq[:, j], start=(j == 0), stop=(j == DC - 1))
        tt = cx.tpub()
        f64 = 64.0 if name == "q" else 1.0
        nc.vector.wait_ge(cx.stn.h, tt)
        op = nc.vector.tensor_scalar(out=row[0:1, :], in0=cx.ps[2][0:1, 0:QT],
                                     scalar1=f64 / C, scalar2=f64 * EPS_RMS,
                                     op0=OP.mult, op1=OP.add)
        cx.slot_set("b2", cx.sv, cx.sv.inc(op))
        nc.scalar.wait_ge(cx.sv.h, cx.sv.n)
        nc.scalar.activation(out=row[0:1, :], in_=row[0:1, :], func=AF.Ln)
        nc.scalar.drain()
        op = nc.scalar.activation(out=row[0:1, :], in_=row[0:1, :],
                                  func=AF.Exp, scale=-0.5)
        ts = cx.ss.inc(op)
        nc.tensor.wait_ge(cx.ss.h, ts)
        cx.slot_wait(nc.tensor, "b4")
        nc.tensor.matmul(cx.ps[4][:, 0:QT], cc["ones_f"][0:1, :], row[0:1, 0:QT],
                         start=True, stop=True)
        tt = cx.tpub()
        nc.scalar.wait_ge(cx.stn.h, tt)
        cp = nc.scalar.copy(out=rqb, in_=cx.ps[4][:, 0:QT])
        cx.slot_set("b4", cx.ss, cx.ss.inc(cp))
    nc.vector.wait_ge(cx.ss.h, cx.ss.n)
    for (src, dst, rqb) in ((qf, qh, rqb_q), (kf, kh, rqb_k)):
        for j in range(DC):
            nc.vector.scalar_tensor_tensor(out=dst[:, j], in0=src[:, j],
                                           scalar=cc["lnw_c"][:, j:j + 1],
                                           in1=rqb, op0=OP.mult, op1=OP.mult)
        nc.vector.drain()
    tick_qh = cx.vpub()
    nc.tensor.wait_ge(cx.ss.h, cx.ss.n)
    for h in range(NH):
        ch, r0 = h // 2, (h % 2) * 64
        for kc in range(NSUB):

            def consume(psl, tick, h=h, kc=kc):
                nc.scalar.wait_ge(cx.stn.h, tick)
                cp = nc.scalar.copy(out=vtm[:, h, kc * 64:(kc + 1) * 64],
                                    in_=psl)
                return cx.ss, cx.ss.inc(cp)

            _tp128(cx, vf[r0:r0 + 64, ch, kc * 128:(kc + 1) * 128], consume,
                   bf=True, ident_slice=slice(r0, r0 + 64))
    tick_vtm = cx.ss.n
    nc.tensor.wait_ge(cx.sv.h, tick_qh)
    nc.tensor.wait_ge(cx.ss.h, tick_vtm)
    cx.slot_set("b2", cx.sv, cx.sv.n)
    cx.slot_set("b3", cx.sv, cx.sv.n)
    cx.slot_set("b4", cx.ss, cx.ss.n)
    jobs = [(h, qs) for h in range(NH) for qs in range(NSUB)]
    for j, (h, qs) in enumerate(jobs):
        ch, r0 = h // 2, (h % 2) * 64
        jp = j % 4
        pkey = f"b{2 + (j % 2)}"
        cx.slot_wait(nc.tensor, pkey)
        pslot = cx.ps[2 + (j % 2)][:, 0:QT]
        nc.tensor.matmul(pslot, qh[r0:r0 + 64, ch, qs * 128:(qs + 1) * 128],
                         kh[r0:r0 + 64, ch, 0:QT], start=True, stop=True)
        tt = cx.tpub()
        nc.vector.wait_ge(cx.stn.h, tt)
        cx.slot_wait(nc.vector, ("scsb", jp))
        op = nc.vector.tensor_tensor(out=scsb[:, jp], in0=pslot,
                                     in1=cc["tri0" if qs == 0 else "tri1"],
                                     op=OP.add)
        cx.slot_set(pkey, cx.sv, cx.sv.inc(op))
        nc.vector.drain()
        nc.vector.reduce_max(out=mx[:, j:j + 1], in_=scsb[:, jp], axis=AX.X)
        nc.vector.drain()
        op = nc.vector.tensor_scalar(out=nb[:, j:j + 1], in0=mx[:, j:j + 1],
                                     scalar1=-1.0, scalar2=None, op0=OP.mult)
        tick_v = cx.sv.inc(op)
        nc.scalar.wait_ge(cx.sv.h, tick_v)
        op = nc.scalar.activation(out=probs_all[:, h, qs], in_=scsb[:, jp],
                                  func=AF.Exp, scale=1.0, bias=nb[:, j:j + 1],
                                  accum_out=sume[:, j:j + 1])
        ts = cx.ss.inc(op)
        cx.slot_set(("scsb", jp), cx.ss, ts)
        nc.vector.wait_ge(cx.ss.h, ts)
        nc.vector.reciprocal(out=rsum[:, j:j + 1], in_=sume[:, j:j + 1])
        nc.vector.drain()
        op = nc.vector.tensor_scalar(out=probs_all[:, h, qs],
                                     in0=probs_all[:, h, qs],
                                     scalar1=rsum[:, j:j + 1], scalar2=None,
                                     op0=OP.mult)
        cx.sv.inc(op)
    tick_p = cx.sv.n
    nc.tensor.wait_ge(cx.sv.h, tick_p)
    for h in range(NH):
        ch, r0 = h // 2, (h % 2) * 64
        for qs in range(NSUB):
            for kc in range(NSUB):

                def consume(psl, tick, kc=kc, qs=qs):
                    nc.scalar.wait_ge(cx.stn.h, tick)
                    cp = nc.scalar.copy(out=pT[:, kc, qs * 128:(qs + 1) * 128],
                                        in_=psl)
                    return cx.ss, cx.ss.inc(cp)

                _tp128(cx, probs_all[:, h, qs, kc * 128:(kc + 1) * 128],
                       consume, bf=True)
        ts = cx.ss.n
        nc.tensor.wait_ge(cx.ss.h, ts)
        pkey = "b4"
        cx.slot_wait(nc.tensor, pkey)
        pslot = cx.ps[4][0:64, 0:QT]
        mm = None
        for kc in range(NSUB):
            mm = nc.tensor.matmul(pslot, vtm[:, h, kc * 64:(kc + 1) * 64],
                                  pT[:, kc, 0:QT], start=(kc == 0),
                                  stop=(kc == NSUB - 1))
        tt = cx.stn.inc(mm)
        nc.scalar.wait_ge(cx.stn.h, tt)
        cp = nc.scalar.copy(out=yatt[r0:r0 + 64, ch], in_=pslot)
        cx.slot_set(pkey, cx.ss, cx.ss.inc(cp))
    nc.tensor.wait_ge(cx.ss.h, cx.ss.n)
    for mi in range(DC):
        pkey = f"b{mi % 2}"
        pslot = cx.ps[mi % 2][:, 0:QT]
        tt = _fm_matmul(cx, wproj_bf, yatt, range(DC), mi, pslot, pkey)
        nc.scalar.wait_ge(cx.stn.h, tt)
        cp = nc.scalar.copy(out=dproj[:, mi], in_=pslot)
        cx.slot_set(pkey, cx.ss, cx.ss.inc(cp))
    nc.tensor.wait_ge(cx.ss.h, cx.ss.n)
    _residual_add_fm_delta(cx, dproj, x_tm, NSUB, 1.0)




def _pass1(cx, out="s_x3"):
    nc, d, cc = cx.nc, cx.dram, cx.cc
    NT = cx.NT
    cx.wpool.mark()
    cx.fpool.mark()
    cx.bpool.mark()
    wq_bf = cx.wpool.alloc(BF16, [DC, C])
    wproj_bf = cx.wpool.alloc(BF16, [DC, C])
    wqkv_sl = [cx.wpool.alloc(BF16, [DC, C]) for _ in range(2)]
    cx.cc["halfbias_bc"] = _build_bcast(cx, "fbias_r", cx.wpool, scale=0.5)
    stage = cx.fpool.alloc(F32, [C])
    stage_i8 = cx.bpool.alloc(I8, [C])
    for kc in range(DC):
        t0 = cx.load(stage_i8, d["wq_T"][kc * 128:(kc + 1) * 128, :])
        nc.vector.wait_ge(cx.ld.h, t0)
        nc.vector.tensor_copy(out=wq_bf[:, kc], in_=stage_i8)
        nc.vector.drain()
        t0 = cx.load(stage, d["w_proj"][kc * 128:(kc + 1) * 128, :])
        nc.vector.wait_ge(cx.ld.h, t0)
        nc.vector.tensor_copy(out=wproj_bf[:, kc], in_=stage)
        tick_w = cx.vpub()
        nc.sync.wait_ge(cx.sv.h, tick_w)

    def wqkv_get(phase):
        slot = phase % 2
        key = ("wqkv", slot)
        cx.slot_wait(nc.sync, key)
        tl = cx.load(wqkv_sl[slot],
                     d["b_wqkv"][:, phase * C:(phase + 1) * C]
                     .rearrange("(a p) c -> p a c", p=128))
        return wqkv_sl[slot], tl, key

    xa = [cx.fpool.alloc(F32, [NSUB, C])]
    xn1 = cx.fpool.alloc(F32, [NSUB, C])
    xq_tm = cx.bpool.alloc(BF16, [NSUB, C])
    xq_fm = cx.bpool.alloc(BF16, [DC, QT])
    out_fm = cx.fpool.alloc(F32, [DC, QT])
    scrq = (cx.fpool.alloc(F32, [NSUB]), cx.fpool.alloc(F32, [NSUB]),
            cx.fpool.alloc(F32, [C]), cx.fpool.alloc(F32, [C]))
    xn2 = cx.bpool.alloc(BF16, [NSUB, C])
    xn2_fm = cx.bpool.alloc(BF16, [DC, QT])
    qf = cx.bpool.alloc(BF16, [DC, QT])
    kf = cx.bpool.alloc(BF16, [DC, QT])
    vf = cx.bpool.alloc(BF16, [DC, QT])
    sq = xn2_fm
    vtm = cx.bpool.alloc(BF16, [NH, NSUB * 64])
    scsb = cx.fpool.alloc(F32, [4, QT])
    probs_all = cx.bpool.alloc(BF16, [NH, NSUB, QT])
    pT = cx.bpool.alloc(BF16, [NSUB, QT])
    yatt = cx.bpool.alloc(BF16, [DC, QT])
    NJ = NH * NSUB
    scra = (cx.fpool.alloc(F32, [QT]), cx.fpool.alloc(F32, [QT]),
            cx.fpool.alloc(F32, [QT]), cx.fpool.alloc(F32, [NJ]),
            cx.fpool.alloc(F32, [NJ]), cx.fpool.alloc(F32, [NJ]),
            cx.fpool.alloc(F32, [NJ]), out_fm)
    qb1 = (xn1, xq_tm, xq_fm, out_fm, scrq)
    ab = (xn2, xn2_fm, qf, kf, vf, sq, qf, kf, vtm, scsb, probs_all, pT, yatt,
          scra)

    tile_end = {}
    for k in range(NT):
        p = 0
        if k >= 1:
            for sem, tick in tile_end[k - 1]:
                for eng in (nc.vector, nc.scalar, nc.tensor):
                    eng.wait_ge(sem.h, tick)
        if k >= 1:
            for sem, tick in tile_end[k - 1]:
                nc.sync.wait_ge(sem.h, tick)
        tk = cx.load(xa[p], d["x0"][k * QT:(k + 1) * QT, :]
                     .rearrange("(a p) c -> p a c", p=128))
        for eng in (nc.vector, nc.scalar, nc.tensor):
            eng.wait_ge(cx.ld.h, tk)
        if "ffn0a" in cx.chain:
            _quant_ffn(cx, qb1, xa[p], wq_bf)
        if "attn0" in cx.chain:
            _attn(cx, ab, xa[p], wqkv_get, wproj_bf)
        if "ffn0b" in cx.chain:
            _quant_ffn(cx, qb1, xa[p], wq_bf)
        tick_v = cx.vpub()
        nc.sync.wait_ge(cx.sv.h, tick_v)
        cx.store(d[out][k * QT:(k + 1) * QT, :]
                 .rearrange("(a p) c -> p a c", p=128), xa[p])
        tile_end[k] = [(cx.sv, cx.sv.n), (cx.ss, cx.ss.n), (cx.stn, cx.stn.n),
                       (cx.st, cx.st.n)]
    cx.wpool.release()
    cx.fpool.release()
    cx.bpool.release()



def _rwkv_delta(cx, bufs, x_tm, weights, carry, first):
    nc, cc = cx.nc, cx.cc
    (xn4, xf_buf, xr, rbf, kbuf, dproj) = bufs
    wr_bf, wk_bf, wv_bf, omr, omk = weights
    tick_ln = _ln_tm(cx, x_tm, xn4, NSUB)
    nc.vector.wait_ge(cx.ss.h, cx.ss.n)
    if first:
        nc.vector.memset(xf_buf[:, :, 0:1], 0.0)
    else:
        nc.vector.tensor_copy(out=xf_buf[:, :, 0:1], in_=carry)
    tcar = cx.vpub()
    nc.tensor.wait_ge(cx.sv.h, tick_ln)
    tick = _transpose_to_fm(cx, xn4, xf_buf.rearrange("p a b -> p a b")
                            [:, :, 1:QT + 1], NSUB)
    nc.vector.wait_ge(cx.ss.h, tick)
    nc.vector.tensor_copy(out=carry, in_=xf_buf[:, :, QT:QT + 1])
    nc.vector.drain()
    for phase in range(2):
        mu = cc["mur_c"] if phase == 0 else cc["muk_c"]
        om = omr if phase == 0 else omk
        if phase == 1:
            nc.vector.wait_ge(cx.stn.h, cx.stn.n)
        for j in range(DC):
            nc.vector.tensor_scalar(out=xr[:, j], in0=xf_buf[:, j, 0:QT],
                                    scalar1=om[:, j:j + 1], scalar2=None,
                                    op0=OP.mult)
            nc.vector.drain()
            nc.vector.scalar_tensor_tensor(out=xr[:, j],
                                           in0=xf_buf[:, j, 1:QT + 1],
                                           scalar=mu[:, j:j + 1], in1=xr[:, j],
                                           op0=OP.mult, op1=OP.add)
            nc.vector.drain()
        tick_mix = cx.vpub()
        nc.tensor.wait_ge(cx.sv.h, tick_mix)
        w = wr_bf if phase == 0 else wk_bf
        for mi in range(DC):
            pkey = f"b{mi % 2}"
            pslot = cx.ps[mi % 2][:, 0:QT]
            tt = _fm_matmul(cx, w, xr, range(DC), mi, pslot, pkey)
            nc.scalar.wait_ge(cx.stn.h, tt)
            if phase == 0:
                cp = nc.scalar.activation(out=rbf[:, mi], in_=pslot,
                                          func=AF.Sigmoid)
            else:
                cp = nc.scalar.activation(out=kbuf[:, mi], in_=pslot,
                                          func=AF.Relu)
            cx.slot_set(pkey, cx.ss, cx.ss.inc(cp))
    nc.vector.wait_ge(cx.ss.h, cx.ss.n)
    for j in range(DC):
        nc.vector.tensor_tensor(out=kbuf[:, j], in0=kbuf[:, j], in1=kbuf[:, j],
                                op=OP.mult)
    nc.vector.drain()
    tk2 = cx.vpub()
    nc.tensor.wait_ge(cx.sv.h, tk2)
    for mi in range(DC):
        pkey = f"b{mi % 2}"
        pslot = cx.ps[mi % 2][:, 0:QT]
        tt = _fm_matmul(cx, wv_bf, kbuf, range(DC), mi, pslot, pkey)
        nc.vector.wait_ge(cx.stn.h, tt)
        op = nc.vector.tensor_tensor(out=dproj[:, mi], in0=rbf[:, mi],
                                     in1=pslot, op=OP.mult)
        cx.slot_set(pkey, cx.sv, cx.sv.inc(op))
    nc.tensor.wait_ge(cx.sv.h, cx.sv.n)


def _rwkv_weights(cx):
    nc, d = cx.nc, cx.dram
    wr_bf = cx.wpool.alloc(BF16, [DC, C])
    wk_bf = cx.wpool.alloc(BF16, [DC, C])
    wv_bf = cx.wpool.alloc(BF16, [DC, C])
    stage = cx.fpool.alloc(F32, [C])
    for name, dst in (("wr_T", wr_bf), ("wk_T", wk_bf), ("wv_T", wv_bf)):
        for kc in range(DC):
            t0 = cx.load(stage, d[name][kc * 128:(kc + 1) * 128, :])
            nc.vector.wait_ge(cx.ld.h, t0)
            nc.vector.tensor_copy(out=dst[:, kc], in_=stage)
            tw = cx.vpub()
            nc.sync.wait_ge(cx.sv.h, tw)
    omr = cx.wpool.alloc(F32, [DC])
    omk = cx.wpool.alloc(F32, [DC])
    nc.vector.wait_ge(cx.ld.h, cx.ld.n)
    nc.vector.tensor_scalar(out=omr, in0=cx.cc["mur_c"], scalar1=-1.0,
                            scalar2=1.0, op0=OP.mult, op1=OP.add)
    nc.vector.drain()
    nc.vector.tensor_scalar(out=omk, in0=cx.cc["muk_c"], scalar1=-1.0,
                            scalar2=1.0, op0=OP.mult, op1=OP.add)
    cx.vpub()
    return (wr_bf, wk_bf, wv_bf, omr, omk)


def _pass2(cx, out=None):
    nc, d, cc = cx.nc, cx.dram, cx.cc
    NT = cx.NT
    cx.wpool.mark()
    cx.fpool.mark()
    cx.bpool.mark()
    weights = _rwkv_weights(cx)
    cc["gamma_bc"] = _build_bcast(cx, "gamma_r", cx.wpool)
    cc["beta_bc"] = _build_bcast(cx, "beta_r", cx.wpool)
    cc["outb_bc"] = _build_bcast(cx, "outb_r", cx.wpool)

    xa = cx.fpool.alloc(F32, [NSUB, C])
    a_tm = cx.fpool.alloc(F32, [NSUB, C])
    h1 = cx.fpool.alloc(F32, [NSUB, C])
    dproj = cx.fpool.alloc(F32, [DC, QT])
    xn4 = cx.bpool.alloc(BF16, [NSUB, C])
    xf_buf = cx.bpool.alloc(BF16, [DC, QT + 1])
    xr = cx.bpool.alloc(BF16, [DC, QT])
    rbf = cx.bpool.alloc(BF16, [DC, QT])
    kbuf = cx.bpool.alloc(BF16, [DC, QT])
    h_tm = cx.bpool.alloc(BF16, [NSUB, C])
    carry = cx.bpool.alloc(BF16, [DC, 1])
    rw_bufs = (xn4, xf_buf, xr, rbf, kbuf, dproj)

    tile_end = {}
    for k in range(NT):
        if k >= 1:
            for sem, tick in tile_end[k - 1]:
                for eng in (nc.vector, nc.scalar, nc.tensor):
                    eng.wait_ge(sem.h, tick)
                nc.sync.wait_ge(sem.h, tick)
        tk = cx.load(xa, d["s_x3"][k * QT:(k + 1) * QT, :]
                     .rearrange("(a p) c -> p a c", p=128))
        for eng in (nc.vector, nc.scalar, nc.tensor):
            eng.wait_ge(cx.ld.h, tk)
        _rwkv_delta(cx, rw_bufs, xa, weights, carry, first=(k == 0))
        _residual_add_fm_delta(cx, dproj, xa, NSUB, 0.5)
        _ln_tm(cx, xa, a_tm, NSUB)
        _ln_tm(cx, a_tm, h1, NSUB)
        for i in range(NSUB):
            nc.vector.tensor_tensor(out=h1[:, i], in0=h1[:, i],
                                    in1=cc["gamma_bc"], op=OP.mult)
            nc.vector.drain()
            nc.vector.tensor_tensor(out=h_tm[:, i], in0=h1[:, i],
                                    in1=cc["beta_bc"], op=OP.add)
            nc.vector.drain()
        for i in range(NSUB):
            nc.vector.tensor_tensor(out=h1[:, i], in0=xa[:, i], in1=a_tm[:, i],
                                    op=OP.add)
            nc.vector.drain()
            nc.vector.tensor_tensor(out=h1[:, i], in0=h1[:, i],
                                    in1=cc["outb_bc"], op=OP.add)
            nc.vector.drain()
        tick_v = cx.vpub()
        nc.sync.wait_ge(cx.sv.h, tick_v)
        cx.store(d["s_h"][k * QT:(k + 1) * QT, :]
                 .rearrange("(a p) c -> p a c", p=128), h_tm)
        cx.store(d["s_p5a"][k * QT:(k + 1) * QT, :]
                 .rearrange("(a p) c -> p a c", p=128), h1)
        if out:
            cx.store(d[out][k * QT:(k + 1) * QT, :]
                     .rearrange("(a p) c -> p a c", p=128), xa)
        tile_end[k] = [(cx.sv, cx.sv.n), (cx.ss, cx.ss.n), (cx.stn, cx.stn.n),
                       (cx.st, cx.st.n)]
    cx.wpool.release()
    cx.fpool.release()
    cx.bpool.release()


def _mamba_pass(cx, dgn, out=None):
    nc, d, cc = cx.nc, cx.dram, cx.cc
    NT = cx.NT
    rev = dgn == "b"
    src_p5 = "s_p5a" if dgn == "f" else "s_p5b"
    dst_p5 = "s_p5b" if dgn == "f" else (out or "s_x5")
    cx.wpool.mark()
    cx.fpool.mark()
    cx.bpool.mark()
    inw_sl = [cx.wpool.alloc(BF16, [DC, C]) for _ in range(2)]
    weff_sl = [cx.wpool.alloc(BF16, [DIC, 256]) for _ in range(2)]
    xproj_bf = cx.wpool.alloc(BF16, [DIC, 128])
    dtw_bf = cx.wpool.alloc(BF16, [DI])
    cx.fpool.mark()
    stage = cx.fpool.alloc(F32, [DI])
    for kc in range(DIC):
        t0 = cx.load(stage[:, 0:128], d[f"xproj_T_{dgn}"][kc * 128:(kc + 1) * 128, :])
        nc.vector.wait_ge(cx.ld.h, t0)
        nc.vector.tensor_copy(out=xproj_bf[:, kc], in_=stage[:, 0:128])
        tw = cx.vpub()
        nc.sync.wait_ge(cx.sv.h, tw)
    t0 = cx.load(stage[0:DTR, :], d[f"dtw_T_{dgn}"][:, :])
    nc.vector.wait_ge(cx.ld.h, t0)
    nc.vector.tensor_copy(out=dtw_bf[0:DTR, :], in_=stage[0:DTR, :])
    cx.vpub()
    nc.sync.wait_ge(cx.sv.h, cx.sv.n)
    cx.fpool.release()

    h_tm = cx.fpool.alloc(F32, [0]) if False else cx.bpool.alloc(BF16, [NSUB, C])
    h_fm = cx.bpool.alloc(BF16, [DC, QT])
    xi_buf = cx.bpool.alloc(BF16, [DIC, QT + 3])
    zs_fm = cx.bpool.alloc(BF16, [DIC, QT])
    xc_fm = cx.bpool.alloc(BF16, [DIC, QT])
    rows_bf = cx.bpool.alloc(BF16, [QT])
    bnr_bf = cx.bpool.alloc(BF16, [QT])
    ccr_bf = cx.bpool.alloc(BF16, [QT])
    du_bf = cx.bpool.alloc(BF16, [QT])
    bb_all = cx.bpool.alloc(BF16, [DS, QT])
    cb_all = cx.bpool.alloc(BF16, [DS, QT])
    w_half = [cx.bpool.alloc(BF16, [DS // 2, QT]) for _ in range(2)]
    hall = xi_buf.rearrange("p a b -> p (a b)")[:, 0:DS * QT] \
        .rearrange("p (s q) -> p s q", s=DS)
    yp_fm = zs_fm

    dn_fm = cx.fpool.alloc(F32, [DIC, QT])
    e_half = [cx.fpool.alloc(F32, [DS // 2, QT]) for _ in range(2)]
    sgbuf = cx.fpool.alloc(F32, [QT])
    convacc2 = [cx.fpool.alloc(F32, [QT]) for _ in range(2)]
    cgtmp = cx.fpool.alloc(F32, [QT])
    t1f = cx.fpool.alloc(F32, [QT])
    ys_f = cx.fpool.alloc(F32, [QT])
    fix8 = cx.fpool.alloc(F32, [DS // 2])
    xdbl = cx.fpool.alloc(F32, [QT])
    p5_tm = cx.fpool.alloc(F32, [NSUB, C])
    state = cx.fpool.alloc(F32, [DIC, DS])
    carry3 = cx.bpool.alloc(BF16, [DIC, 3])
    pevac = [cx.fpool.alloc(F32, [QT]) for _ in range(2)]

    nc.vector.wait_ge(cx.ld.h, cx.ld.n)
    nc.vector.memset(state.rearrange("p a b -> p (a b)"), 0.0)
    nc.vector.drain()
    nc.vector.memset(carry3.rearrange("p a b -> p (a b)"), 0.0)
    tick0 = cx.vpub()
    nc.sync.wait_ge(cx.sv.h, tick0)

    sub_map = list(range(NSUB - 1, -1, -1)) if rev else None
    perm = "rev" if rev else None

    def stream_w(dram_ap, slot_bufs, qi, key_pfx, view):
        slot = qi % 2
        key = (key_pfx, slot)
        cx.slot_wait(nc.sync, key)
        tl = cx.load(slot_bufs[slot] if view is None else view(slot_bufs[slot]),
                     dram_ap)
        return slot_bufs[slot], tl, key

    tile_end = {}
    for kk in range(NT):
        k = NT - 1 - kk if rev else kk
        if kk >= 1:
            for sem, tick in tile_end[kk - 1]:
                for eng in (nc.vector, nc.scalar, nc.tensor, nc.gpsimd):
                    eng.wait_ge(sem.h, tick)
                nc.sync.wait_ge(sem.h, tick)
        tk = cx.load(h_tm, d["s_h"][k * QT:(k + 1) * QT, :]
                     .rearrange("(a p) c -> p a c", p=128))
        for eng in (nc.vector, nc.scalar, nc.tensor):
            eng.wait_ge(cx.ld.h, tk)
        _transpose_to_fm(cx, h_tm, h_fm, NSUB, src_bf=True, perm=perm,
                         sub_map=sub_map)
        nc.vector.wait_ge(cx.ss.h, cx.ss.n)
        nc.vector.tensor_copy(out=xi_buf[:, :, 0:3], in_=carry3[:, :, :])
        tcar = cx.vpub()
        nc.tensor.wait_ge(cx.sv.h, tcar)
        cw = cc[f"convw_{dgn}"]
        silu_tick = {}
        for qg in range(4):
            wv, tl, wkey = stream_w(
                d[f"b_inw_{dgn}"][:, qg * C:(qg + 1) * C]
                .rearrange("(a p) c -> p a c", p=128), inw_sl, qg, "inw", None)
            nc.tensor.wait_ge(cx.ld.h, tl)
            for mi in range(8):
                mg = qg * 8 + mi
                pkey = f"b{mi % 2}"
                pslot = cx.ps[mi % 2][:, 0:QT]
                tt = _fm_matmul(cx, wv, h_fm, range(DC), mi, pslot, pkey)
                nc.scalar.wait_ge(cx.stn.h, tt)
                if mg < DIC:
                    cp = nc.scalar.copy(out=xi_buf[:, mg, 3:QT + 3], in_=pslot)
                    te = cx.ss.inc(cp)
                    cx.slot_set(pkey, cx.ss, te)
                    c = mg
                    ca = convacc2[c % 2]
                    nc.vector.wait_ge(cx.ss.h, te)
                    if c >= 2:
                        nc.vector.wait_ge(cx.ss.h, silu_tick[c - 2])
                    nc.vector.tensor_scalar(out=ca, in0=xi_buf[:, c, 0:QT],
                                            scalar1=cw[:, c, 0:1], scalar2=None,
                                            op0=OP.mult)
                    nc.vector.drain()
                    for j in range(1, 4):
                        nc.vector.scalar_tensor_tensor(
                            out=ca, in0=xi_buf[:, c, j:j + QT],
                            scalar=cw[:, c, j:j + 1], in1=ca,
                            op0=OP.mult, op1=OP.add)
                        nc.vector.drain()
                    tcv = cx.sv.inc(nc.vector.drain())
                    nc.scalar.wait_ge(cx.sv.h, tcv)
                    cp2 = nc.scalar.activation(
                        out=xc_fm[:, c], in_=ca, func=AF.Silu,
                        bias=cc[f"convb_c_{dgn}"][:, c:c + 1])
                    silu_tick[c] = cx.ss.inc(cp2)
                else:
                    cp = nc.scalar.activation(out=zs_fm[:, mg - DIC], in_=pslot,
                                              func=AF.Silu)
                    cx.slot_set(pkey, cx.ss, cx.ss.inc(cp))
            cx.slot_set(wkey, cx.stn, cx.stn.n)
        nc.vector.tensor_copy(out=carry3[:, :, :], in_=xi_buf[:, :, QT:QT + 3])
        nc.vector.drain()
        cx.slot_wait(nc.tensor, "b2")
        for kc in range(DIC):
            nc.tensor.wait_ge(cx.ss.h, silu_tick[kc])
            nc.tensor.matmul(cx.ps[2][:, 0:QT], xproj_bf[:, kc],
                             xc_fm[:, kc], start=(kc == 0), stop=(kc == DIC - 1))
        tt = cx.tpub()
        nc.scalar.wait_ge(cx.stn.h, tt)
        nc.scalar.copy(out=xdbl[:, :], in_=cx.ps[2][:, 0:QT])
        tsd = cx.spub()
        cx.slot_set("b2", cx.ss, tsd)
        nc.vector.wait_ge(cx.ss.h, tsd)
        nc.vector.tensor_copy(out=rows_bf[0:DTR, :], in_=xdbl[0:DTR, :])
        nc.vector.drain()
        nc.vector.tensor_scalar(out=bnr_bf[0:DS, :],
                                in0=xdbl[DTR:DTR + DS, :], scalar1=-1.0,
                                scalar2=None, op0=OP.mult)
        nc.vector.drain()
        nc.vector.tensor_copy(out=ccr_bf[0:DS, :], in_=xdbl[96:96 + DS, :])
        trows = cx.vpub()
        nc.tensor.wait_ge(cx.sv.h, trows)
        for s in range(DS):
            for (rsrc, dst) in ((bnr_bf, bb_all), (ccr_bf, cb_all)):
                cx.slot_wait(nc.tensor, "b3")
                nc.tensor.matmul(cx.ps[3][:, 0:QT],
                                 cc["sel_bf"][0:DS, s * 128:(s + 1) * 128],
                                 rsrc[0:DS, 0:QT],
                                 start=True, stop=True)
                tt = cx.tpub()
                nc.vector.wait_ge(cx.stn.h, tt)
                op = nc.vector.tensor_copy(out=dst[:, s], in_=cx.ps[3][:, 0:QT])
                cx.slot_set("b3", cx.sv, cx.sv.inc(op))
        SH = DS // 2
        half_i = 0
        e_cons = {}
        for c in range(DIC):
            pkey = f"b{c % 2}"
            pslot = cx.ps[c % 2][:, 0:QT]
            cx.slot_wait(nc.tensor, pkey)
            nc.tensor.matmul(pslot, dtw_bf[0:DTR, c * 128:(c + 1) * 128],
                             rows_bf[0:DTR, 0:QT], start=True, stop=True)
            tt = cx.tpub()
            nc.scalar.wait_ge(cx.stn.h, tt)
            nc.scalar.activation(out=sgbuf, in_=pslot, func=AF.Sigmoid,
                                 scale=-1.0, bias=cc[f"ndtb_{dgn}"][:, c:c + 1])
            nc.scalar.drain()
            op = nc.scalar.activation(out=dn_fm[:, c], in_=sgbuf, func=AF.Ln)
            ts_dn = cx.ss.inc(op)
            cx.slot_set(pkey, cx.ss, ts_dn)
            nc.scalar.drain()
            nc.vector.wait_ge(cx.ss.h, ts_dn)
            du = nc.vector.tensor_tensor(out=du_bf, in0=dn_fm[:, c],
                                         in1=xc_fm[:, c], op=OP.mult)
            nc.vector.drain()
            for hf in range(2):
                par = half_i % 2
                eb = e_half[par]
                wb = w_half[par]
                s0 = hf * SH
                if par in e_cons:
                    nc.scalar.wait_ge(cx.sv.h, e_cons[par])
                for j in range(SH):
                    ei = nc.scalar.activation(
                        out=eb[:, j], in_=dn_fm[:, c], func=AF.Exp,
                        scale=cc[f"apos_{dgn}"][:, c, s0 + j:s0 + j + 1])
                tse = cx.ss.inc(ei)
                nc.vector.wait_ge(cx.ss.h, tse)
                wi = nc.vector.tensor_tensor(
                    out=wb[:, :, :],
                    in0=du_bf[:].unsqueeze(1).broadcast_to([128, SH, QT]),
                    in1=bb_all[:, s0:s0 + SH, :], op=OP.mult)
                nc.vector.drain()
                nc.vector.tensor_tensor(out=fix8, in0=eb[:, :, 0],
                                        in1=state[:, c, s0:s0 + SH],
                                        op=OP.mult)
                nc.vector.drain()
                nc.vector.tensor_tensor(out=wb[:, :, 0], in0=wb[:, :, 0],
                                        in1=fix8, op=OP.add)
                nc.vector.drain()
                nc.vector.memset(eb[:, :, 0:1], 0.0)
                nc.vector.drain()
                sc = nc.vector.tensor_tensor_scan(
                    out=hall[:, s0:s0 + SH, :].rearrange("p a b -> p (a b)"),
                    data0=eb[:, :, :].rearrange("p a b -> p (a b)"),
                    data1=wb[:, :, :].rearrange("p a b -> p (a b)"),
                    initial=0.0, op0=OP.mult, op1=OP.add)
                e_cons[par] = cx.sv.inc(sc)
                nc.vector.drain()
                nc.vector.tensor_copy(out=state[:, c, s0:s0 + SH],
                                      in_=hall[:, s0:s0 + SH, QT - 1])
                nc.vector.drain()
                half_i += 1
            hc = nc.vector.tensor_tensor(
                out=hall[:, :, :].rearrange("p a b -> p (a b)"),
                in0=hall[:, :, :].rearrange("p a b -> p (a b)"),
                in1=cb_all[:, :, :].rearrange("p a b -> p (a b)"), op=OP.mult)
            nc.vector.drain()
            stride = 1
            while stride < DS // 2:
                nw = DS // (2 * stride)
                nc.vector.tensor_tensor(
                    out=hall[:, 0:nw, :].rearrange("p a b -> p (a b)"),
                    in0=hall[:, 0:nw, :].rearrange("p a b -> p (a b)"),
                    in1=hall[:, nw:2 * nw, :].rearrange("p a b -> p (a b)"),
                    op=OP.add)
                nc.vector.drain()
                stride *= 2
            nc.vector.tensor_tensor(out=ys_f, in0=hall[:, 0, :],
                                    in1=hall[:, 1, :], op=OP.add)
            nc.vector.drain()
            nc.vector.scalar_tensor_tensor(out=t1f, in0=xc_fm[:, c],
                                           scalar=cc[f"dd_c_{dgn}"][:, c:c + 1],
                                           in1=ys_f, op0=OP.mult, op1=OP.add)
            nc.vector.drain()
            gi = nc.vector.tensor_tensor(out=yp_fm[:, c], in0=t1f,
                                         in1=zs_fm[:, c], op=OP.mult)
            cx.sv.inc(gi)
        tick_scan = cx.sv.n
        tkp = cx.load(p5_tm, d[src_p5][k * QT:(k + 1) * QT, :]
                      .rearrange("(a p) c -> p a c", p=128))
        nc.tensor.wait_ge(cx.sv.h, tick_scan)
        nc.vector.wait_ge(cx.ld.h, tkp)
        for qg in range(4):
            wv, tl, wkey = stream_w(
                d[f"b_weff_{dgn}"][:, qg * 256:(qg + 1) * 256]
                .rearrange("(a p) c -> p a c", p=128), weff_sl, qg, "weff",
                None)
            nc.tensor.wait_ge(cx.ld.h, tl)
            for mi in range(2):
                mg = qg * 2 + mi
                pkey = f"b{mi % 2}"
                pslot = cx.ps[mi % 2][:, 0:QT]
                tt = _fm_matmul(cx, wv, yp_fm, range(DIC), mi, pslot, pkey)
                nc.scalar.wait_ge(cx.stn.h, tt)
                nc.scalar.copy(out=pevac[mi % 2], in_=pslot)
                tse = cx.spub()
                cx.slot_set(pkey, cx.ss, tse)
                for i in range(NSUB):

                    def consume(psl, tick, i=i, mg=mg):
                        nc.vector.wait_ge(cx.stn.h, tick)
                        di = (sub_map[i] if sub_map else i)
                        dst = p5_tm[:, di, mg * 128:(mg + 1) * 128]
                        op = nc.vector.tensor_tensor(out=dst, in0=dst, in1=psl,
                                                     op=OP.add)
                        return cx.sv, cx.sv.inc(op)

                    nc.tensor.wait_ge(cx.ss.h, tse)
                    _tp128(cx, pevac[mi % 2][:, i * 128:(i + 1) * 128],
                           consume, bf=False, perm=perm)
            cx.slot_set(wkey, cx.stn, cx.stn.n)
        tick_v = cx.vpub()
        nc.sync.wait_ge(cx.sv.h, tick_v)
        cx.store(d[dst_p5][k * QT:(k + 1) * QT, :]
                 .rearrange("(a p) c -> p a c", p=128), p5_tm)
        tile_end[kk] = [(cx.sv, cx.sv.n), (cx.ss, cx.ss.n), (cx.stn, cx.stn.n),
                        (cx.st, cx.st.n), (cx.sg, cx.sg.n)]
    cx.wpool.release()
    cx.fpool.release()
    cx.bpool.release()


def _pass5(cx):
    nc, d, cc = cx.nc, cx.dram, cx.cc
    NT = cx.NT
    cx.wpool.mark()
    cx.fpool.mark()
    cx.bpool.mark()
    weights = _rwkv_weights(cx)
    xa = cx.fpool.alloc(F32, [NSUB, C])
    dproj = cx.fpool.alloc(F32, [DC, QT])
    xn4 = cx.bpool.alloc(BF16, [NSUB, C])
    xf_buf = cx.bpool.alloc(BF16, [DC, QT + 1])
    xr = cx.bpool.alloc(BF16, [DC, QT])
    rbf = cx.bpool.alloc(BF16, [DC, QT])
    kbuf = cx.bpool.alloc(BF16, [DC, QT])
    carry = cx.bpool.alloc(BF16, [DC, 1])
    rw_bufs = (xn4, xf_buf, xr, rbf, kbuf, dproj)

    tile_end = {}
    for k in range(NT):
        if k >= 1:
            for sem, tick in tile_end[k - 1]:
                for eng in (nc.vector, nc.scalar, nc.tensor):
                    eng.wait_ge(sem.h, tick)
                nc.sync.wait_ge(sem.h, tick)
        tk = cx.load(xa, d["s_x5"][k * QT:(k + 1) * QT, :]
                     .rearrange("(a p) c -> p a c", p=128))
        for eng in (nc.vector, nc.scalar, nc.tensor):
            eng.wait_ge(cx.ld.h, tk)
        _rwkv_delta(cx, rw_bufs, xa, weights, carry, first=(k == 0))
        _residual_add_fm_delta(cx, dproj, xa, NSUB, 0.5)
        tick_v = cx.vpub()
        nc.sync.wait_ge(cx.sv.h, tick_v)
        cx.store(d["y"][k * QT:(k + 1) * QT, :]
                 .rearrange("(a p) c -> p a c", p=128), xa)
        tile_end[k] = [(cx.sv, cx.sv.n), (cx.ss, cx.ss.n), (cx.stn, cx.stn.n),
                       (cx.st, cx.st.n)]
    cx.wpool.release()
    cx.fpool.release()
    cx.bpool.release()


# version 06
# speedup vs baseline: 1.3276x; 1.0608x over previous
import contextlib

import numpy as np
import ml_dtypes

import concourse.bass as bass
import concourse.mybir as mybir
from concourse.bass_utils import run_bass_kernel_spmd
from concourse.alu_op_type import AluOpType as OP

F32 = mybir.dt.float32
BF16 = mybir.dt.bfloat16
I8 = mybir.dt.int8
AF = mybir.ActivationFunctionType
AX = mybir.AxisListType

C = 1024
DC = C // 128
DI = 2048
DIC = DI // 128
DS = 16
DTR = 64
NH = 16
HD = 64
LS = 256
QT = 256
NSUB = QT // 128
EPS_LN = 1e-5
EPS_RMS = 1e-6
EPS_Q = 1e-6
NEG = -1.0e30

CHAIN = ("ffn0a", "attn0", "ffn0b", "rwkva", "mamba", "rwkvb")


def _cdiv(a, b):
    return (a + b - 1) // b


class Pool:

    def __init__(self, nc, ctx, name, nbytes):
        assert nbytes % 64 == 0
        self.t = ctx.enter_context(nc.sbuf_tensor(name, [128, nbytes], I8))
        self.name = name
        self.nbytes = nbytes
        self.cur = 0
        self.marks = []

    def alloc(self, dtype, shape):
        esz = mybir.dt.size(dtype)
        n = 1
        for s in shape:
            n *= s
        nb = _cdiv(n * esz, 64) * 64
        assert self.cur + nb <= self.nbytes, (
            f"pool {self.name} overflow: {self.cur}+{nb}>{self.nbytes}")
        ap = self.t[:, self.cur:self.cur + n * esz].bitcast(dtype)
        self.cur += nb
        if len(shape) > 1:
            dims = " ".join(f"d{i}" for i in range(len(shape)))
            kw = {f"d{i}": s for i, s in enumerate(shape)}
            ap = ap.rearrange(f"p ({dims}) -> p {dims}", **kw)
        return ap

    def mark(self):
        self.marks.append(self.cur)

    def release(self):
        self.cur = self.marks.pop()


class Sem:
    def __init__(self, nc, ctx, name, step=1):
        self.h = ctx.enter_context(nc.semaphore(name))
        self.n = 0
        self.step = step

    def inc(self, instr):
        instr.then_inc(self.h, self.step)
        self.n += self.step
        return self.n


class Ctx:
    def load(self, dst, src):
        self.nc.sync.wait_ge(self.ld.h, self.ld.n)
        self.nc.sync.dma_start(out=dst, in_=src).then_inc(self.ld.h, 16)
        self.ld.n += 16
        return self.ld.n

    def loadw(self, dst, src):
        self.nc.scalar.wait_ge(self.lw.h, self.lw.n)
        self.nc.scalar.dma_start(out=dst, in_=src).then_inc(self.lw.h, 16)
        self.lw.n += 16
        return self.lw.n

    def store(self, dst, src):
        self.nc.sync.wait_ge(self.st.h, self.st.n)
        self.nc.sync.dma_start(out=dst, in_=src).then_inc(self.st.h, 16)
        self.st.n += 16
        return self.st.n

    def slot_wait(self, eng, key):
        rec = self.slots.get(key)
        if rec is not None:
            sem, tick = rec
            eng.wait_ge(sem.h, tick)

    def slot_set(self, key, sem, tick):
        self.slots[key] = (sem, tick)

    def vpub(self):
        return self.sv.inc(self.nc.vector.drain())

    def spub(self):
        return self.ss.inc(self.nc.scalar.drain())

    def tpub(self):
        return self.stn.inc(self.nc.tensor.drain())

    def gpub(self):
        return self.sg.inc(self.nc.gpsimd.drain())



def kernel(**inputs):
    T = inputs["L0_attn_x"].shape[1]
    n_cores = 8
    x = np.ascontiguousarray(np.asarray(inputs["L0_attn_x"], dtype=np.float32))

    def t(a):
        return np.ascontiguousarray(np.asarray(a, dtype=np.float32).T)

    def cols(v, n):
        v = np.asarray(v, dtype=np.float32).reshape(n, 128)
        return np.ascontiguousarray(v.T)

    common = host_inputs(inputs, t, cols)
    nc = build_program(T=T)
    in_maps = []
    for c in range(n_cores):
        m = dict(common)
        m["x0"] = np.ascontiguousarray(x[c])
        in_maps.append(m)
    res = run_bass_kernel_spmd(nc, in_maps, core_ids=list(range(n_cores)))
    out = np.stack([np.asarray(res.results[c]["y"]) for c in range(n_cores)], axis=0)
    return out.astype(np.float32)


def host_inputs(inputs, t, cols):
    common = {
        "wq_T": np.ascontiguousarray(np.asarray(inputs["L0_ffn_weight_q"]).T),
        "w_qkv": np.ascontiguousarray(np.asarray(inputs["L0_attn_w_qkv"], dtype=np.float32)),
        "w_proj": np.ascontiguousarray(np.asarray(inputs["L0_attn_w_proj"], dtype=np.float32)),
        "lnw_c": cols(inputs["L0_attn_ln_w"], DC),
        "wsc_c": cols(inputs["L0_ffn_weight_scale"], DC),
        "fbias_c": cols(inputs["L0_ffn_bias"], DC),
        "fbias_r": np.asarray(inputs["L0_ffn_bias"], np.float32).reshape(1, C),
        "gamma_r": np.asarray(inputs["L1_attn_gamma"], np.float32).reshape(1, C),
        "beta_r": np.asarray(inputs["L1_attn_beta"], np.float32).reshape(1, C),
        "outb_r": np.asarray(inputs["L1_attn_out_b"], np.float32).reshape(1, C),
        "wr_T": t(inputs["L1_ffn_W_r"]),
        "wk_T": t(inputs["L1_ffn_W_k"]),
        "wv_T": t(inputs["L1_ffn_W_v"]),
        "mur_c": cols(inputs["L1_ffn_mu_r"], DC),
        "muk_c": cols(inputs["L1_ffn_mu_k"], DC),
        "outw_T": t(inputs["L1_attn_out_w"]),
    }
    for d in ("f", "b"):
        common[f"inw_T_{d}"] = t(inputs[f"L1_attn_in_w_{d}"])
        xp = t(inputs[f"L1_attn_xproj_w_{d}"])
        xpp = np.zeros((DI, 128), np.float32)
        xpp[:, 0:DTR + DS] = xp[:, 0:DTR + DS]
        xpp[:, 96:96 + DS] = xp[:, DTR + DS:]
        common[f"xproj_T_{d}"] = xpp
        common[f"dtw_T_{d}"] = t(inputs[f"L1_attn_dt_w_{d}"])
        common[f"mout_{d}"] = np.ascontiguousarray(
            np.asarray(inputs[f"L1_attn_mout_w_{d}"], np.float32))
        common[f"convw_{d}"] = np.ascontiguousarray(
            np.asarray(inputs[f"L1_attn_conv_w_{d}"], np.float32).reshape(DI, 4)
            .reshape(DIC, 128, 4).transpose(1, 0, 2))
        common[f"convb_c_{d}"] = cols(inputs[f"L1_attn_conv_b_{d}"], DIC)
        common[f"dtb_c_{d}"] = cols(inputs[f"L1_attn_dt_b_{d}"], DIC)
        common[f"dd_c_{d}"] = cols(inputs[f"L1_attn_D_{d}"], DIC)
        common[f"alog_{d}"] = np.ascontiguousarray(
            np.asarray(inputs[f"L1_attn_A_log_{d}"], np.float32)
            .reshape(DIC, 128, DS).transpose(1, 0, 2).reshape(128, DIC * DS))

    ident = np.eye(128)
    prev = ident[:, ::-1]
    tri0 = np.where(np.arange(LS)[None, :] <= np.arange(128)[:, None], 0.0, NEG)
    tri1 = np.where(np.arange(LS)[None, :] <= 128 + np.arange(128)[:, None], 0.0, NEG)
    sel = np.zeros((DS, DS * 128), np.float32)
    for s in range(DS):
        sel[s, s * 128:(s + 1) * 128] = 1.0
    common.update({
        "ident_bf": ident.astype(ml_dtypes.bfloat16),
        "ident_f": ident.astype(np.float32),
        "prev_bf": prev.astype(ml_dtypes.bfloat16),
        "prev_f": prev.astype(np.float32),
        "ones_bf": np.ones((1, 128), ml_dtypes.bfloat16),
        "ones_col_bf": np.ones((128, 1), ml_dtypes.bfloat16),
        "ones_f": np.ones((1, 128), np.float32),
        "tri0": tri0.astype(np.float32),
        "tri1": tri1.astype(np.float32),
        "sel_bf": sel.astype(ml_dtypes.bfloat16),
    })
    return common



def build_program(T=8192, chain=CHAIN, taps=(), dbg_level=99):
    nc = bass.Bass("TRN2", target_bir_lowering=False)
    NT = T // QT
    cx = Ctx()
    cx.dbg = dbg_level
    cx.nc = nc
    cx.T = T
    cx.NT = NT
    cx.chain = chain
    cx.taps = set(taps)
    cx.tap_dram = {}
    cx.slots = {}

    dram = {}

    def din(name, shape, dt):
        dram[name] = nc.dram_tensor(name, shape, dt, kind="ExternalInput").ap()

    din("x0", [T, C], F32)
    din("wq_T", [C, C], I8)
    din("w_qkv", [C, 3 * C], F32)
    din("w_proj", [C, C], F32)
    din("lnw_c", [128, DC], F32)
    din("wsc_c", [128, DC], F32)
    din("fbias_c", [128, DC], F32)
    din("fbias_r", [1, C], F32)
    din("gamma_r", [1, C], F32)
    din("beta_r", [1, C], F32)
    din("outb_r", [1, C], F32)
    din("wr_T", [C, C], F32)
    din("wk_T", [C, C], F32)
    din("wv_T", [C, C], F32)
    din("mur_c", [128, DC], F32)
    din("muk_c", [128, DC], F32)
    din("outw_T", [DI, C], F32)
    for d in ("f", "b"):
        din(f"inw_T_{d}", [C, 4 * C], F32)
        din(f"xproj_T_{d}", [DI, 128], F32)
        din(f"dtw_T_{d}", [DTR, DI], F32)
        din(f"mout_{d}", [C, DI], F32)
        din(f"convw_{d}", [128, DIC, 4], F32)
        din(f"convb_c_{d}", [128, DIC], F32)
        din(f"dtb_c_{d}", [128, DIC], F32)
        din(f"dd_c_{d}", [128, DIC], F32)
        din(f"alog_{d}", [128, DIC * DS], F32)
    din("ident_bf", [128, 128], BF16)
    din("ident_f", [128, 128], F32)
    din("prev_bf", [128, 128], BF16)
    din("prev_f", [128, 128], F32)
    din("ones_bf", [1, 128], BF16)
    din("ones_col_bf", [128, 1], BF16)
    din("ones_f", [1, 128], F32)
    din("tri0", [128, LS], F32)
    din("tri1", [128, LS], F32)
    din("sel_bf", [DS, DS * 128], BF16)

    dram["y"] = nc.dram_tensor("y", [T, C], F32, kind="ExternalOutput").ap()
    dram["s_x3"] = nc.dram_tensor("s_x3", [T, C], F32, kind="Internal").ap()
    dram["s_h"] = nc.dram_tensor("s_h", [T, C], BF16, kind="Internal").ap()
    dram["s_p5a"] = nc.dram_tensor("s_p5a", [T, C], F32, kind="Internal").ap()
    dram["s_p5b"] = nc.dram_tensor("s_p5b", [T, C], F32, kind="Internal").ap()
    dram["s_x5"] = nc.dram_tensor("s_x5", [T, C], F32, kind="Internal").ap()
    dram["b_wqkv"] = nc.dram_tensor("b_wqkv", [C, 3 * C], BF16, kind="Internal").ap()
    dram["b_inw_f"] = nc.dram_tensor("b_inw_f", [C, 4 * C], BF16, kind="Internal").ap()
    dram["b_inw_b"] = nc.dram_tensor("b_inw_b", [C, 4 * C], BF16, kind="Internal").ap()
    dram["b_outw_f"] = nc.dram_tensor("b_outw_f", [C, C], BF16, kind="Internal").ap()
    dram["b_outw_b"] = nc.dram_tensor("b_outw_b", [C, C], BF16, kind="Internal").ap()
    dram["b_weff_f"] = nc.dram_tensor("b_weff_f", [DI, C], BF16, kind="Internal").ap()
    dram["b_weff_b"] = nc.dram_tensor("b_weff_b", [DI, C], BF16, kind="Internal").ap()
    for tname in cx.taps:
        dram["tap_" + tname] = nc.dram_tensor(
            "tap_" + tname, [T, C], F32, kind="ExternalOutput").ap()
    cx.dram = dram

    with contextlib.ExitStack() as ctx:
        cx.ctx = ctx
        cx.wpool = Pool(nc, ctx, "wpool", 53 * 1024)
        cx.fpool = Pool(nc, ctx, "fpool", 50 * 1024)
        cx.bpool = Pool(nc, ctx, "bpool", 88 * 1024)
        cx.cpool = Pool(nc, ctx, "cpool", 16 * 1024)

        ps = []
        for i in range(5):
            ps.append(ctx.enter_context(nc.psum_tensor(f"psA{i}", [128, 512], F32)))
        cx.ps = ps
        cx.ps_tb = ctx.enter_context(nc.psum_tensor("ps_tb", [128, 2048], BF16))
        cx.ps_tf = ctx.enter_context(nc.psum_tensor("ps_tf", [128, 512], F32))
        cx.tb_par = 0
        cx.tf_par = 0

        cx.ld = Sem(nc, ctx, "ld", 16)
        cx.lw = Sem(nc, ctx, "lw", 16)
        cx.st = Sem(nc, ctx, "st", 16)
        cx.sv = Sem(nc, ctx, "sv")
        cx.ss = Sem(nc, ctx, "ss")
        cx.stn = Sem(nc, ctx, "stn")
        cx.sg = Sem(nc, ctx, "sg")

        cx.block = ctx.enter_context(nc.Block())
        _emit(cx)

    return nc


def _barrier(cx):
    nc = cx.nc
    cx.vpub()
    cx.spub()
    cx.tpub()
    cx.gpub()
    ticks = [(s, s.n) for s in (cx.sv, cx.ss, cx.stn, cx.sg, cx.ld, cx.st,
                                cx.lw)]
    for eng in (nc.sync, nc.vector, nc.scalar, nc.tensor, nc.gpsimd):
        for sem, n in ticks:
            eng.wait_ge(sem.h, n)
    cx.slots = {}


def _emit(cx):
    _load_consts(cx)
    _weight_prep(cx)
    _barrier(cx)
    last = {"ffn0a", "attn0", "ffn0b"} & set(cx.chain)
    p1_out = "y" if not ({"rwkva", "mamba", "rwkvb"} & set(cx.chain)) else "s_x3"
    _pass1(cx, out=p1_out)
    if "rwkva" in cx.chain:
        _barrier(cx)
        p2_out = "y" if "mamba" not in cx.chain else None
        _pass2(cx, out=p2_out)
    if "mamba" in cx.chain:
        _barrier(cx)
        _mamba_pass(cx, "f")
        _barrier(cx)
        _mamba_pass(cx, "b", out=("y" if "rwkvb" not in cx.chain else "s_x5"))
    if "rwkvb" in cx.chain:
        _barrier(cx)
        _pass5(cx)
    cx.nc.sync.wait_ge(cx.st.h, cx.st.n)



def _load_consts(cx):
    nc, d = cx.nc, cx.dram
    cc = {}
    full = [
        ("ident_bf", BF16, [128]), ("ident_f", F32, [128]),
        ("prev_bf", BF16, [128]), ("prev_f", F32, [128]),
        ("tri0", F32, [LS]), ("tri1", F32, [LS]),
        ("lnw_c", F32, [DC]), ("wsc_c", F32, [DC]), ("fbias_c", F32, [DC]),
        ("mur_c", F32, [DC]), ("muk_c", F32, [DC]),
        ("convw_f", F32, [DIC, 4]), ("convw_b", F32, [DIC, 4]),
        ("convb_c_f", F32, [DIC]), ("convb_c_b", F32, [DIC]),
        ("dtb_c_f", F32, [DIC]), ("dtb_c_b", F32, [DIC]),
        ("dd_c_f", F32, [DIC]), ("dd_c_b", F32, [DIC]),
        ("ones_col_bf", BF16, [1]),
    ]
    for name, dt, shape in full:
        buf = cx.cpool.alloc(dt, shape)
        cx.load(buf, d[name][:, :] if len(d[name].shape) == 2 else d[name][:, :, :])
        cc[name] = buf
    rows = [("ones_bf", BF16, 1, 128), ("ones_f", F32, 1, 128),
            ("sel_bf", BF16, DS, DS * 128)]
    for name, dt, np_, nf in rows:
        buf = cx.cpool.alloc(dt, [nf])
        cx.load(buf[0:np_], d[name][:, :])
        cc[name] = buf
    cx.cc = cc

    nc.scalar.wait_ge(cx.ld.h, cx.ld.n)
    cx.fpool.mark()
    alog_st = cx.fpool.alloc(F32, [DIC * DS])
    for dgn in ("f", "b"):
        tl = cx.load(alog_st, d[f"alog_{dgn}"][:, :])
        nc.scalar.wait_ge(cx.ld.h, tl)
        apos = cx.cpool.alloc(F32, [DIC * DS])
        op = nc.scalar.activation(out=apos, in_=alog_st, func=AF.Exp)
        nc.sync.wait_ge(cx.ss.h, cx.ss.inc(op))
        cc[f"apos_{dgn}"] = apos.rearrange("p (c s) -> p c s", c=DIC)
        ndtb = cx.cpool.alloc(F32, [DIC])
        nc.scalar.mul(out=ndtb, in_=cc[f"dtb_c_{dgn}"], mul=-1.0)
        cc[f"ndtb_{dgn}"] = ndtb
    cx.fpool.release()
    cx.spub()


def _build_bcast(cx, row_dram_name, pool, scale=None):
    nc, d, cc = cx.nc, cx.dram, cx.cc
    bc = pool.alloc(F32, [C])
    cx.fpool.mark()
    rowstage = cx.fpool.alloc(F32, [C])
    tl = cx.load(rowstage[0:1, :], d[row_dram_name][:, :])
    nc.tensor.wait_ge(cx.ld.h, tl)
    for j in range(C // 512):
        cx.slot_wait(nc.tensor, "b4")
        nc.tensor.matmul(cx.ps[4][:, :], cc["ones_f"][0:1, :],
                         rowstage[0:1, j * 512:(j + 1) * 512],
                         start=True, stop=True)
        tt = cx.tpub()
        nc.vector.wait_ge(cx.stn.h, tt)
        nc.vector.tensor_copy(out=bc[:, j * 512:(j + 1) * 512], in_=cx.ps[4][:, :])
        cx.slot_set("b4", cx.sv, cx.vpub())
    if scale is not None:
        nc.vector.tensor_scalar(out=bc, in0=bc, scalar1=scale, scalar2=None,
                                op0=OP.mult)
        cx.vpub()
    nc.sync.wait_ge(cx.stn.h, cx.stn.n)
    cx.fpool.release()
    return bc


def _weight_prep(cx):
    nc, d = cx.nc, cx.dram
    cx.fpool.mark()
    cx.bpool.mark()
    fbuf = cx.fpool.alloc(F32, [4096])
    bb = cx.bpool.alloc(BF16, [4096])
    jobs = [(d[f"inw_T_{g}"], d[f"b_inw_{g}"], C, 4 * C) for g in ("f", "b")]
    jobs += [(d["outw_T"][0:C, :], d["b_outw_f"], C, C),
             (d["outw_T"][C:2 * C, :], d["b_outw_b"], C, C),
             (d["w_qkv"], d["b_wqkv"], C, 3 * C)]
    for src, dst, rows, width in jobs:
        for r in range(rows // 128):
            t0 = cx.load(fbuf[:, 0:width], src[r * 128:(r + 1) * 128, :])
            nc.vector.wait_ge(cx.ld.h, t0)
            nc.vector.tensor_copy(out=bb[:, 0:width], in_=fbuf[:, 0:width])
            tick = cx.vpub()
            nc.sync.wait_ge(cx.sv.h, tick)
            t1 = cx.store(dst[r * 128:(r + 1) * 128, :], bb[:, 0:width])
            nc.sync.wait_ge(cx.st.h, t1)
    cx.fpool.release()
    cx.bpool.release()

    cx.wpool.mark()
    cx.fpool.mark()
    mo_f32 = cx.fpool.alloc(F32, [DI])
    mo_bf = cx.wpool.alloc(BF16, [DC, DI])
    ow_bf = cx.wpool.alloc(BF16, [DC, C])
    wout = cx.wpool.alloc(BF16, [C])
    for dgn in ("f", "b"):
        for kc in range(DC):
            nc.sync.wait_ge(cx.sv.h, cx.sv.n)
            t0 = cx.load(mo_f32, d[f"mout_{dgn}"][kc * 128:(kc + 1) * 128, :])
            nc.vector.wait_ge(cx.ld.h, t0)
            nc.vector.tensor_copy(out=mo_bf[:, kc], in_=mo_f32)
            cx.vpub()
            t1 = cx.load(ow_bf[:, kc], d[f"b_outw_{dgn}"][kc * 128:(kc + 1) * 128, :])
        tick = cx.vpub()
        nc.tensor.wait_ge(cx.sv.h, tick)
        nc.tensor.wait_ge(cx.ld.h, cx.ld.n)
        for mi in range(DIC):
            for nj in range(2):
                key = f"b{nj % 2}"
                cx.slot_wait(nc.tensor, key)
                pslot = cx.ps[nj % 2][:, :]
                for kc in range(DC):
                    nc.tensor.matmul(pslot, mo_bf[:, kc, mi * 128:(mi + 1) * 128],
                                     ow_bf[:, kc, nj * 512:(nj + 1) * 512],
                                     start=(kc == 0), stop=(kc == DC - 1))
                tt = cx.tpub()
                nc.scalar.wait_ge(cx.stn.h, tt)
                nc.scalar.copy(out=wout[:, nj * 512:(nj + 1) * 512], in_=pslot)
                cx.slot_set(key, cx.ss, cx.spub())
            nc.sync.wait_ge(cx.ss.h, cx.ss.n)
            ts = cx.store(d[f"b_weff_{dgn}"][mi * 128:(mi + 1) * 128, :], wout)
            nc.scalar.wait_ge(cx.st.h, ts)
    cx.wpool.release()
    cx.fpool.release()



def _ln_tm(cx, x_tm, xn_out, nsub, width=C, eps=EPS_LN, ret_ab=False):
    nc = cx.nc
    f = cx.fpool
    f.mark()
    s_sum = f.alloc(F32, [nsub])
    s_sq = f.alloc(F32, [nsub])
    junk = f.alloc(F32, [width])
    mean = f.alloc(F32, [nsub])
    nmrs = f.alloc(F32, [nsub])
    rstd = f.alloc(F32, [nsub])
    tmp = f.alloc(F32, [nsub])
    r2 = f.alloc(F32, [nsub])
    nc.vector.drain()
    nc.vector.wait_ge(cx.ss.h, cx.ss.n)
    nc.scalar.wait_ge(cx.sv.h, cx.sv.n)
    for i in range(nsub):
        nc.vector.reduce_sum(out=s_sum[:, i:i + 1], in_=x_tm[:, i], axis=AX.X)
        if i:
            nc.scalar.drain()
        nc.scalar.activation(out=junk, in_=x_tm[:, i], func=AF.Square,
                             accum_out=s_sq[:, i:i + 1])
    nc.vector.drain()
    tsq = cx.spub()
    nc.vector.wait_ge(cx.ss.h, tsq)
    inv_w = 1.0 / width
    nc.vector.tensor_scalar(out=mean, in0=s_sum, scalar1=inv_w, scalar2=None,
                            op0=OP.mult)
    nc.vector.drain()
    nc.vector.scalar_tensor_tensor(out=tmp, in0=mean, scalar=-1.0, in1=mean,
                                   op0=OP.mult, op1=OP.mult)
    nc.vector.drain()
    nc.vector.scalar_tensor_tensor(out=tmp, in0=s_sq, scalar=inv_w, in1=tmp,
                                   op0=OP.mult, op1=OP.add)
    nc.vector.drain()
    op = nc.vector.tensor_scalar(out=tmp, in0=tmp, scalar1=eps, scalar2=None,
                                 op0=OP.add)
    tick_v = cx.sv.inc(op)
    nc.scalar.wait_ge(cx.sv.h, tick_v)
    nc.scalar.activation(out=r2, in_=tmp, func=AF.Ln)
    nc.scalar.drain()
    op = nc.scalar.activation(out=rstd, in_=r2, func=AF.Exp, scale=-0.5)
    tick_s = cx.ss.inc(op)
    nc.vector.wait_ge(cx.ss.h, tick_s)
    nc.vector.scalar_tensor_tensor(out=nmrs, in0=mean, scalar=-1.0, in1=rstd,
                                   op0=OP.mult, op1=OP.mult)
    nc.vector.drain()
    tick_s = None
    if xn_out is not None:
        for i in range(nsub):
            nc.vector.tensor_scalar(out=xn_out[:, i],
                                    in0=x_tm[:, i], scalar1=rstd[:, i:i + 1],
                                    scalar2=nmrs[:, i:i + 1],
                                    op0=OP.mult, op1=OP.add)
        nc.vector.drain()
        tick_s = cx.vpub()
    if ret_ab:
        tick_v = cx.vpub()
        return tick_s, tick_v, rstd, nmrs
    f.release()
    return tick_s


def _tp128(cx, src_ap, consume, bf=True, perm=None, ident_slice=None,
           f2=False):
    nc = cx.nc
    idn = cx.cc[("prev_" if perm == "rev" else "ident_") + ("bf" if bf else "f")]
    if ident_slice is not None:
        idn = idn[ident_slice, ident_slice]
    out_w = src_ap.shape[0]
    if bf:
        par = cx.tb_par
        cx.tb_par ^= 1
        key = ("tb", par)
        psl = cx.ps_tb[:, par * 1024:par * 1024 + out_w]
    elif f2 and cx.tf_par:
        cx.tf_par ^= 1
        key = "b2"
        psl = cx.ps[2][:, 0:out_w]
    else:
        cx.tf_par ^= 1
        key = ("tf", 0)
        psl = cx.ps_tf[:, 0:out_w]
    cx.slot_wait(nc.tensor, key)
    tp = nc.tensor.transpose(psl[0:src_ap.shape[-1], :], src_ap, idn)
    tick = cx.stn.inc(tp)
    cx.slot_set(key, *consume(psl[0:src_ap.shape[-1], :], tick))


def _transpose_to_fm(cx, src_tm, dst_fm, nsub, src_bf=True, perm=None,
                     n_feat=C, sub_map=None):
    nc = cx.nc
    fc = n_feat // 128
    for i in range(nsub):
        di = sub_map[i] if sub_map else i
        for j in range(fc):

            def consume(psl, tick, di=di, j=j):
                nc.scalar.wait_ge(cx.stn.h, tick)
                cp = nc.scalar.copy(out=dst_fm[:, j, di * 128:(di + 1) * 128],
                                    in_=psl)
                return cx.ss, cx.ss.inc(cp)

            _tp128(cx, src_tm[:, i, j * 128:(j + 1) * 128], consume,
                   bf=src_bf, perm=perm)
    return cx.ss.n


def _fm_matmul(cx, w_view, rhs_fm, kcs, mi, pslot, pkey, extra=None):
    nc = cx.nc
    cx.slot_wait(nc.tensor, pkey)
    n = len(kcs)
    mm = None
    for ii, kc in enumerate(kcs):
        mm = nc.tensor.matmul(pslot, w_view[:, kc, mi * 128:(mi + 1) * 128],
                              rhs_fm[:, kc], start=(ii == 0),
                              stop=(extra is None and ii == n - 1))
    if extra is not None:
        lhsT, rhs = extra
        mm = nc.tensor.matmul(pslot, lhsT, rhs, start=False, stop=True)
    return cx.stn.inc(mm)


def _residual_add_fm_delta(cx, delta_fm, x_tm, nsub, scale, extra_tm=None):
    nc = cx.nc
    for i in range(nsub):
        for j in range(DC):

            def consume(psl, tick, i=i, j=j):
                nc.vector.wait_ge(cx.stn.h, tick)
                dst = x_tm[:, i, j * 128:(j + 1) * 128]
                op = nc.vector.scalar_tensor_tensor(out=dst, in0=psl,
                                                    scalar=scale, in1=dst,
                                                    op0=OP.mult, op1=OP.add)
                if extra_tm is not None:
                    nc.vector.drain()
                    op = nc.vector.tensor_tensor(
                        out=dst, in0=dst,
                        in1=extra_tm[:, i, j * 128:(j + 1) * 128], op=OP.add)
                return cx.sv, cx.sv.inc(op)

            _tp128(cx, delta_fm[:, j, i * 128:(i + 1) * 128], consume, bf=False)
    return cx.sv.n



def _quant_ffn(cx, bufs, x_tm, wq_bf):
    nc = cx.nc
    cc = cx.cc
    xn1, xq_tm, xq_fm, out_fm, scr = bufs
    amax, xsch, t2, frac = scr
    _ln_tm(cx, x_tm, xn1, NSUB)
    if cx.dbg < 2:
        return
    for i in range(NSUB):
        nc.vector.reduce_max(out=amax[:, i:i + 1], in_=xn1[:, i], axis=AX.X,
                             apply_absolute_value=True)
    nc.vector.drain()
    nc.vector.tensor_scalar(out=amax, in0=amax, scalar1=EPS_Q, scalar2=1.0 / 127.0,
                            op0=OP.max, op1=OP.mult)
    nc.vector.drain()
    nc.vector.tensor_scalar(out=xsch, in0=amax, scalar1=0.5, scalar2=None,
                            op0=OP.mult)
    nc.vector.drain()
    nc.vector.reciprocal(out=amax, in_=amax)
    nc.vector.drain()
    i32 = frac.bitcast(mybir.dt.int32)
    fb = frac
    for i in range(NSUB):
        nc.vector.tensor_scalar(out=t2, in0=xn1[:, i], scalar1=amax[:, i:i + 1],
                                scalar2=512.5, op0=OP.mult, op1=OP.add)
        nc.vector.drain()
        nc.vector.tensor_copy(out=i32, in_=t2)
        nc.vector.drain()
        nc.vector.tensor_copy(out=xn1[:, i], in_=i32)
        nc.vector.drain()
        nc.vector.tensor_tensor(out=fb, in0=xn1[:, i], in1=t2, op=OP.is_gt)
        nc.vector.drain()
        nc.vector.tensor_tensor(out=t2, in0=xn1[:, i], in1=fb, op=OP.subtract)
        nc.vector.drain()
        nc.vector.tensor_scalar(out=t2, in0=t2, scalar1=639.0,
                                scalar2=384.0, op0=OP.min, op1=OP.max)
        nc.vector.drain()
        nc.vector.tensor_scalar(out=xq_tm[:, i], in0=t2, scalar1=-512.0,
                                scalar2=None, op0=OP.add)
        nc.vector.drain()
    tick_q = cx.vpub()
    if cx.dbg < 3:
        return
    nc.tensor.wait_ge(cx.sv.h, tick_q)
    _transpose_to_fm(cx, xq_tm, xq_fm, NSUB, src_bf=True)
    if cx.dbg < 4:
        return
    nc.tensor.wait_ge(cx.ss.h, cx.ss.n)
    for mi in range(DC):
        pkey = f"b{mi % 2}"
        pslot = cx.ps[mi % 2][:, 0:QT]
        tt = _fm_matmul(cx, wq_bf, xq_fm, range(DC), mi, pslot, pkey)
        nc.vector.wait_ge(cx.stn.h, tt)
        op = nc.vector.tensor_scalar(out=out_fm[:, mi], in0=pslot,
                                     scalar1=cc["wsc_c"][:, mi:mi + 1],
                                     scalar2=None, op0=OP.mult)
        cx.slot_set(pkey, cx.sv, cx.sv.inc(op))
    nc.tensor.wait_ge(cx.sv.h, cx.sv.n)
    for i in range(NSUB):
        for j in range(DC):

            def consume(psl, tick, i=i, j=j):
                nc.vector.wait_ge(cx.stn.h, tick)
                dst = x_tm[:, i, j * 128:(j + 1) * 128]
                nc.vector.scalar_tensor_tensor(
                    out=frac[:, 0:128], in0=psl, scalar=xsch[:, i:i + 1],
                    in1=cx.cc["halfbias_bc"][:, j * 128:(j + 1) * 128],
                    op0=OP.mult, op1=OP.add)
                nc.vector.drain()
                op = nc.vector.tensor_tensor(out=dst, in0=dst,
                                             in1=frac[:, 0:128], op=OP.add)
                return cx.sv, cx.sv.inc(op)

            _tp128(cx, out_fm[:, j, i * 128:(i + 1) * 128], consume, bf=False)


def _attn(cx, bufs, x_tm, wqkv_get, wproj_bf):
    nc = cx.nc
    cc = cx.cc
    (xn2, xn2_fm, qf, kf, vf, sq, qh, kh, vtm, scsb, probs_all, pT, yatt,
     scr) = bufs
    row, rqb_q, rqb_k, mx, nb, sume, rsum, dproj = scr
    tick_ln = _ln_tm(cx, x_tm, xn2, NSUB)
    nc.tensor.wait_ge(cx.sv.h, tick_ln)
    tick = _transpose_to_fm(cx, xn2, xn2_fm, NSUB, src_bf=True)
    nc.tensor.wait_ge(cx.ss.h, tick)
    for half6 in range(6):
        wv, tl, wkey = wqkv_get(half6)
        nc.tensor.wait_ge(cx.ld.h, tl)
        for mi in range(4):
            mg = half6 * 4 + mi
            pkey = f"b{mg % 2}"
            pslot = cx.ps[mg % 2][:, 0:QT]
            tt = _fm_matmul(cx, wv, xn2_fm, range(DC), mi, pslot, pkey)
            nc.scalar.wait_ge(cx.stn.h, tt)
            dst = (qf, kf, vf)[mg // DC][:, mg % DC]
            cp = nc.scalar.copy(out=dst, in_=pslot)
            cx.slot_set(pkey, cx.ss, cx.ss.inc(cp))
        cx.slot_set(wkey, cx.stn, cx.stn.n)
    for name, src, rqb in (("q", qf, rqb_q), ("k", kf, rqb_k)):
        nc.scalar.wait_ge(cx.ss.h, cx.ss.n)
        for j in range(DC):
            nc.scalar.activation(out=sq[:, j], in_=src[:, j], func=AF.Square)
        ts = cx.spub()
        nc.tensor.wait_ge(cx.ss.h, ts)
        cx.slot_wait(nc.tensor, "b2")
        for j in range(DC):
            nc.tensor.matmul(cx.ps[2][0:1, 0:QT], cc["ones_col_bf"][:, 0:1],
                             sq[:, j], start=(j == 0), stop=(j == DC - 1))
        tt = cx.tpub()
        f64 = 64.0 if name == "q" else 1.0
        nc.vector.wait_ge(cx.stn.h, tt)
        op = nc.vector.tensor_scalar(out=row[0:1, :], in0=cx.ps[2][0:1, 0:QT],
                                     scalar1=f64 / C, scalar2=f64 * EPS_RMS,
                                     op0=OP.mult, op1=OP.add)
        cx.slot_set("b2", cx.sv, cx.sv.inc(op))
        nc.scalar.wait_ge(cx.sv.h, cx.sv.n)
        nc.scalar.activation(out=row[0:1, :], in_=row[0:1, :], func=AF.Ln)
        nc.scalar.drain()
        op = nc.scalar.activation(out=row[0:1, :], in_=row[0:1, :],
                                  func=AF.Exp, scale=-0.5)
        ts = cx.ss.inc(op)
        nc.tensor.wait_ge(cx.ss.h, ts)
        cx.slot_wait(nc.tensor, "b4")
        nc.tensor.matmul(cx.ps[4][:, 0:QT], cc["ones_f"][0:1, :], row[0:1, 0:QT],
                         start=True, stop=True)
        tt = cx.tpub()
        nc.scalar.wait_ge(cx.stn.h, tt)
        cp = nc.scalar.copy(out=rqb, in_=cx.ps[4][:, 0:QT])
        cx.slot_set("b4", cx.ss, cx.ss.inc(cp))
    nc.vector.wait_ge(cx.ss.h, cx.ss.n)
    for (src, dst, rqb) in ((qf, qh, rqb_q), (kf, kh, rqb_k)):
        for j in range(DC):
            nc.vector.scalar_tensor_tensor(out=dst[:, j], in0=src[:, j],
                                           scalar=cc["lnw_c"][:, j:j + 1],
                                           in1=rqb, op0=OP.mult, op1=OP.mult)
        nc.vector.drain()
    tick_qh = cx.vpub()
    nc.tensor.wait_ge(cx.ss.h, cx.ss.n)
    for h in range(NH):
        ch, r0 = h // 2, (h % 2) * 64
        for kc in range(NSUB):

            def consume(psl, tick, h=h, kc=kc):
                nc.scalar.wait_ge(cx.stn.h, tick)
                cp = nc.scalar.copy(out=vtm[:, h, kc * 64:(kc + 1) * 64],
                                    in_=psl)
                return cx.ss, cx.ss.inc(cp)

            _tp128(cx, vf[r0:r0 + 64, ch, kc * 128:(kc + 1) * 128], consume,
                   bf=True, ident_slice=slice(r0, r0 + 64))
    tick_vtm = cx.ss.n
    nc.tensor.wait_ge(cx.sv.h, tick_qh)
    nc.tensor.wait_ge(cx.ss.h, tick_vtm)
    cx.slot_set("b2", cx.sv, cx.sv.n)
    cx.slot_set("b3", cx.sv, cx.sv.n)
    cx.slot_set("b4", cx.ss, cx.ss.n)
    jobs = [(h, qs) for h in range(NH) for qs in range(NSUB)]
    for j, (h, qs) in enumerate(jobs):
        ch, r0 = h // 2, (h % 2) * 64
        jp = j % 4
        pkey = f"b{2 + (j % 2)}"
        cx.slot_wait(nc.tensor, pkey)
        pslot = cx.ps[2 + (j % 2)][:, 0:QT]
        nc.tensor.matmul(pslot, qh[r0:r0 + 64, ch, qs * 128:(qs + 1) * 128],
                         kh[r0:r0 + 64, ch, 0:QT], start=True, stop=True)
        tt = cx.tpub()
        nc.vector.wait_ge(cx.stn.h, tt)
        cx.slot_wait(nc.vector, ("scsb", jp))
        op = nc.vector.tensor_tensor(out=scsb[:, jp], in0=pslot,
                                     in1=cc["tri0" if qs == 0 else "tri1"],
                                     op=OP.add)
        cx.slot_set(pkey, cx.sv, cx.sv.inc(op))
        nc.vector.drain()
        nc.vector.reduce_max(out=mx[:, j:j + 1], in_=scsb[:, jp], axis=AX.X)
        nc.vector.drain()
        op = nc.vector.tensor_scalar(out=nb[:, j:j + 1], in0=mx[:, j:j + 1],
                                     scalar1=-1.0, scalar2=None, op0=OP.mult)
        tick_v = cx.sv.inc(op)
        nc.scalar.wait_ge(cx.sv.h, tick_v)
        op = nc.scalar.activation(out=probs_all[:, h, qs], in_=scsb[:, jp],
                                  func=AF.Exp, scale=1.0, bias=nb[:, j:j + 1],
                                  accum_out=sume[:, j:j + 1])
        ts = cx.ss.inc(op)
        cx.slot_set(("scsb", jp), cx.ss, ts)
        nc.vector.wait_ge(cx.ss.h, ts)
        nc.vector.reciprocal(out=rsum[:, j:j + 1], in_=sume[:, j:j + 1])
        nc.vector.drain()
        op = nc.vector.tensor_scalar(out=probs_all[:, h, qs],
                                     in0=probs_all[:, h, qs],
                                     scalar1=rsum[:, j:j + 1], scalar2=None,
                                     op0=OP.mult)
        cx.sv.inc(op)
    tick_p = cx.sv.n
    nc.tensor.wait_ge(cx.sv.h, tick_p)
    for h in range(NH):
        ch, r0 = h // 2, (h % 2) * 64
        for qs in range(NSUB):
            for kc in range(NSUB):

                def consume(psl, tick, kc=kc, qs=qs):
                    nc.scalar.wait_ge(cx.stn.h, tick)
                    cp = nc.scalar.copy(out=pT[:, kc, qs * 128:(qs + 1) * 128],
                                        in_=psl)
                    return cx.ss, cx.ss.inc(cp)

                _tp128(cx, probs_all[:, h, qs, kc * 128:(kc + 1) * 128],
                       consume, bf=True)
        ts = cx.ss.n
        nc.tensor.wait_ge(cx.ss.h, ts)
        pkey = "b4"
        cx.slot_wait(nc.tensor, pkey)
        pslot = cx.ps[4][0:64, 0:QT]
        mm = None
        for kc in range(NSUB):
            mm = nc.tensor.matmul(pslot, vtm[:, h, kc * 64:(kc + 1) * 64],
                                  pT[:, kc, 0:QT], start=(kc == 0),
                                  stop=(kc == NSUB - 1))
        tt = cx.stn.inc(mm)
        nc.scalar.wait_ge(cx.stn.h, tt)
        cp = nc.scalar.copy(out=yatt[r0:r0 + 64, ch], in_=pslot)
        cx.slot_set(pkey, cx.ss, cx.ss.inc(cp))
    nc.tensor.wait_ge(cx.ss.h, cx.ss.n)
    for mi in range(DC):
        pkey = f"b{mi % 2}"
        pslot = cx.ps[mi % 2][:, 0:QT]
        tt = _fm_matmul(cx, wproj_bf, yatt, range(DC), mi, pslot, pkey)
        nc.scalar.wait_ge(cx.stn.h, tt)
        cp = nc.scalar.copy(out=dproj[:, mi], in_=pslot)
        cx.slot_set(pkey, cx.ss, cx.ss.inc(cp))
    nc.tensor.wait_ge(cx.ss.h, cx.ss.n)
    _residual_add_fm_delta(cx, dproj, x_tm, NSUB, 1.0)




def _pass1(cx, out="s_x3"):
    nc, d, cc = cx.nc, cx.dram, cx.cc
    NT = cx.NT
    cx.wpool.mark()
    cx.fpool.mark()
    cx.bpool.mark()
    wq_bf = cx.wpool.alloc(BF16, [DC, C])
    wproj_bf = cx.wpool.alloc(BF16, [DC, C])
    wqkv_sl = [cx.wpool.alloc(BF16, [DC, 512]) for _ in range(2)]
    cx.cc["halfbias_bc"] = _build_bcast(cx, "fbias_r", cx.wpool, scale=0.5)
    stage = cx.fpool.alloc(F32, [C])
    stage_i8 = cx.bpool.alloc(I8, [C])
    for kc in range(DC):
        t0 = cx.load(stage_i8, d["wq_T"][kc * 128:(kc + 1) * 128, :])
        nc.vector.wait_ge(cx.ld.h, t0)
        nc.vector.tensor_copy(out=wq_bf[:, kc], in_=stage_i8)
        nc.vector.drain()
        t0 = cx.load(stage, d["w_proj"][kc * 128:(kc + 1) * 128, :])
        nc.vector.wait_ge(cx.ld.h, t0)
        nc.vector.tensor_copy(out=wproj_bf[:, kc], in_=stage)
        tick_w = cx.vpub()
        nc.sync.wait_ge(cx.sv.h, tick_w)

    def wqkv_get(half6):
        slot = half6 % 2
        key = ("wqkv", slot)
        cx.slot_wait(nc.sync, key)
        tl = cx.load(wqkv_sl[slot],
                     d["b_wqkv"][:, half6 * 512:(half6 + 1) * 512]
                     .rearrange("(a p) c -> p a c", p=128))
        return wqkv_sl[slot], tl, key

    xa = [cx.fpool.alloc(F32, [NSUB, C])]
    xn1 = cx.fpool.alloc(F32, [NSUB, C])
    xq_tm = cx.bpool.alloc(BF16, [NSUB, C])
    xq_fm = cx.bpool.alloc(BF16, [DC, QT])
    out_fm = cx.fpool.alloc(F32, [DC, QT])
    scrq = (cx.fpool.alloc(F32, [NSUB]), cx.fpool.alloc(F32, [NSUB]),
            cx.fpool.alloc(F32, [C]), cx.fpool.alloc(F32, [C]))
    xn2 = cx.bpool.alloc(BF16, [NSUB, C])
    xn2_fm = cx.bpool.alloc(BF16, [DC, QT])
    qf = cx.bpool.alloc(BF16, [DC, QT])
    kf = cx.bpool.alloc(BF16, [DC, QT])
    vf = cx.bpool.alloc(BF16, [DC, QT])
    sq = xn2_fm
    vtm = cx.bpool.alloc(BF16, [NH, NSUB * 64])
    scsb = cx.fpool.alloc(F32, [4, QT])
    probs_all = cx.bpool.alloc(BF16, [NH, NSUB, QT])
    pT = cx.bpool.alloc(BF16, [NSUB, QT])
    yatt = cx.bpool.alloc(BF16, [DC, QT])
    NJ = NH * NSUB
    scra = (cx.fpool.alloc(F32, [QT]), cx.fpool.alloc(F32, [QT]),
            cx.fpool.alloc(F32, [QT]), cx.fpool.alloc(F32, [NJ]),
            cx.fpool.alloc(F32, [NJ]), cx.fpool.alloc(F32, [NJ]),
            cx.fpool.alloc(F32, [NJ]), out_fm)
    qb1 = (xn1, xq_tm, xq_fm, out_fm, scrq)
    ab = (xn2, xn2_fm, qf, kf, vf, sq, qf, kf, vtm, scsb, probs_all, pT, yatt,
          scra)

    tile_end = {}
    for k in range(NT):
        p = 0
        if k >= 1:
            for sem, tick in tile_end[k - 1]:
                for eng in (nc.vector, nc.scalar, nc.tensor):
                    eng.wait_ge(sem.h, tick)
        if k >= 1:
            for sem, tick in tile_end[k - 1]:
                nc.sync.wait_ge(sem.h, tick)
        tk = cx.load(xa[p], d["x0"][k * QT:(k + 1) * QT, :]
                     .rearrange("(a p) c -> p a c", p=128))
        for eng in (nc.vector, nc.scalar, nc.tensor):
            eng.wait_ge(cx.ld.h, tk)
        if "ffn0a" in cx.chain:
            _quant_ffn(cx, qb1, xa[p], wq_bf)
        if "attn0" in cx.chain:
            _attn(cx, ab, xa[p], wqkv_get, wproj_bf)
        if "ffn0b" in cx.chain:
            _quant_ffn(cx, qb1, xa[p], wq_bf)
        tick_v = cx.vpub()
        nc.sync.wait_ge(cx.sv.h, tick_v)
        cx.store(d[out][k * QT:(k + 1) * QT, :]
                 .rearrange("(a p) c -> p a c", p=128), xa[p])
        tile_end[k] = [(cx.sv, cx.sv.n), (cx.ss, cx.ss.n), (cx.stn, cx.stn.n),
                       (cx.st, cx.st.n)]
    cx.wpool.release()
    cx.fpool.release()
    cx.bpool.release()



def _rwkv_delta(cx, bufs, x_tm, weights, carry, first):
    nc, cc = cx.nc, cx.cc
    (xn4, xf_buf, xr, rbf, kbuf, dproj) = bufs
    wr_bf, wk_bf, wv_bf, omr, omk = weights
    tick_ln = _ln_tm(cx, x_tm, xn4, NSUB)
    nc.vector.wait_ge(cx.ss.h, cx.ss.n)
    if first:
        nc.vector.memset(xf_buf[:, :, 0:1], 0.0)
    else:
        nc.vector.tensor_copy(out=xf_buf[:, :, 0:1], in_=carry)
    tcar = cx.vpub()
    nc.tensor.wait_ge(cx.sv.h, tick_ln)
    tick = _transpose_to_fm(cx, xn4, xf_buf.rearrange("p a b -> p a b")
                            [:, :, 1:QT + 1], NSUB)
    nc.vector.wait_ge(cx.ss.h, tick)
    nc.vector.tensor_copy(out=carry, in_=xf_buf[:, :, QT:QT + 1])
    nc.vector.drain()
    for phase in range(2):
        mu = cc["mur_c"] if phase == 0 else cc["muk_c"]
        om = omr if phase == 0 else omk
        if phase == 1:
            nc.vector.wait_ge(cx.stn.h, cx.stn.n)
        for j in range(DC):
            nc.vector.tensor_scalar(out=xr[:, j], in0=xf_buf[:, j, 0:QT],
                                    scalar1=om[:, j:j + 1], scalar2=None,
                                    op0=OP.mult)
            nc.vector.drain()
            nc.vector.scalar_tensor_tensor(out=xr[:, j],
                                           in0=xf_buf[:, j, 1:QT + 1],
                                           scalar=mu[:, j:j + 1], in1=xr[:, j],
                                           op0=OP.mult, op1=OP.add)
            nc.vector.drain()
        tick_mix = cx.vpub()
        nc.tensor.wait_ge(cx.sv.h, tick_mix)
        w = wr_bf if phase == 0 else wk_bf
        for mi in range(DC):
            pkey = f"b{mi % 2}"
            pslot = cx.ps[mi % 2][:, 0:QT]
            tt = _fm_matmul(cx, w, xr, range(DC), mi, pslot, pkey)
            nc.scalar.wait_ge(cx.stn.h, tt)
            if phase == 0:
                cp = nc.scalar.activation(out=rbf[:, mi], in_=pslot,
                                          func=AF.Sigmoid)
            else:
                cp = nc.scalar.activation(out=kbuf[:, mi], in_=pslot,
                                          func=AF.Relu)
            cx.slot_set(pkey, cx.ss, cx.ss.inc(cp))
    nc.vector.wait_ge(cx.ss.h, cx.ss.n)
    for j in range(DC):
        nc.vector.tensor_tensor(out=kbuf[:, j], in0=kbuf[:, j], in1=kbuf[:, j],
                                op=OP.mult)
    nc.vector.drain()
    tk2 = cx.vpub()
    nc.tensor.wait_ge(cx.sv.h, tk2)
    for mi in range(DC):
        pkey = f"b{mi % 2}"
        pslot = cx.ps[mi % 2][:, 0:QT]
        tt = _fm_matmul(cx, wv_bf, kbuf, range(DC), mi, pslot, pkey)
        nc.vector.wait_ge(cx.stn.h, tt)
        op = nc.vector.tensor_tensor(out=dproj[:, mi], in0=rbf[:, mi],
                                     in1=pslot, op=OP.mult)
        cx.slot_set(pkey, cx.sv, cx.sv.inc(op))
    nc.tensor.wait_ge(cx.sv.h, cx.sv.n)


def _rwkv_weights(cx):
    nc, d = cx.nc, cx.dram
    wr_bf = cx.wpool.alloc(BF16, [DC, C])
    wk_bf = cx.wpool.alloc(BF16, [DC, C])
    wv_bf = cx.wpool.alloc(BF16, [DC, C])
    stage = cx.fpool.alloc(F32, [C])
    for name, dst in (("wr_T", wr_bf), ("wk_T", wk_bf), ("wv_T", wv_bf)):
        for kc in range(DC):
            t0 = cx.load(stage, d[name][kc * 128:(kc + 1) * 128, :])
            nc.vector.wait_ge(cx.ld.h, t0)
            nc.vector.tensor_copy(out=dst[:, kc], in_=stage)
            tw = cx.vpub()
            nc.sync.wait_ge(cx.sv.h, tw)
    omr = cx.wpool.alloc(F32, [DC])
    omk = cx.wpool.alloc(F32, [DC])
    nc.vector.wait_ge(cx.ld.h, cx.ld.n)
    nc.vector.tensor_scalar(out=omr, in0=cx.cc["mur_c"], scalar1=-1.0,
                            scalar2=1.0, op0=OP.mult, op1=OP.add)
    nc.vector.drain()
    nc.vector.tensor_scalar(out=omk, in0=cx.cc["muk_c"], scalar1=-1.0,
                            scalar2=1.0, op0=OP.mult, op1=OP.add)
    cx.vpub()
    return (wr_bf, wk_bf, wv_bf, omr, omk)


def _pass2(cx, out=None):
    nc, d, cc = cx.nc, cx.dram, cx.cc
    NT = cx.NT
    cx.wpool.mark()
    cx.fpool.mark()
    cx.bpool.mark()
    weights = _rwkv_weights(cx)
    cc["gamma_bc"] = _build_bcast(cx, "gamma_r", cx.fpool)
    cc["beta_bc"] = _build_bcast(cx, "beta_r", cx.fpool)
    cc["outb_bc"] = _build_bcast(cx, "outb_r", cx.wpool)

    xa = cx.fpool.alloc(F32, [NSUB, C])
    a_tm = cx.fpool.alloc(F32, [NSUB, C])
    h1 = cx.fpool.alloc(F32, [NSUB, C])
    dproj = cx.fpool.alloc(F32, [DC, QT])
    xn4 = cx.bpool.alloc(BF16, [NSUB, C])
    xf_buf = cx.bpool.alloc(BF16, [DC, QT + 1])
    xr = cx.bpool.alloc(BF16, [DC, QT])
    rbf = cx.bpool.alloc(BF16, [DC, QT])
    kbuf = cx.bpool.alloc(BF16, [DC, QT])
    h_tm = cx.bpool.alloc(BF16, [NSUB, C])
    carry = cx.bpool.alloc(BF16, [DC, 1])
    rw_bufs = (xn4, xf_buf, xr, rbf, kbuf, dproj)

    tile_end = {}
    for k in range(NT):
        if k >= 1:
            for sem, tick in tile_end[k - 1]:
                for eng in (nc.vector, nc.scalar, nc.tensor):
                    eng.wait_ge(sem.h, tick)
                nc.sync.wait_ge(sem.h, tick)
        tk = cx.load(xa, d["s_x3"][k * QT:(k + 1) * QT, :]
                     .rearrange("(a p) c -> p a c", p=128))
        for eng in (nc.vector, nc.scalar, nc.tensor):
            eng.wait_ge(cx.ld.h, tk)
        _rwkv_delta(cx, rw_bufs, xa, weights, carry, first=(k == 0))
        _residual_add_fm_delta(cx, dproj, xa, NSUB, 0.5)
        _ln_tm(cx, xa, a_tm, NSUB)
        _ln_tm(cx, a_tm, h1, NSUB)
        for i in range(NSUB):
            nc.vector.tensor_tensor(out=h1[:, i], in0=h1[:, i],
                                    in1=cc["gamma_bc"], op=OP.mult)
            nc.vector.drain()
            nc.vector.tensor_tensor(out=h_tm[:, i], in0=h1[:, i],
                                    in1=cc["beta_bc"], op=OP.add)
            nc.vector.drain()
        for i in range(NSUB):
            nc.vector.tensor_tensor(out=h1[:, i], in0=xa[:, i], in1=a_tm[:, i],
                                    op=OP.add)
            nc.vector.drain()
            nc.vector.tensor_tensor(out=h1[:, i], in0=h1[:, i],
                                    in1=cc["outb_bc"], op=OP.add)
            nc.vector.drain()
        tick_v = cx.vpub()
        nc.sync.wait_ge(cx.sv.h, tick_v)
        cx.store(d["s_h"][k * QT:(k + 1) * QT, :]
                 .rearrange("(a p) c -> p a c", p=128), h_tm)
        cx.store(d["s_p5a"][k * QT:(k + 1) * QT, :]
                 .rearrange("(a p) c -> p a c", p=128), h1)
        if out:
            cx.store(d[out][k * QT:(k + 1) * QT, :]
                     .rearrange("(a p) c -> p a c", p=128), xa)
        tile_end[k] = [(cx.sv, cx.sv.n), (cx.ss, cx.ss.n), (cx.stn, cx.stn.n),
                       (cx.st, cx.st.n)]
    cx.wpool.release()
    cx.fpool.release()
    cx.bpool.release()


def _mamba_pass(cx, dgn, out=None):
    nc, d, cc = cx.nc, cx.dram, cx.cc
    NT = cx.NT
    rev = dgn == "b"
    src_p5 = "s_p5a" if dgn == "f" else "s_p5b"
    dst_p5 = "s_p5b" if dgn == "f" else (out or "s_x5")
    cx.wpool.mark()
    cx.fpool.mark()
    cx.bpool.mark()
    inw_sl = [cx.wpool.alloc(BF16, [DC, 512]) for _ in range(2)]
    weff_sl = [cx.wpool.alloc(BF16, [DIC, 256]) for _ in range(2)]
    xproj_bf = cx.wpool.alloc(BF16, [DIC, 128])
    dtw_bf = cx.wpool.alloc(BF16, [DI])
    cx.fpool.mark()
    stage = cx.fpool.alloc(F32, [DI])
    for kc in range(DIC):
        t0 = cx.load(stage[:, 0:128], d[f"xproj_T_{dgn}"][kc * 128:(kc + 1) * 128, :])
        nc.vector.wait_ge(cx.ld.h, t0)
        nc.vector.tensor_copy(out=xproj_bf[:, kc], in_=stage[:, 0:128])
        tw = cx.vpub()
        nc.sync.wait_ge(cx.sv.h, tw)
    t0 = cx.load(stage[0:DTR, :], d[f"dtw_T_{dgn}"][:, :])
    nc.vector.wait_ge(cx.ld.h, t0)
    nc.vector.tensor_copy(out=dtw_bf[0:DTR, :], in_=stage[0:DTR, :])
    cx.vpub()
    nc.sync.wait_ge(cx.sv.h, cx.sv.n)
    cx.fpool.release()

    h_tm = [cx.bpool.alloc(BF16, [NSUB, C]) for _ in range(2)]
    h_fm = cx.bpool.alloc(BF16, [DC, QT])
    xi_buf = cx.bpool.alloc(BF16, [DIC, QT + 3])
    hall = cx.bpool.alloc(BF16, [DS, QT])
    zs_fm = [cx.bpool.alloc(BF16, [DIC, QT]) for _ in range(2)]
    xc_fm = [cx.bpool.alloc(BF16, [DIC, QT]) for _ in range(2)]
    rows_bf = [cx.bpool.alloc(BF16, [QT]) for _ in range(2)]
    bcr = [cx.bpool.alloc(BF16, [2, QT]) for _ in range(2)]
    du_bf = cx.bpool.alloc(BF16, [QT])
    bc_all = cx.bpool.alloc(BF16, [DS, 2, QT])
    w_half = [cx.bpool.alloc(BF16, [DS // 2, QT]) for _ in range(2)]
    carry3 = cx.bpool.alloc(BF16, [DIC, 3])

    dn_fm = [cx.fpool.alloc(F32, [QT]) for _ in range(2)]
    e_half = [cx.fpool.alloc(F32, [DS // 2, QT]) for _ in range(2)]
    sgbuf = cx.fpool.alloc(F32, [QT])
    convacc = [cx.fpool.alloc(F32, [QT]) for _ in range(DIC)]
    cgtmp = cx.fpool.alloc(F32, [QT])
    ys_f = cx.fpool.alloc(F32, [QT])
    fix8 = cx.fpool.alloc(F32, [DS // 2])
    xdbl = cx.fpool.alloc(F32, [QT])
    p5_tm = cx.fpool.alloc(F32, [NSUB, C])
    state = cx.fpool.alloc(F32, [DIC, DS])
    pevac = [cx.fpool.alloc(F32, [QT]) for _ in range(2)]

    nc.vector.wait_ge(cx.ld.h, cx.ld.n)
    nc.vector.memset(state.rearrange("p a b -> p (a b)"), 0.0)
    nc.vector.drain()
    nc.vector.memset(carry3.rearrange("p a b -> p (a b)"), 0.0)
    tick0 = cx.vpub()
    nc.sync.wait_ge(cx.sv.h, tick0)
    nc.gpsimd.wait_ge(cx.sv.h, tick0)

    sub_map = list(range(NSUB - 1, -1, -1)) if rev else None
    perm = "rev" if rev else None
    cw = cc[f"convw_{dgn}"]

    def stream_w(dram_ap, slot_bufs, qi, key_pfx):
        slot = qi % 2
        key = (key_pfx, slot)
        cx.slot_wait(nc.sync, key)
        tl = cx.load(slot_bufs[slot], dram_ap)
        return slot_bufs[slot], tl, key

    def stage_a_front(kk):
        k = NT - 1 - kk if rev else kk
        par = kk % 2
        cx.slot_wait(nc.sync, ("htm", par))
        tk = cx.load(h_tm[par], d["s_h"][k * QT:(k + 1) * QT, :]
                     .rearrange("(a p) c -> p a c", p=128))
        nc.tensor.wait_ge(cx.ld.h, tk)
        tfm = _transpose_to_fm(cx, h_tm[par], h_fm, NSUB, src_bf=True,
                               perm=perm, sub_map=sub_map)
        cx.slot_set(("htm", par), cx.stn, cx.stn.n)
        cx.slot_wait(nc.scalar, "xi")
        cx.slot_wait(nc.gpsimd, "xi")
        gp = nc.gpsimd.tensor_copy(out=xi_buf[:, :, 0:3], in_=carry3[:, :, :])
        tcar = cx.sg.inc(gp)
        nc.scalar.wait_ge(cx.sg.h, tcar)
        nc.tensor.wait_ge(cx.ss.h, tfm)
        conv_tick = {}
        for eg in range(8):
            wv, tl, wkey = stream_w(
                d[f"b_inw_{dgn}"][:, eg * 512:(eg + 1) * 512]
                .rearrange("(a p) c -> p a c", p=128), inw_sl, eg, "inw")
            nc.tensor.wait_ge(cx.ld.h, tl)
            for mi in range(4):
                mg = eg * 4 + mi
                pkey = f"b{mg % 2}"
                pslot = cx.ps[mg % 2][:, 0:QT]
                tt = _fm_matmul(cx, wv, h_fm, range(DC), mi, pslot, pkey)
                nc.scalar.wait_ge(cx.stn.h, tt)
                if mg < DIC:
                    cp = nc.scalar.copy(out=xi_buf[:, mg, 3:QT + 3], in_=pslot)
                    te = cx.ss.inc(cp)
                    cx.slot_set(pkey, cx.ss, te)
                    c = mg
                    ca = convacc[c]
                    nc.gpsimd.wait_ge(cx.ss.h, te)
                    if c in cx.mp_silu:
                        nc.gpsimd.wait_ge(cx.ss.h, cx.mp_silu[c])
                    nc.gpsimd.tensor_tensor(
                        out=ca, in0=xi_buf[:, c, 0:QT],
                        in1=cw[:, c, 0:1].broadcast_to([128, QT]), op=OP.mult)
                    nc.gpsimd.drain()
                    for j in range(1, 4):
                        nc.gpsimd.tensor_tensor(
                            out=cgtmp, in0=xi_buf[:, c, j:j + QT],
                            in1=cw[:, c, j:j + 1].broadcast_to([128, QT]),
                            op=OP.mult)
                        nc.gpsimd.drain()
                        nc.gpsimd.tensor_tensor(out=ca, in0=ca, in1=cgtmp,
                                                op=OP.add)
                        nc.gpsimd.drain()
                    conv_tick[c] = cx.sg.inc(nc.gpsimd.drain())
                else:
                    cp = nc.scalar.activation(out=zs_fm[par][:, mg - DIC],
                                              in_=pslot, func=AF.Silu)
                    cx.slot_set(pkey, cx.ss, cx.ss.inc(cp))
            cx.slot_set(wkey, cx.stn, cx.stn.n)
        gp = nc.gpsimd.tensor_copy(out=carry3[:, :, :],
                                   in_=xi_buf[:, :, QT:QT + 3])
        tcar2 = cx.sg.inc(gp)
        cx.slot_set("xi", cx.sg, tcar2)
        cx.mp_conv = conv_tick

    def stage_a_back(kk):
        par = kk % 2
        silu_tick = {}
        for c in range(DIC):
            nc.scalar.wait_ge(cx.sg.h, cx.mp_conv[c])
            cp2 = nc.scalar.activation(
                out=xc_fm[par][:, c], in_=convacc[c], func=AF.Silu,
                bias=cc[f"convb_c_{dgn}"][:, c:c + 1])
            silu_tick[c] = cx.ss.inc(cp2)
            cx.mp_silu[c] = silu_tick[c]
        cx.slot_wait(nc.tensor, "b2")
        mm = None
        for kc in range(DIC):
            nc.tensor.wait_ge(cx.ss.h, silu_tick[kc])
            mm = nc.tensor.matmul(cx.ps[2][:, 0:QT], xproj_bf[:, kc],
                                  xc_fm[par][:, kc], start=(kc == 0),
                                  stop=(kc == DIC - 1))
        tt = cx.stn.inc(mm)
        nc.scalar.wait_ge(cx.stn.h, tt)
        cx.slot_wait(nc.scalar, "xdbl")
        cp = nc.scalar.copy(out=xdbl[:, :], in_=cx.ps[2][:, 0:QT])
        tsd = cx.ss.inc(cp)
        cx.slot_set("b2", cx.ss, tsd)
        nc.gpsimd.wait_ge(cx.ss.h, tsd)
        cx.slot_wait(nc.gpsimd, ("rows", par))
        nc.gpsimd.tensor_copy(out=rows_bf[par][0:DTR, :], in_=xdbl[0:DTR, :])
        nc.gpsimd.drain()
        nc.gpsimd.tensor_scalar(out=bcr[par][0:DS, 0, :],
                                in0=xdbl[DTR:DTR + DS, :], scalar1=-1.0,
                                scalar2=None, op0=OP.mult)
        nc.gpsimd.drain()
        gp = nc.gpsimd.tensor_copy(out=bcr[par][0:DS, 1, :],
                                   in_=xdbl[96:96 + DS, :])
        trows = cx.sg.inc(gp)
        cx.slot_set("xdbl", cx.sg, trows)
        nc.tensor.wait_ge(cx.sg.h, trows)
        bcflat = bcr[par].rearrange("p a b -> p (a b)")
        for s in range(DS):
            key = f"b{3 + (s % 2)}"
            cx.slot_wait(nc.tensor, key)
            mm = nc.tensor.matmul(cx.ps[3 + (s % 2)][:, 0:512],
                                  cc["sel_bf"][0:DS, s * 128:(s + 1) * 128],
                                  bcflat[0:DS, :], start=True, stop=True)
            tt = cx.stn.inc(mm)
            nc.scalar.wait_ge(cx.stn.h, tt)
            op = nc.scalar.copy(
                out=bc_all[:, s, :, :].rearrange("p a b -> p (a b)"),
                in_=cx.ps[3 + (s % 2)][:, 0:512])
            cx.slot_set(key, cx.ss, cx.ss.inc(op))
        cx.mp_trows[kk] = trows

    def stage_b1(kk):
        k = NT - 1 - kk if rev else kk
        par = kk % 2
        nc.sync.wait_ge(cx.st.h, cx.st.n)
        cx.mp_tkp = cx.load(p5_tm, d[src_p5][k * QT:(k + 1) * QT, :]
                            .rearrange("(a p) c -> p a c", p=128))
        SH = DS // 2
        nc.tensor.wait_ge(cx.sg.h, cx.mp_trows[kk])
        for c in range(DIC):
            dnb = dn_fm[c % 2]
            pkey = f"b{c % 2}"
            pslot = cx.ps[c % 2][:, 0:QT]
            cx.slot_wait(nc.tensor, pkey)
            mm = nc.tensor.matmul(pslot, dtw_bf[0:DTR, c * 128:(c + 1) * 128],
                                  rows_bf[par][0:DTR, 0:QT], start=True,
                                  stop=True)
            tt = cx.stn.inc(mm)
            nc.scalar.wait_ge(cx.stn.h, tt)
            if (c % 2) in cx.mp_du:
                nc.scalar.wait_ge(cx.sv.h, cx.mp_du[c % 2])
            nc.scalar.activation(out=sgbuf, in_=pslot, func=AF.Sigmoid,
                                 scale=-1.0, bias=cc[f"ndtb_{dgn}"][:, c:c + 1])
            nc.scalar.drain()
            op = nc.scalar.activation(out=dnb, in_=sgbuf, func=AF.Ln)
            ts_dn = cx.ss.inc(op)
            cx.slot_set(pkey, cx.ss, ts_dn)
            nc.scalar.drain()
            nc.vector.wait_ge(cx.ss.h, ts_dn)
            du_op = nc.vector.tensor_tensor(out=du_bf, in0=dnb,
                                            in1=xc_fm[par][:, c], op=OP.mult)
            cx.mp_du[c % 2] = cx.sv.inc(du_op)
            nc.vector.drain()
            for hf in range(2):
                epar = cx.mp_half_i % 2
                eb = e_half[epar]
                wb = w_half[epar]
                s0 = hf * SH
                if epar in cx.mp_e_cons:
                    nc.scalar.wait_ge(cx.sv.h, cx.mp_e_cons[epar])
                ei = None
                for j in range(SH):
                    ei = nc.scalar.activation(
                        out=eb[:, j], in_=dnb, func=AF.Exp,
                        scale=cc[f"apos_{dgn}"][:, c, s0 + j:s0 + j + 1])
                tse = cx.ss.inc(ei)
                nc.vector.wait_ge(cx.ss.h, tse)
                nc.vector.tensor_tensor(
                    out=wb[:, :, :],
                    in0=du_bf[:].unsqueeze(1).broadcast_to([128, SH, QT]),
                    in1=bc_all[:, s0:s0 + SH, 0, :], op=OP.mult)
                nc.vector.drain()
                nc.vector.tensor_tensor(out=fix8, in0=eb[:, :, 0],
                                        in1=state[:, c, s0:s0 + SH], op=OP.mult)
                nc.vector.drain()
                nc.vector.tensor_tensor(out=wb[:, :, 0], in0=wb[:, :, 0],
                                        in1=fix8, op=OP.add)
                nc.vector.drain()
                nc.vector.memset(eb[:, :, 0:1], 0.0)
                nc.vector.drain()
                sc = nc.vector.tensor_tensor_scan(
                    out=hall[:, s0:s0 + SH, :].rearrange("p a b -> p (a b)"),
                    data0=eb[:, :, :].rearrange("p a b -> p (a b)"),
                    data1=wb[:, :, :].rearrange("p a b -> p (a b)"),
                    initial=0.0, op0=OP.mult, op1=OP.add)
                cx.mp_e_cons[epar] = cx.sv.inc(sc)
                nc.vector.drain()
                nc.vector.tensor_copy(out=state[:, c, s0:s0 + SH],
                                      in_=hall[:, s0:s0 + SH, QT - 1])
                nc.vector.drain()
                cx.mp_half_i += 1
            hc = nc.vector.tensor_tensor(
                out=hall[:, :, :], in0=hall[:, :, :],
                in1=bc_all[:, :, 1, :], op=OP.mult)
            nc.vector.drain()
            stride = 1
            while stride < DS // 2:
                nw = DS // (2 * stride)
                nc.vector.tensor_tensor(
                    out=hall[:, 0:nw, :].rearrange("p a b -> p (a b)"),
                    in0=hall[:, 0:nw, :].rearrange("p a b -> p (a b)"),
                    in1=hall[:, nw:2 * nw, :].rearrange("p a b -> p (a b)"),
                    op=OP.add)
                nc.vector.drain()
                stride *= 2
            nc.vector.tensor_tensor(out=ys_f, in0=hall[:, 0, :],
                                    in1=hall[:, 1, :], op=OP.add)
            nc.vector.drain()
            nc.vector.scalar_tensor_tensor(out=ys_f, in0=xc_fm[par][:, c],
                                           scalar=cc[f"dd_c_{dgn}"][:, c:c + 1],
                                           in1=ys_f, op0=OP.mult, op1=OP.add)
            nc.vector.drain()
            gi = nc.vector.tensor_tensor(out=zs_fm[par][:, c], in0=ys_f,
                                         in1=zs_fm[par][:, c], op=OP.mult)
            cx.sv.inc(gi)
        cx.mp_scan_tick = cx.sv.n

    def stage_b2(kk):
        k = NT - 1 - kk if rev else kk
        par = kk % 2
        tick_scan = cx.mp_scan_tick
        nc.tensor.wait_ge(cx.sv.h, tick_scan)
        nc.vector.wait_ge(cx.ld.h, cx.mp_tkp)
        for qg in range(4):
            wv, tl, wkey = stream_w(
                d[f"b_weff_{dgn}"][:, qg * 256:(qg + 1) * 256]
                .rearrange("(a p) c -> p a c", p=128), weff_sl, qg, "weff")
            nc.tensor.wait_ge(cx.ld.h, tl)
            for mi in range(2):
                mg = qg * 2 + mi
                pkey = f"b{mg % 2}"
                pslot = cx.ps[mg % 2][:, 0:QT]
                tt = _fm_matmul(cx, wv, zs_fm[par], range(DIC), mi, pslot, pkey)
                nc.scalar.wait_ge(cx.stn.h, tt)
                cx.slot_wait(nc.scalar, ("pev", mg % 2))
                cp = nc.scalar.copy(out=pevac[mg % 2], in_=pslot)
                tse = cx.ss.inc(cp)
                cx.slot_set(pkey, cx.ss, tse)
                for i in range(NSUB):

                    def consume(psl, tick, i=i, mg=mg):
                        nc.vector.wait_ge(cx.stn.h, tick)
                        di = (sub_map[i] if sub_map else i)
                        dst = p5_tm[:, di, mg * 128:(mg + 1) * 128]
                        op = nc.vector.tensor_tensor(out=dst, in0=dst, in1=psl,
                                                     op=OP.add)
                        return cx.sv, cx.sv.inc(op)

                    nc.tensor.wait_ge(cx.ss.h, tse)
                    _tp128(cx, pevac[mg % 2][:, i * 128:(i + 1) * 128],
                           consume, bf=False, perm=perm, f2=True)
                cx.slot_set(("pev", mg % 2), cx.stn, cx.stn.n)
            cx.slot_set(wkey, cx.stn, cx.stn.n)
        tick_v = cx.vpub()
        nc.sync.wait_ge(cx.sv.h, tick_v)
        cx.store(d[dst_p5][k * QT:(k + 1) * QT, :]
                 .rearrange("(a p) c -> p a c", p=128), p5_tm)

    cx.mp_half_i = 0
    cx.mp_e_cons = {}
    cx.mp_scan_tick = 0
    cx.mp_silu = {}
    cx.mp_du = {}
    cx.mp_trows = {}
    stage_a_front(0)
    stage_a_back(0)
    for kk in range(NT):
        stage_b1(kk)
        if kk + 1 < NT:
            stage_a_front(kk + 1)
        stage_b2(kk)
        if kk + 1 < NT:
            stage_a_back(kk + 1)
    cx.wpool.release()
    cx.fpool.release()
    cx.bpool.release()


def _pass5(cx):
    nc, d, cc = cx.nc, cx.dram, cx.cc
    NT = cx.NT
    cx.wpool.mark()
    cx.fpool.mark()
    cx.bpool.mark()
    weights = _rwkv_weights(cx)
    xa = cx.fpool.alloc(F32, [NSUB, C])
    dproj = cx.fpool.alloc(F32, [DC, QT])
    xn4 = cx.bpool.alloc(BF16, [NSUB, C])
    xf_buf = cx.bpool.alloc(BF16, [DC, QT + 1])
    xr = cx.bpool.alloc(BF16, [DC, QT])
    rbf = cx.bpool.alloc(BF16, [DC, QT])
    kbuf = cx.bpool.alloc(BF16, [DC, QT])
    carry = cx.bpool.alloc(BF16, [DC, 1])
    rw_bufs = (xn4, xf_buf, xr, rbf, kbuf, dproj)

    tile_end = {}
    for k in range(NT):
        if k >= 1:
            for sem, tick in tile_end[k - 1]:
                for eng in (nc.vector, nc.scalar, nc.tensor):
                    eng.wait_ge(sem.h, tick)
                nc.sync.wait_ge(sem.h, tick)
        tk = cx.load(xa, d["s_x5"][k * QT:(k + 1) * QT, :]
                     .rearrange("(a p) c -> p a c", p=128))
        for eng in (nc.vector, nc.scalar, nc.tensor):
            eng.wait_ge(cx.ld.h, tk)
        _rwkv_delta(cx, rw_bufs, xa, weights, carry, first=(k == 0))
        _residual_add_fm_delta(cx, dproj, xa, NSUB, 0.5)
        tick_v = cx.vpub()
        nc.sync.wait_ge(cx.sv.h, tick_v)
        cx.store(d["y"][k * QT:(k + 1) * QT, :]
                 .rearrange("(a p) c -> p a c", p=128), xa)
        tile_end[k] = [(cx.sv, cx.sv.n), (cx.ss, cx.ss.n), (cx.stn, cx.stn.n),
                       (cx.st, cx.st.n)]
    cx.wpool.release()
    cx.fpool.release()
    cx.bpool.release()


# version 07 (final)
# speedup vs baseline: 1.4006x; 1.0550x over previous
import contextlib

import numpy as np
import ml_dtypes

import concourse.bass as bass
import concourse.mybir as mybir
from concourse.bass_utils import run_bass_kernel_spmd
from concourse.alu_op_type import AluOpType as OP

F32 = mybir.dt.float32
BF16 = mybir.dt.bfloat16
I8 = mybir.dt.int8
AF = mybir.ActivationFunctionType
AX = mybir.AxisListType

C = 1024
DC = C // 128
DI = 2048
DIC = DI // 128
DS = 16
DTR = 64
NH = 16
HD = 64
LS = 256
QT = 256
NSUB = QT // 128
EPS_LN = 1e-5
EPS_RMS = 1e-6
EPS_Q = 1e-6
NEG = -1.0e30

CHAIN = ("ffn0a", "attn0", "ffn0b", "rwkva", "mamba", "rwkvb")


def _cdiv(a, b):
    return (a + b - 1) // b


class Pool:

    def __init__(self, nc, ctx, name, nbytes):
        assert nbytes % 64 == 0
        self.t = ctx.enter_context(nc.sbuf_tensor(name, [128, nbytes], I8))
        self.name = name
        self.nbytes = nbytes
        self.cur = 0
        self.marks = []

    def alloc(self, dtype, shape):
        esz = mybir.dt.size(dtype)
        n = 1
        for s in shape:
            n *= s
        nb = _cdiv(n * esz, 64) * 64
        assert self.cur + nb <= self.nbytes, (
            f"pool {self.name} overflow: {self.cur}+{nb}>{self.nbytes}")
        ap = self.t[:, self.cur:self.cur + n * esz].bitcast(dtype)
        self.cur += nb
        if len(shape) > 1:
            dims = " ".join(f"d{i}" for i in range(len(shape)))
            kw = {f"d{i}": s for i, s in enumerate(shape)}
            ap = ap.rearrange(f"p ({dims}) -> p {dims}", **kw)
        return ap

    def mark(self):
        self.marks.append(self.cur)

    def release(self):
        self.cur = self.marks.pop()


class Sem:
    def __init__(self, nc, ctx, name, step=1):
        self.h = ctx.enter_context(nc.semaphore(name))
        self.n = 0
        self.step = step

    def inc(self, instr):
        instr.then_inc(self.h, self.step)
        self.n += self.step
        return self.n


class Ctx:
    def load(self, dst, src):
        self.nc.sync.wait_ge(self.ld.h, self.ld.n)
        self.nc.sync.dma_start(out=dst, in_=src).then_inc(self.ld.h, 16)
        self.ld.n += 16
        return self.ld.n

    def loadw(self, dst, src):
        self.nc.scalar.wait_ge(self.lw.h, self.lw.n)
        self.nc.scalar.dma_start(out=dst, in_=src).then_inc(self.lw.h, 16)
        self.lw.n += 16
        return self.lw.n

    def store(self, dst, src):
        self.nc.sync.wait_ge(self.st.h, self.st.n)
        self.nc.sync.dma_start(out=dst, in_=src).then_inc(self.st.h, 16)
        self.st.n += 16
        return self.st.n

    def slot_wait(self, eng, key):
        rec = self.slots.get(key)
        if rec is not None:
            sem, tick = rec
            eng.wait_ge(sem.h, tick)

    def slot_set(self, key, sem, tick):
        self.slots[key] = (sem, tick)

    def vpub(self):
        return self.sv.inc(self.nc.vector.drain())

    def spub(self):
        return self.ss.inc(self.nc.scalar.drain())

    def tpub(self):
        return self.stn.inc(self.nc.tensor.drain())

    def gpub(self):
        return self.sg.inc(self.nc.gpsimd.drain())



def kernel(**inputs):
    T = inputs["L0_attn_x"].shape[1]
    n_cores = 8
    x = np.ascontiguousarray(np.asarray(inputs["L0_attn_x"], dtype=np.float32))

    def t(a):
        return np.ascontiguousarray(np.asarray(a, dtype=np.float32).T)

    def cols(v, n):
        v = np.asarray(v, dtype=np.float32).reshape(n, 128)
        return np.ascontiguousarray(v.T)

    common = host_inputs(inputs, t, cols)
    nc = build_program(T=T)
    in_maps = []
    for c in range(n_cores):
        m = dict(common)
        m["x0"] = np.ascontiguousarray(x[c])
        in_maps.append(m)
    res = run_bass_kernel_spmd(nc, in_maps, core_ids=list(range(n_cores)))
    out = np.stack([np.asarray(res.results[c]["y"]) for c in range(n_cores)], axis=0)
    return out.astype(np.float32)


def host_inputs(inputs, t, cols):
    common = {
        "wq_T": np.ascontiguousarray(np.asarray(inputs["L0_ffn_weight_q"]).T),
        "w_qkv": np.ascontiguousarray(np.asarray(inputs["L0_attn_w_qkv"], dtype=np.float32)),
        "w_proj": np.ascontiguousarray(np.asarray(inputs["L0_attn_w_proj"], dtype=np.float32)),
        "lnw_c": cols(inputs["L0_attn_ln_w"], DC),
        "wsc_c": cols(inputs["L0_ffn_weight_scale"], DC),
        "fbias_c": cols(inputs["L0_ffn_bias"], DC),
        "fbias_r": np.asarray(inputs["L0_ffn_bias"], np.float32).reshape(1, C),
        "gamma_r": np.asarray(inputs["L1_attn_gamma"], np.float32).reshape(1, C),
        "beta_r": np.asarray(inputs["L1_attn_beta"], np.float32).reshape(1, C),
        "outb_r": np.asarray(inputs["L1_attn_out_b"], np.float32).reshape(1, C),
        "wr_T": t(inputs["L1_ffn_W_r"]),
        "wk_T": t(inputs["L1_ffn_W_k"]),
        "wv_T": t(inputs["L1_ffn_W_v"]),
        "mur_c": cols(inputs["L1_ffn_mu_r"], DC),
        "muk_c": cols(inputs["L1_ffn_mu_k"], DC),
        "outw_T": t(inputs["L1_attn_out_w"]),
    }
    for d in ("f", "b"):
        common[f"inw_T_{d}"] = t(inputs[f"L1_attn_in_w_{d}"])
        xp = t(inputs[f"L1_attn_xproj_w_{d}"])
        xpp = np.zeros((DI, 128), np.float32)
        xpp[:, 0:DTR + DS] = xp[:, 0:DTR + DS]
        xpp[:, 96:96 + DS] = xp[:, DTR + DS:]
        common[f"xproj_T_{d}"] = xpp
        common[f"dtw_T_{d}"] = t(inputs[f"L1_attn_dt_w_{d}"])
        common[f"mout_{d}"] = np.ascontiguousarray(
            np.asarray(inputs[f"L1_attn_mout_w_{d}"], np.float32))
        common[f"convw_{d}"] = np.ascontiguousarray(
            np.asarray(inputs[f"L1_attn_conv_w_{d}"], np.float32).reshape(DI, 4)
            .reshape(DIC, 128, 4).transpose(1, 0, 2))
        common[f"convb_c_{d}"] = cols(inputs[f"L1_attn_conv_b_{d}"], DIC)
        common[f"dtb_c_{d}"] = cols(inputs[f"L1_attn_dt_b_{d}"], DIC)
        common[f"dd_c_{d}"] = cols(inputs[f"L1_attn_D_{d}"], DIC)
        common[f"alog_{d}"] = np.ascontiguousarray(
            np.asarray(inputs[f"L1_attn_A_log_{d}"], np.float32)
            .reshape(DIC, 128, DS).transpose(1, 0, 2).reshape(128, DIC * DS))

    ident = np.eye(128)
    prev = ident[:, ::-1]
    tri0 = np.where(np.arange(LS)[None, :] <= np.arange(128)[:, None], 0.0, NEG)
    tri1 = np.where(np.arange(LS)[None, :] <= 128 + np.arange(128)[:, None], 0.0, NEG)
    sel = np.zeros((DS, DS * 128), np.float32)
    for s in range(DS):
        sel[s, s * 128:(s + 1) * 128] = 1.0
    common.update({
        "ident_bf": ident.astype(ml_dtypes.bfloat16),
        "ident_f": ident.astype(np.float32),
        "prev_bf": prev.astype(ml_dtypes.bfloat16),
        "prev_f": prev.astype(np.float32),
        "ones_bf": np.ones((1, 128), ml_dtypes.bfloat16),
        "ones_col_bf": np.ones((128, 1), ml_dtypes.bfloat16),
        "ones_f": np.ones((1, 128), np.float32),
        "tri0": tri0.astype(np.float32),
        "tri1": tri1.astype(np.float32),
        "sel_bf": sel.astype(ml_dtypes.bfloat16),
    })
    return common



def build_program(T=8192, chain=CHAIN, taps=(), dbg_level=99):
    nc = bass.Bass("TRN2", target_bir_lowering=False)
    NT = T // QT
    cx = Ctx()
    cx.dbg = dbg_level
    cx.nc = nc
    cx.T = T
    cx.NT = NT
    cx.chain = chain
    cx.taps = set(taps)
    cx.tap_dram = {}
    cx.slots = {}

    dram = {}

    def din(name, shape, dt):
        dram[name] = nc.dram_tensor(name, shape, dt, kind="ExternalInput").ap()

    din("x0", [T, C], F32)
    din("wq_T", [C, C], I8)
    din("w_qkv", [C, 3 * C], F32)
    din("w_proj", [C, C], F32)
    din("lnw_c", [128, DC], F32)
    din("wsc_c", [128, DC], F32)
    din("fbias_c", [128, DC], F32)
    din("fbias_r", [1, C], F32)
    din("gamma_r", [1, C], F32)
    din("beta_r", [1, C], F32)
    din("outb_r", [1, C], F32)
    din("wr_T", [C, C], F32)
    din("wk_T", [C, C], F32)
    din("wv_T", [C, C], F32)
    din("mur_c", [128, DC], F32)
    din("muk_c", [128, DC], F32)
    din("outw_T", [DI, C], F32)
    for d in ("f", "b"):
        din(f"inw_T_{d}", [C, 4 * C], F32)
        din(f"xproj_T_{d}", [DI, 128], F32)
        din(f"dtw_T_{d}", [DTR, DI], F32)
        din(f"mout_{d}", [C, DI], F32)
        din(f"convw_{d}", [128, DIC, 4], F32)
        din(f"convb_c_{d}", [128, DIC], F32)
        din(f"dtb_c_{d}", [128, DIC], F32)
        din(f"dd_c_{d}", [128, DIC], F32)
        din(f"alog_{d}", [128, DIC * DS], F32)
    din("ident_bf", [128, 128], BF16)
    din("ident_f", [128, 128], F32)
    din("prev_bf", [128, 128], BF16)
    din("prev_f", [128, 128], F32)
    din("ones_bf", [1, 128], BF16)
    din("ones_col_bf", [128, 1], BF16)
    din("ones_f", [1, 128], F32)
    din("tri0", [128, LS], F32)
    din("tri1", [128, LS], F32)
    din("sel_bf", [DS, DS * 128], BF16)

    dram["y"] = nc.dram_tensor("y", [T, C], F32, kind="ExternalOutput").ap()
    dram["s_x3"] = nc.dram_tensor("s_x3", [T, C], F32, kind="Internal").ap()
    dram["s_h"] = nc.dram_tensor("s_h", [T, C], BF16, kind="Internal").ap()
    dram["s_p5a"] = nc.dram_tensor("s_p5a", [T, C], F32, kind="Internal").ap()
    dram["s_p5b"] = nc.dram_tensor("s_p5b", [T, C], F32, kind="Internal").ap()
    dram["s_x5"] = nc.dram_tensor("s_x5", [T, C], F32, kind="Internal").ap()
    dram["b_wqkv"] = nc.dram_tensor("b_wqkv", [C, 3 * C], BF16, kind="Internal").ap()
    dram["b_inw_f"] = nc.dram_tensor("b_inw_f", [C, 4 * C], BF16, kind="Internal").ap()
    dram["b_inw_b"] = nc.dram_tensor("b_inw_b", [C, 4 * C], BF16, kind="Internal").ap()
    dram["b_outw_f"] = nc.dram_tensor("b_outw_f", [C, C], BF16, kind="Internal").ap()
    dram["b_outw_b"] = nc.dram_tensor("b_outw_b", [C, C], BF16, kind="Internal").ap()
    dram["b_weff_f"] = nc.dram_tensor("b_weff_f", [DI, C], BF16, kind="Internal").ap()
    dram["b_weff_b"] = nc.dram_tensor("b_weff_b", [DI, C], BF16, kind="Internal").ap()
    for tname in cx.taps:
        dram["tap_" + tname] = nc.dram_tensor(
            "tap_" + tname, [T, C], F32, kind="ExternalOutput").ap()
    cx.dram = dram

    with contextlib.ExitStack() as ctx:
        cx.ctx = ctx
        cx.wpool = Pool(nc, ctx, "wpool", 53 * 1024)
        cx.fpool = Pool(nc, ctx, "fpool", 50 * 1024)
        cx.bpool = Pool(nc, ctx, "bpool", 88 * 1024)
        cx.cpool = Pool(nc, ctx, "cpool", 16 * 1024)

        ps = []
        for i in range(5):
            ps.append(ctx.enter_context(nc.psum_tensor(f"psA{i}", [128, 512], F32)))
        cx.ps = ps
        cx.ps_tb = ctx.enter_context(nc.psum_tensor("ps_tb", [128, 2048], BF16))
        cx.ps_tf = ctx.enter_context(nc.psum_tensor("ps_tf", [128, 512], F32))
        cx.tb_par = 0
        cx.tf_par = 0

        cx.ld = Sem(nc, ctx, "ld", 16)
        cx.lw = Sem(nc, ctx, "lw", 16)
        cx.st = Sem(nc, ctx, "st", 16)
        cx.sv = Sem(nc, ctx, "sv")
        cx.ss = Sem(nc, ctx, "ss")
        cx.stn = Sem(nc, ctx, "stn")
        cx.sg = Sem(nc, ctx, "sg")

        cx.block = ctx.enter_context(nc.Block())
        _emit(cx)

    return nc


def _barrier(cx):
    nc = cx.nc
    cx.vpub()
    cx.spub()
    cx.tpub()
    cx.gpub()
    ticks = [(s, s.n) for s in (cx.sv, cx.ss, cx.stn, cx.sg, cx.ld, cx.st,
                                cx.lw)]
    for eng in (nc.sync, nc.vector, nc.scalar, nc.tensor, nc.gpsimd):
        for sem, n in ticks:
            eng.wait_ge(sem.h, n)
    cx.slots = {}


def _emit(cx):
    _load_consts(cx)
    _weight_prep(cx)
    _barrier(cx)
    last = {"ffn0a", "attn0", "ffn0b"} & set(cx.chain)
    p1_out = "y" if not ({"rwkva", "mamba", "rwkvb"} & set(cx.chain)) else "s_x3"
    _pass1(cx, out=p1_out)
    if "rwkva" in cx.chain:
        _barrier(cx)
        p2_out = "y" if "mamba" not in cx.chain else None
        _pass2(cx, out=p2_out)
    if "mamba" in cx.chain:
        _barrier(cx)
        _mamba_pass(cx, "f")
        _barrier(cx)
        _mamba_pass(cx, "b", out=("y" if "rwkvb" not in cx.chain else "s_x5"))
    if "rwkvb" in cx.chain:
        _barrier(cx)
        _pass5(cx)
    cx.nc.sync.wait_ge(cx.st.h, cx.st.n)



def _load_consts(cx):
    nc, d = cx.nc, cx.dram
    cc = {}
    full = [
        ("ident_bf", BF16, [128]), ("ident_f", F32, [128]),
        ("prev_bf", BF16, [128]), ("prev_f", F32, [128]),
        ("tri0", F32, [LS]), ("tri1", F32, [LS]),
        ("lnw_c", F32, [DC]), ("wsc_c", F32, [DC]), ("fbias_c", F32, [DC]),
        ("mur_c", F32, [DC]), ("muk_c", F32, [DC]),
        ("convw_f", F32, [DIC, 4]), ("convw_b", F32, [DIC, 4]),
        ("convb_c_f", F32, [DIC]), ("convb_c_b", F32, [DIC]),
        ("dtb_c_f", F32, [DIC]), ("dtb_c_b", F32, [DIC]),
        ("dd_c_f", F32, [DIC]), ("dd_c_b", F32, [DIC]),
        ("ones_col_bf", BF16, [1]),
    ]
    for name, dt, shape in full:
        buf = cx.cpool.alloc(dt, shape)
        cx.load(buf, d[name][:, :] if len(d[name].shape) == 2 else d[name][:, :, :])
        cc[name] = buf
    rows = [("ones_bf", BF16, 1, 128), ("ones_f", F32, 1, 128),
            ("sel_bf", BF16, DS, DS * 128)]
    for name, dt, np_, nf in rows:
        buf = cx.cpool.alloc(dt, [nf])
        cx.load(buf[0:np_], d[name][:, :])
        cc[name] = buf
    cx.cc = cc

    nc.scalar.wait_ge(cx.ld.h, cx.ld.n)
    cx.fpool.mark()
    alog_st = cx.fpool.alloc(F32, [DIC * DS])
    for dgn in ("f", "b"):
        tl = cx.load(alog_st, d[f"alog_{dgn}"][:, :])
        nc.scalar.wait_ge(cx.ld.h, tl)
        apos = cx.cpool.alloc(F32, [DIC * DS])
        op = nc.scalar.activation(out=apos, in_=alog_st, func=AF.Exp)
        nc.sync.wait_ge(cx.ss.h, cx.ss.inc(op))
        cc[f"apos_{dgn}"] = apos.rearrange("p (c s) -> p c s", c=DIC)
        ndtb = cx.cpool.alloc(F32, [DIC])
        nc.scalar.mul(out=ndtb, in_=cc[f"dtb_c_{dgn}"], mul=-1.0)
        cc[f"ndtb_{dgn}"] = ndtb
    cx.fpool.release()
    cx.spub()


def _build_bcast(cx, row_dram_name, pool, scale=None):
    nc, d, cc = cx.nc, cx.dram, cx.cc
    bc = pool.alloc(F32, [C])
    cx.fpool.mark()
    rowstage = cx.fpool.alloc(F32, [C])
    tl = cx.load(rowstage[0:1, :], d[row_dram_name][:, :])
    nc.tensor.wait_ge(cx.ld.h, tl)
    for j in range(C // 512):
        cx.slot_wait(nc.tensor, "b4")
        nc.tensor.matmul(cx.ps[4][:, :], cc["ones_f"][0:1, :],
                         rowstage[0:1, j * 512:(j + 1) * 512],
                         start=True, stop=True)
        tt = cx.tpub()
        nc.vector.wait_ge(cx.stn.h, tt)
        nc.vector.tensor_copy(out=bc[:, j * 512:(j + 1) * 512], in_=cx.ps[4][:, :])
        cx.slot_set("b4", cx.sv, cx.vpub())
    if scale is not None:
        nc.vector.tensor_scalar(out=bc, in0=bc, scalar1=scale, scalar2=None,
                                op0=OP.mult)
        cx.vpub()
    nc.sync.wait_ge(cx.stn.h, cx.stn.n)
    cx.fpool.release()
    return bc


def _weight_prep(cx):
    nc, d = cx.nc, cx.dram
    cx.fpool.mark()
    cx.bpool.mark()
    fbuf = cx.fpool.alloc(F32, [4096])
    bb = cx.bpool.alloc(BF16, [4096])
    jobs = [(d[f"inw_T_{g}"], d[f"b_inw_{g}"], C, 4 * C) for g in ("f", "b")]
    jobs += [(d["outw_T"][0:C, :], d["b_outw_f"], C, C),
             (d["outw_T"][C:2 * C, :], d["b_outw_b"], C, C),
             (d["w_qkv"], d["b_wqkv"], C, 3 * C)]
    for src, dst, rows, width in jobs:
        for r in range(rows // 128):
            t0 = cx.load(fbuf[:, 0:width], src[r * 128:(r + 1) * 128, :])
            nc.vector.wait_ge(cx.ld.h, t0)
            nc.vector.tensor_copy(out=bb[:, 0:width], in_=fbuf[:, 0:width])
            tick = cx.vpub()
            nc.sync.wait_ge(cx.sv.h, tick)
            t1 = cx.store(dst[r * 128:(r + 1) * 128, :], bb[:, 0:width])
            nc.sync.wait_ge(cx.st.h, t1)
    cx.fpool.release()
    cx.bpool.release()

    cx.wpool.mark()
    cx.fpool.mark()
    mo_f32 = cx.fpool.alloc(F32, [DI])
    mo_bf = cx.wpool.alloc(BF16, [DC, DI])
    ow_bf = cx.wpool.alloc(BF16, [DC, C])
    wout = cx.wpool.alloc(BF16, [C])
    for dgn in ("f", "b"):
        for kc in range(DC):
            nc.sync.wait_ge(cx.sv.h, cx.sv.n)
            t0 = cx.load(mo_f32, d[f"mout_{dgn}"][kc * 128:(kc + 1) * 128, :])
            nc.vector.wait_ge(cx.ld.h, t0)
            nc.vector.tensor_copy(out=mo_bf[:, kc], in_=mo_f32)
            cx.vpub()
            t1 = cx.load(ow_bf[:, kc], d[f"b_outw_{dgn}"][kc * 128:(kc + 1) * 128, :])
        tick = cx.vpub()
        nc.tensor.wait_ge(cx.sv.h, tick)
        nc.tensor.wait_ge(cx.ld.h, cx.ld.n)
        for mi in range(DIC):
            for nj in range(2):
                key = f"b{nj % 2}"
                cx.slot_wait(nc.tensor, key)
                pslot = cx.ps[nj % 2][:, :]
                for kc in range(DC):
                    nc.tensor.matmul(pslot, mo_bf[:, kc, mi * 128:(mi + 1) * 128],
                                     ow_bf[:, kc, nj * 512:(nj + 1) * 512],
                                     start=(kc == 0), stop=(kc == DC - 1))
                tt = cx.tpub()
                nc.scalar.wait_ge(cx.stn.h, tt)
                nc.scalar.copy(out=wout[:, nj * 512:(nj + 1) * 512], in_=pslot)
                cx.slot_set(key, cx.ss, cx.spub())
            nc.sync.wait_ge(cx.ss.h, cx.ss.n)
            ts = cx.store(d[f"b_weff_{dgn}"][mi * 128:(mi + 1) * 128, :], wout)
            nc.scalar.wait_ge(cx.st.h, ts)
    cx.wpool.release()
    cx.fpool.release()



def _ln_tm(cx, x_tm, xn_out, nsub, width=C, eps=EPS_LN, ret_ab=False):
    nc = cx.nc
    f = cx.fpool
    f.mark()
    s_sum = f.alloc(F32, [nsub])
    s_sq = f.alloc(F32, [nsub])
    junk = f.alloc(F32, [width])
    mean = f.alloc(F32, [nsub])
    nmrs = f.alloc(F32, [nsub])
    rstd = f.alloc(F32, [nsub])
    tmp = f.alloc(F32, [nsub])
    r2 = f.alloc(F32, [nsub])
    nc.vector.drain()
    nc.vector.wait_ge(cx.ss.h, cx.ss.n)
    nc.scalar.wait_ge(cx.sv.h, cx.sv.n)
    for i in range(nsub):
        nc.vector.reduce_sum(out=s_sum[:, i:i + 1], in_=x_tm[:, i], axis=AX.X)
        if i:
            nc.scalar.drain()
        nc.scalar.activation(out=junk, in_=x_tm[:, i], func=AF.Square,
                             accum_out=s_sq[:, i:i + 1])
    nc.vector.drain()
    tsq = cx.spub()
    nc.vector.wait_ge(cx.ss.h, tsq)
    inv_w = 1.0 / width
    nc.vector.tensor_scalar(out=mean, in0=s_sum, scalar1=inv_w, scalar2=None,
                            op0=OP.mult)
    nc.vector.drain()
    nc.vector.scalar_tensor_tensor(out=tmp, in0=mean, scalar=-1.0, in1=mean,
                                   op0=OP.mult, op1=OP.mult)
    nc.vector.drain()
    nc.vector.scalar_tensor_tensor(out=tmp, in0=s_sq, scalar=inv_w, in1=tmp,
                                   op0=OP.mult, op1=OP.add)
    nc.vector.drain()
    op = nc.vector.tensor_scalar(out=tmp, in0=tmp, scalar1=eps, scalar2=None,
                                 op0=OP.add)
    tick_v = cx.sv.inc(op)
    nc.scalar.wait_ge(cx.sv.h, tick_v)
    nc.scalar.activation(out=r2, in_=tmp, func=AF.Ln)
    nc.scalar.drain()
    op = nc.scalar.activation(out=rstd, in_=r2, func=AF.Exp, scale=-0.5)
    tick_s = cx.ss.inc(op)
    nc.vector.wait_ge(cx.ss.h, tick_s)
    nc.vector.scalar_tensor_tensor(out=nmrs, in0=mean, scalar=-1.0, in1=rstd,
                                   op0=OP.mult, op1=OP.mult)
    nc.vector.drain()
    tick_s = None
    if xn_out is not None:
        for i in range(nsub):
            nc.vector.tensor_scalar(out=xn_out[:, i],
                                    in0=x_tm[:, i], scalar1=rstd[:, i:i + 1],
                                    scalar2=nmrs[:, i:i + 1],
                                    op0=OP.mult, op1=OP.add)
        nc.vector.drain()
        tick_s = cx.vpub()
    if ret_ab:
        tick_v = cx.vpub()
        return tick_s, tick_v, rstd, nmrs
    f.release()
    return tick_s


def _tp128(cx, src_ap, consume, bf=True, perm=None, ident_slice=None,
           f2=False):
    nc = cx.nc
    idn = cx.cc[("prev_" if perm == "rev" else "ident_") + ("bf" if bf else "f")]
    if ident_slice is not None:
        idn = idn[ident_slice, ident_slice]
    out_w = src_ap.shape[0]
    if bf:
        par = cx.tb_par
        cx.tb_par ^= 1
        key = ("tb", par)
        psl = cx.ps_tb[:, par * 1024:par * 1024 + out_w]
    elif f2 and cx.tf_par:
        cx.tf_par ^= 1
        key = "b2"
        psl = cx.ps[2][:, 0:out_w]
    else:
        cx.tf_par ^= 1
        key = ("tf", 0)
        psl = cx.ps_tf[:, 0:out_w]
    cx.slot_wait(nc.tensor, key)
    tp = nc.tensor.transpose(psl[0:src_ap.shape[-1], :], src_ap, idn)
    tick = cx.stn.inc(tp)
    cx.slot_set(key, *consume(psl[0:src_ap.shape[-1], :], tick))


def _transpose_to_fm(cx, src_tm, dst_fm, nsub, src_bf=True, perm=None,
                     n_feat=C, sub_map=None):
    nc = cx.nc
    fc = n_feat // 128
    for i in range(nsub):
        di = sub_map[i] if sub_map else i
        for j in range(fc):

            def consume(psl, tick, di=di, j=j):
                nc.scalar.wait_ge(cx.stn.h, tick)
                cp = nc.scalar.copy(out=dst_fm[:, j, di * 128:(di + 1) * 128],
                                    in_=psl)
                return cx.ss, cx.ss.inc(cp)

            _tp128(cx, src_tm[:, i, j * 128:(j + 1) * 128], consume,
                   bf=src_bf, perm=perm)
    return cx.ss.n


def _fm_matmul(cx, w_view, rhs_fm, kcs, mi, pslot, pkey, extra=None):
    nc = cx.nc
    cx.slot_wait(nc.tensor, pkey)
    n = len(kcs)
    mm = None
    for ii, kc in enumerate(kcs):
        mm = nc.tensor.matmul(pslot, w_view[:, kc, mi * 128:(mi + 1) * 128],
                              rhs_fm[:, kc], start=(ii == 0),
                              stop=(extra is None and ii == n - 1))
    if extra is not None:
        lhsT, rhs = extra
        mm = nc.tensor.matmul(pslot, lhsT, rhs, start=False, stop=True)
    return cx.stn.inc(mm)


def _residual_add_fm_delta(cx, delta_fm, x_tm, nsub, scale, extra_tm=None):
    nc = cx.nc
    for i in range(nsub):
        for j in range(DC):

            def consume(psl, tick, i=i, j=j):
                nc.vector.wait_ge(cx.stn.h, tick)
                dst = x_tm[:, i, j * 128:(j + 1) * 128]
                op = nc.vector.scalar_tensor_tensor(out=dst, in0=psl,
                                                    scalar=scale, in1=dst,
                                                    op0=OP.mult, op1=OP.add)
                if extra_tm is not None:
                    nc.vector.drain()
                    op = nc.vector.tensor_tensor(
                        out=dst, in0=dst,
                        in1=extra_tm[:, i, j * 128:(j + 1) * 128], op=OP.add)
                return cx.sv, cx.sv.inc(op)

            _tp128(cx, delta_fm[:, j, i * 128:(i + 1) * 128], consume, bf=False)
    return cx.sv.n



def _quant_ffn(cx, bufs, x_tm, wq_bf):
    nc = cx.nc
    cc = cx.cc
    xn1, xq_tm, xq_fm, out_fm, scr = bufs
    amax, xsch, t2, frac = scr
    _ln_tm(cx, x_tm, xn1, NSUB)
    if cx.dbg < 2:
        return
    for i in range(NSUB):
        nc.vector.reduce_max(out=amax[:, i:i + 1], in_=xn1[:, i], axis=AX.X,
                             apply_absolute_value=True)
    nc.vector.drain()
    nc.vector.tensor_scalar(out=amax, in0=amax, scalar1=EPS_Q, scalar2=1.0 / 127.0,
                            op0=OP.max, op1=OP.mult)
    nc.vector.drain()
    nc.vector.tensor_scalar(out=xsch, in0=amax, scalar1=0.5, scalar2=None,
                            op0=OP.mult)
    nc.vector.drain()
    nc.vector.reciprocal(out=amax, in_=amax)
    nc.vector.drain()
    i32 = frac.bitcast(mybir.dt.int32)
    fb = frac
    for i in range(NSUB):
        nc.vector.tensor_scalar(out=t2, in0=xn1[:, i], scalar1=amax[:, i:i + 1],
                                scalar2=512.5, op0=OP.mult, op1=OP.add)
        nc.vector.drain()
        nc.vector.tensor_copy(out=i32, in_=t2)
        nc.vector.drain()
        nc.vector.tensor_copy(out=xn1[:, i], in_=i32)
        nc.vector.drain()
        nc.vector.tensor_tensor(out=fb, in0=xn1[:, i], in1=t2, op=OP.is_gt)
        nc.vector.drain()
        nc.vector.tensor_tensor(out=t2, in0=xn1[:, i], in1=fb, op=OP.subtract)
        nc.vector.drain()
        nc.vector.tensor_scalar(out=t2, in0=t2, scalar1=639.0,
                                scalar2=384.0, op0=OP.min, op1=OP.max)
        nc.vector.drain()
        nc.vector.tensor_scalar(out=xq_tm[:, i], in0=t2, scalar1=-512.0,
                                scalar2=None, op0=OP.add)
        nc.vector.drain()
    tick_q = cx.vpub()
    if cx.dbg < 3:
        return
    nc.tensor.wait_ge(cx.sv.h, tick_q)
    _transpose_to_fm(cx, xq_tm, xq_fm, NSUB, src_bf=True)
    if cx.dbg < 4:
        return
    nc.tensor.wait_ge(cx.ss.h, cx.ss.n)
    for mi in range(DC):
        pkey = f"b{mi % 2}"
        pslot = cx.ps[mi % 2][:, 0:QT]
        tt = _fm_matmul(cx, wq_bf, xq_fm, range(DC), mi, pslot, pkey)
        nc.vector.wait_ge(cx.stn.h, tt)
        op = nc.vector.tensor_scalar(out=out_fm[:, mi], in0=pslot,
                                     scalar1=cc["wsc_c"][:, mi:mi + 1],
                                     scalar2=None, op0=OP.mult)
        cx.slot_set(pkey, cx.sv, cx.sv.inc(op))
    nc.tensor.wait_ge(cx.sv.h, cx.sv.n)
    for i in range(NSUB):
        for j in range(DC):

            def consume(psl, tick, i=i, j=j):
                nc.vector.wait_ge(cx.stn.h, tick)
                dst = x_tm[:, i, j * 128:(j + 1) * 128]
                nc.vector.scalar_tensor_tensor(
                    out=frac[:, 0:128], in0=psl, scalar=xsch[:, i:i + 1],
                    in1=cx.cc["halfbias_bc"][:, j * 128:(j + 1) * 128],
                    op0=OP.mult, op1=OP.add)
                nc.vector.drain()
                op = nc.vector.tensor_tensor(out=dst, in0=dst,
                                             in1=frac[:, 0:128], op=OP.add)
                return cx.sv, cx.sv.inc(op)

            _tp128(cx, out_fm[:, j, i * 128:(i + 1) * 128], consume, bf=False)


def _attn(cx, bufs, x_tm, wqkv_get, wproj_bf):
    nc = cx.nc
    cc = cx.cc
    (xn2, xn2_fm, qf, kf, vf, sq, qh, kh, vtm, scsb, probs_all, pT, yatt,
     scr) = bufs
    row, rqb_q, rqb_k, mx, nb, sume, rsum, dproj = scr
    tick_ln = _ln_tm(cx, x_tm, xn2, NSUB)
    nc.tensor.wait_ge(cx.sv.h, tick_ln)
    tick = _transpose_to_fm(cx, xn2, xn2_fm, NSUB, src_bf=True)
    nc.tensor.wait_ge(cx.ss.h, tick)
    for half6 in range(6):
        wv, tl, wkey = wqkv_get(half6)
        nc.tensor.wait_ge(cx.ld.h, tl)
        for mi in range(4):
            mg = half6 * 4 + mi
            pkey = f"b{mg % 2}"
            pslot = cx.ps[mg % 2][:, 0:QT]
            tt = _fm_matmul(cx, wv, xn2_fm, range(DC), mi, pslot, pkey)
            nc.scalar.wait_ge(cx.stn.h, tt)
            dst = (qf, kf, vf)[mg // DC][:, mg % DC]
            cp = nc.scalar.copy(out=dst, in_=pslot)
            cx.slot_set(pkey, cx.ss, cx.ss.inc(cp))
        cx.slot_set(wkey, cx.stn, cx.stn.n)
    for name, src, rqb in (("q", qf, rqb_q), ("k", kf, rqb_k)):
        nc.scalar.wait_ge(cx.ss.h, cx.ss.n)
        for j in range(DC):
            nc.scalar.activation(out=sq[:, j], in_=src[:, j], func=AF.Square)
        ts = cx.spub()
        nc.tensor.wait_ge(cx.ss.h, ts)
        cx.slot_wait(nc.tensor, "b2")
        for j in range(DC):
            nc.tensor.matmul(cx.ps[2][0:1, 0:QT], cc["ones_col_bf"][:, 0:1],
                             sq[:, j], start=(j == 0), stop=(j == DC - 1))
        tt = cx.tpub()
        f64 = 64.0 if name == "q" else 1.0
        nc.vector.wait_ge(cx.stn.h, tt)
        op = nc.vector.tensor_scalar(out=row[0:1, :], in0=cx.ps[2][0:1, 0:QT],
                                     scalar1=f64 / C, scalar2=f64 * EPS_RMS,
                                     op0=OP.mult, op1=OP.add)
        cx.slot_set("b2", cx.sv, cx.sv.inc(op))
        nc.scalar.wait_ge(cx.sv.h, cx.sv.n)
        nc.scalar.activation(out=row[0:1, :], in_=row[0:1, :], func=AF.Ln)
        nc.scalar.drain()
        op = nc.scalar.activation(out=row[0:1, :], in_=row[0:1, :],
                                  func=AF.Exp, scale=-0.5)
        ts = cx.ss.inc(op)
        nc.tensor.wait_ge(cx.ss.h, ts)
        cx.slot_wait(nc.tensor, "b4")
        nc.tensor.matmul(cx.ps[4][:, 0:QT], cc["ones_f"][0:1, :], row[0:1, 0:QT],
                         start=True, stop=True)
        tt = cx.tpub()
        nc.scalar.wait_ge(cx.stn.h, tt)
        cp = nc.scalar.copy(out=rqb, in_=cx.ps[4][:, 0:QT])
        cx.slot_set("b4", cx.ss, cx.ss.inc(cp))
    nc.vector.wait_ge(cx.ss.h, cx.ss.n)
    for (src, dst, rqb) in ((qf, qh, rqb_q), (kf, kh, rqb_k)):
        for j in range(DC):
            nc.vector.scalar_tensor_tensor(out=dst[:, j], in0=src[:, j],
                                           scalar=cc["lnw_c"][:, j:j + 1],
                                           in1=rqb, op0=OP.mult, op1=OP.mult)
        nc.vector.drain()
    tick_qh = cx.vpub()
    nc.tensor.wait_ge(cx.ss.h, cx.ss.n)
    for h in range(NH):
        ch, r0 = h // 2, (h % 2) * 64
        for kc in range(NSUB):

            def consume(psl, tick, h=h, kc=kc):
                nc.scalar.wait_ge(cx.stn.h, tick)
                cp = nc.scalar.copy(out=vtm[:, h, kc * 64:(kc + 1) * 64],
                                    in_=psl)
                return cx.ss, cx.ss.inc(cp)

            _tp128(cx, vf[r0:r0 + 64, ch, kc * 128:(kc + 1) * 128], consume,
                   bf=True, ident_slice=slice(r0, r0 + 64))
    tick_vtm = cx.ss.n
    nc.tensor.wait_ge(cx.sv.h, tick_qh)
    nc.tensor.wait_ge(cx.ss.h, tick_vtm)
    cx.slot_set("b2", cx.sv, cx.sv.n)
    cx.slot_set("b3", cx.sv, cx.sv.n)
    cx.slot_set("b4", cx.ss, cx.ss.n)
    jobs = [(h, qs) for h in range(NH) for qs in range(NSUB)]
    for j, (h, qs) in enumerate(jobs):
        ch, r0 = h // 2, (h % 2) * 64
        jp = j % 4
        pkey = f"b{2 + (j % 2)}"
        cx.slot_wait(nc.tensor, pkey)
        pslot = cx.ps[2 + (j % 2)][:, 0:QT]
        nc.tensor.matmul(pslot, qh[r0:r0 + 64, ch, qs * 128:(qs + 1) * 128],
                         kh[r0:r0 + 64, ch, 0:QT], start=True, stop=True)
        tt = cx.tpub()
        nc.vector.wait_ge(cx.stn.h, tt)
        cx.slot_wait(nc.vector, ("scsb", jp))
        op = nc.vector.tensor_tensor(out=scsb[:, jp], in0=pslot,
                                     in1=cc["tri0" if qs == 0 else "tri1"],
                                     op=OP.add)
        cx.slot_set(pkey, cx.sv, cx.sv.inc(op))
        nc.vector.drain()
        nc.vector.reduce_max(out=mx[:, j:j + 1], in_=scsb[:, jp], axis=AX.X)
        nc.vector.drain()
        op = nc.vector.tensor_scalar(out=nb[:, j:j + 1], in0=mx[:, j:j + 1],
                                     scalar1=-1.0, scalar2=None, op0=OP.mult)
        tick_v = cx.sv.inc(op)
        nc.scalar.wait_ge(cx.sv.h, tick_v)
        op = nc.scalar.activation(out=probs_all[:, h, qs], in_=scsb[:, jp],
                                  func=AF.Exp, scale=1.0, bias=nb[:, j:j + 1],
                                  accum_out=sume[:, j:j + 1])
        ts = cx.ss.inc(op)
        cx.slot_set(("scsb", jp), cx.ss, ts)
        nc.vector.wait_ge(cx.ss.h, ts)
        nc.vector.reciprocal(out=rsum[:, j:j + 1], in_=sume[:, j:j + 1])
        nc.vector.drain()
        op = nc.vector.tensor_scalar(out=probs_all[:, h, qs],
                                     in0=probs_all[:, h, qs],
                                     scalar1=rsum[:, j:j + 1], scalar2=None,
                                     op0=OP.mult)
        cx.sv.inc(op)
    tick_p = cx.sv.n
    nc.tensor.wait_ge(cx.sv.h, tick_p)
    for h in range(NH):
        ch, r0 = h // 2, (h % 2) * 64
        for qs in range(NSUB):
            for kc in range(NSUB):

                def consume(psl, tick, kc=kc, qs=qs):
                    nc.scalar.wait_ge(cx.stn.h, tick)
                    cp = nc.scalar.copy(out=pT[:, kc, qs * 128:(qs + 1) * 128],
                                        in_=psl)
                    return cx.ss, cx.ss.inc(cp)

                _tp128(cx, probs_all[:, h, qs, kc * 128:(kc + 1) * 128],
                       consume, bf=True)
        ts = cx.ss.n
        nc.tensor.wait_ge(cx.ss.h, ts)
        pkey = "b4"
        cx.slot_wait(nc.tensor, pkey)
        pslot = cx.ps[4][0:64, 0:QT]
        mm = None
        for kc in range(NSUB):
            mm = nc.tensor.matmul(pslot, vtm[:, h, kc * 64:(kc + 1) * 64],
                                  pT[:, kc, 0:QT], start=(kc == 0),
                                  stop=(kc == NSUB - 1))
        tt = cx.stn.inc(mm)
        nc.scalar.wait_ge(cx.stn.h, tt)
        cp = nc.scalar.copy(out=yatt[r0:r0 + 64, ch], in_=pslot)
        cx.slot_set(pkey, cx.ss, cx.ss.inc(cp))
    nc.tensor.wait_ge(cx.ss.h, cx.ss.n)
    for mi in range(DC):
        pkey = f"b{mi % 2}"
        pslot = cx.ps[mi % 2][:, 0:QT]
        tt = _fm_matmul(cx, wproj_bf, yatt, range(DC), mi, pslot, pkey)
        nc.scalar.wait_ge(cx.stn.h, tt)
        cp = nc.scalar.copy(out=dproj[:, mi], in_=pslot)
        cx.slot_set(pkey, cx.ss, cx.ss.inc(cp))
    nc.tensor.wait_ge(cx.ss.h, cx.ss.n)
    _residual_add_fm_delta(cx, dproj, x_tm, NSUB, 1.0)




def _pass1(cx, out="s_x3"):
    nc, d, cc = cx.nc, cx.dram, cx.cc
    NT = cx.NT
    cx.wpool.mark()
    cx.fpool.mark()
    cx.bpool.mark()
    wq_bf = cx.wpool.alloc(BF16, [DC, C])
    wproj_bf = cx.wpool.alloc(BF16, [DC, C])
    wqkv_sl = [cx.wpool.alloc(BF16, [DC, 512]) for _ in range(2)]
    cx.cc["halfbias_bc"] = _build_bcast(cx, "fbias_r", cx.wpool, scale=0.5)
    stage = cx.fpool.alloc(F32, [C])
    stage_i8 = cx.bpool.alloc(I8, [C])
    for kc in range(DC):
        t0 = cx.load(stage_i8, d["wq_T"][kc * 128:(kc + 1) * 128, :])
        nc.vector.wait_ge(cx.ld.h, t0)
        nc.vector.tensor_copy(out=wq_bf[:, kc], in_=stage_i8)
        nc.vector.drain()
        t0 = cx.load(stage, d["w_proj"][kc * 128:(kc + 1) * 128, :])
        nc.vector.wait_ge(cx.ld.h, t0)
        nc.vector.tensor_copy(out=wproj_bf[:, kc], in_=stage)
        tick_w = cx.vpub()
        nc.sync.wait_ge(cx.sv.h, tick_w)

    def wqkv_get(half6):
        slot = half6 % 2
        key = ("wqkv", slot)
        cx.slot_wait(nc.sync, key)
        tl = cx.load(wqkv_sl[slot],
                     d["b_wqkv"][:, half6 * 512:(half6 + 1) * 512]
                     .rearrange("(a p) c -> p a c", p=128))
        return wqkv_sl[slot], tl, key

    xa = [cx.fpool.alloc(F32, [NSUB, C])]
    xn1 = cx.fpool.alloc(F32, [NSUB, C])
    xq_tm = cx.bpool.alloc(BF16, [NSUB, C])
    xq_fm = cx.bpool.alloc(BF16, [DC, QT])
    out_fm = cx.fpool.alloc(F32, [DC, QT])
    scrq = (cx.fpool.alloc(F32, [NSUB]), cx.fpool.alloc(F32, [NSUB]),
            cx.fpool.alloc(F32, [C]), cx.fpool.alloc(F32, [C]))
    xn2 = cx.bpool.alloc(BF16, [NSUB, C])
    xn2_fm = cx.bpool.alloc(BF16, [DC, QT])
    qf = cx.bpool.alloc(BF16, [DC, QT])
    kf = cx.bpool.alloc(BF16, [DC, QT])
    vf = cx.bpool.alloc(BF16, [DC, QT])
    sq = xn2_fm
    vtm = cx.bpool.alloc(BF16, [NH, NSUB * 64])
    scsb = cx.fpool.alloc(F32, [4, QT])
    probs_all = cx.bpool.alloc(BF16, [NH, NSUB, QT])
    pT = cx.bpool.alloc(BF16, [NSUB, QT])
    yatt = cx.bpool.alloc(BF16, [DC, QT])
    NJ = NH * NSUB
    scra = (cx.fpool.alloc(F32, [QT]), cx.fpool.alloc(F32, [QT]),
            cx.fpool.alloc(F32, [QT]), cx.fpool.alloc(F32, [NJ]),
            cx.fpool.alloc(F32, [NJ]), cx.fpool.alloc(F32, [NJ]),
            cx.fpool.alloc(F32, [NJ]), out_fm)
    qb1 = (xn1, xq_tm, xq_fm, out_fm, scrq)
    ab = (xn2, xn2_fm, qf, kf, vf, sq, qf, kf, vtm, scsb, probs_all, pT, yatt,
          scra)

    tile_end = {}
    for k in range(NT):
        p = 0
        if k >= 1:
            for sem, tick in tile_end[k - 1]:
                for eng in (nc.vector, nc.scalar, nc.tensor):
                    eng.wait_ge(sem.h, tick)
        if k >= 1:
            for sem, tick in tile_end[k - 1]:
                nc.sync.wait_ge(sem.h, tick)
        tk = cx.load(xa[p], d["x0"][k * QT:(k + 1) * QT, :]
                     .rearrange("(a p) c -> p a c", p=128))
        for eng in (nc.vector, nc.scalar, nc.tensor):
            eng.wait_ge(cx.ld.h, tk)
        if "ffn0a" in cx.chain:
            _quant_ffn(cx, qb1, xa[p], wq_bf)
        if "attn0" in cx.chain:
            _attn(cx, ab, xa[p], wqkv_get, wproj_bf)
        if "ffn0b" in cx.chain:
            _quant_ffn(cx, qb1, xa[p], wq_bf)
        tick_v = cx.vpub()
        nc.sync.wait_ge(cx.sv.h, tick_v)
        cx.store(d[out][k * QT:(k + 1) * QT, :]
                 .rearrange("(a p) c -> p a c", p=128), xa[p])
        tile_end[k] = [(cx.sv, cx.sv.n), (cx.ss, cx.ss.n), (cx.stn, cx.stn.n),
                       (cx.st, cx.st.n)]
    cx.wpool.release()
    cx.fpool.release()
    cx.bpool.release()



def _rwkv_delta(cx, bufs, x_tm, weights, carry, first):
    nc, cc = cx.nc, cx.cc
    (xn4, xf_buf, xr, rbf, kbuf, dproj) = bufs
    wr_bf, wk_bf, wv_bf, omr, omk = weights
    tick_ln = _ln_tm(cx, x_tm, xn4, NSUB)
    nc.vector.wait_ge(cx.ss.h, cx.ss.n)
    if first:
        nc.vector.memset(xf_buf[:, :, 0:1], 0.0)
    else:
        nc.vector.tensor_copy(out=xf_buf[:, :, 0:1], in_=carry)
    tcar = cx.vpub()
    nc.tensor.wait_ge(cx.sv.h, tick_ln)
    tick = _transpose_to_fm(cx, xn4, xf_buf.rearrange("p a b -> p a b")
                            [:, :, 1:QT + 1], NSUB)
    nc.vector.wait_ge(cx.ss.h, tick)
    nc.vector.tensor_copy(out=carry, in_=xf_buf[:, :, QT:QT + 1])
    nc.vector.drain()
    for phase in range(2):
        mu = cc["mur_c"] if phase == 0 else cc["muk_c"]
        om = omr if phase == 0 else omk
        if phase == 1:
            nc.vector.wait_ge(cx.stn.h, cx.stn.n)
        for j in range(DC):
            nc.vector.tensor_scalar(out=xr[:, j], in0=xf_buf[:, j, 0:QT],
                                    scalar1=om[:, j:j + 1], scalar2=None,
                                    op0=OP.mult)
            nc.vector.drain()
            nc.vector.scalar_tensor_tensor(out=xr[:, j],
                                           in0=xf_buf[:, j, 1:QT + 1],
                                           scalar=mu[:, j:j + 1], in1=xr[:, j],
                                           op0=OP.mult, op1=OP.add)
            nc.vector.drain()
        tick_mix = cx.vpub()
        nc.tensor.wait_ge(cx.sv.h, tick_mix)
        w = wr_bf if phase == 0 else wk_bf
        for mi in range(DC):
            pkey = f"b{mi % 2}"
            pslot = cx.ps[mi % 2][:, 0:QT]
            tt = _fm_matmul(cx, w, xr, range(DC), mi, pslot, pkey)
            nc.scalar.wait_ge(cx.stn.h, tt)
            if phase == 0:
                cp = nc.scalar.activation(out=rbf[:, mi], in_=pslot,
                                          func=AF.Sigmoid)
            else:
                cp = nc.scalar.activation(out=kbuf[:, mi], in_=pslot,
                                          func=AF.Relu)
            cx.slot_set(pkey, cx.ss, cx.ss.inc(cp))
    nc.vector.wait_ge(cx.ss.h, cx.ss.n)
    for j in range(DC):
        nc.vector.tensor_tensor(out=kbuf[:, j], in0=kbuf[:, j], in1=kbuf[:, j],
                                op=OP.mult)
    nc.vector.drain()
    tk2 = cx.vpub()
    nc.tensor.wait_ge(cx.sv.h, tk2)
    for mi in range(DC):
        pkey = f"b{mi % 2}"
        pslot = cx.ps[mi % 2][:, 0:QT]
        tt = _fm_matmul(cx, wv_bf, kbuf, range(DC), mi, pslot, pkey)
        nc.vector.wait_ge(cx.stn.h, tt)
        op = nc.vector.tensor_tensor(out=dproj[:, mi], in0=rbf[:, mi],
                                     in1=pslot, op=OP.mult)
        cx.slot_set(pkey, cx.sv, cx.sv.inc(op))
    nc.tensor.wait_ge(cx.sv.h, cx.sv.n)


def _rwkv_weights(cx):
    nc, d = cx.nc, cx.dram
    wr_bf = cx.wpool.alloc(BF16, [DC, C])
    wk_bf = cx.wpool.alloc(BF16, [DC, C])
    wv_bf = cx.wpool.alloc(BF16, [DC, C])
    stage = cx.fpool.alloc(F32, [C])
    for name, dst in (("wr_T", wr_bf), ("wk_T", wk_bf), ("wv_T", wv_bf)):
        for kc in range(DC):
            t0 = cx.load(stage, d[name][kc * 128:(kc + 1) * 128, :])
            nc.vector.wait_ge(cx.ld.h, t0)
            nc.vector.tensor_copy(out=dst[:, kc], in_=stage)
            tw = cx.vpub()
            nc.sync.wait_ge(cx.sv.h, tw)
    omr = cx.wpool.alloc(F32, [DC])
    omk = cx.wpool.alloc(F32, [DC])
    nc.vector.wait_ge(cx.ld.h, cx.ld.n)
    nc.vector.tensor_scalar(out=omr, in0=cx.cc["mur_c"], scalar1=-1.0,
                            scalar2=1.0, op0=OP.mult, op1=OP.add)
    nc.vector.drain()
    nc.vector.tensor_scalar(out=omk, in0=cx.cc["muk_c"], scalar1=-1.0,
                            scalar2=1.0, op0=OP.mult, op1=OP.add)
    cx.vpub()
    return (wr_bf, wk_bf, wv_bf, omr, omk)


def _pass2(cx, out=None):
    nc, d, cc = cx.nc, cx.dram, cx.cc
    NT = cx.NT
    cx.wpool.mark()
    cx.fpool.mark()
    cx.bpool.mark()
    weights = _rwkv_weights(cx)
    cc["gamma_bc"] = _build_bcast(cx, "gamma_r", cx.fpool)
    cc["beta_bc"] = _build_bcast(cx, "beta_r", cx.fpool)
    cc["outb_bc"] = _build_bcast(cx, "outb_r", cx.wpool)

    xa = cx.fpool.alloc(F32, [NSUB, C])
    a_tm = cx.fpool.alloc(F32, [NSUB, C])
    h1 = cx.fpool.alloc(F32, [NSUB, C])
    dproj = cx.fpool.alloc(F32, [DC, QT])
    xn4 = cx.bpool.alloc(BF16, [NSUB, C])
    xf_buf = cx.bpool.alloc(BF16, [DC, QT + 1])
    xr = cx.bpool.alloc(BF16, [DC, QT])
    rbf = cx.bpool.alloc(BF16, [DC, QT])
    kbuf = cx.bpool.alloc(BF16, [DC, QT])
    h_tm = cx.bpool.alloc(BF16, [NSUB, C])
    carry = cx.bpool.alloc(BF16, [DC, 1])
    rw_bufs = (xn4, xf_buf, xr, rbf, kbuf, dproj)

    tile_end = {}
    for k in range(NT):
        if k >= 1:
            for sem, tick in tile_end[k - 1]:
                for eng in (nc.vector, nc.scalar, nc.tensor):
                    eng.wait_ge(sem.h, tick)
                nc.sync.wait_ge(sem.h, tick)
        tk = cx.load(xa, d["s_x3"][k * QT:(k + 1) * QT, :]
                     .rearrange("(a p) c -> p a c", p=128))
        for eng in (nc.vector, nc.scalar, nc.tensor):
            eng.wait_ge(cx.ld.h, tk)
        _rwkv_delta(cx, rw_bufs, xa, weights, carry, first=(k == 0))
        _residual_add_fm_delta(cx, dproj, xa, NSUB, 0.5)
        _ln_tm(cx, xa, a_tm, NSUB)
        _ln_tm(cx, a_tm, h1, NSUB)
        for i in range(NSUB):
            nc.vector.tensor_tensor(out=h1[:, i], in0=h1[:, i],
                                    in1=cc["gamma_bc"], op=OP.mult)
            nc.vector.drain()
            nc.vector.tensor_tensor(out=h_tm[:, i], in0=h1[:, i],
                                    in1=cc["beta_bc"], op=OP.add)
            nc.vector.drain()
        for i in range(NSUB):
            nc.vector.tensor_tensor(out=h1[:, i], in0=xa[:, i], in1=a_tm[:, i],
                                    op=OP.add)
            nc.vector.drain()
            nc.vector.tensor_tensor(out=h1[:, i], in0=h1[:, i],
                                    in1=cc["outb_bc"], op=OP.add)
            nc.vector.drain()
        tick_v = cx.vpub()
        nc.sync.wait_ge(cx.sv.h, tick_v)
        cx.store(d["s_h"][k * QT:(k + 1) * QT, :]
                 .rearrange("(a p) c -> p a c", p=128), h_tm)
        cx.store(d["s_p5a"][k * QT:(k + 1) * QT, :]
                 .rearrange("(a p) c -> p a c", p=128), h1)
        if out:
            cx.store(d[out][k * QT:(k + 1) * QT, :]
                     .rearrange("(a p) c -> p a c", p=128), xa)
        tile_end[k] = [(cx.sv, cx.sv.n), (cx.ss, cx.ss.n), (cx.stn, cx.stn.n),
                       (cx.st, cx.st.n)]
    cx.wpool.release()
    cx.fpool.release()
    cx.bpool.release()


def _mamba_pass(cx, dgn, out=None):
    nc, d, cc = cx.nc, cx.dram, cx.cc
    NT = cx.NT
    rev = dgn == "b"
    src_p5 = "s_p5a" if dgn == "f" else "s_p5b"
    dst_p5 = "s_p5b" if dgn == "f" else (out or "s_x5")
    cx.wpool.mark()
    cx.fpool.mark()
    cx.bpool.mark()
    inw_sl = [cx.wpool.alloc(BF16, [DC, 512]) for _ in range(2)]
    weff_sl = [cx.wpool.alloc(BF16, [DIC, 256]) for _ in range(2)]
    xproj_bf = cx.wpool.alloc(BF16, [DIC, 128])
    dtw_bf = cx.wpool.alloc(BF16, [DI])
    cx.fpool.mark()
    stage = cx.fpool.alloc(F32, [DI])
    for kc in range(DIC):
        t0 = cx.load(stage[:, 0:128], d[f"xproj_T_{dgn}"][kc * 128:(kc + 1) * 128, :])
        nc.vector.wait_ge(cx.ld.h, t0)
        nc.vector.tensor_copy(out=xproj_bf[:, kc], in_=stage[:, 0:128])
        tw = cx.vpub()
        nc.sync.wait_ge(cx.sv.h, tw)
    t0 = cx.load(stage[0:DTR, :], d[f"dtw_T_{dgn}"][:, :])
    nc.vector.wait_ge(cx.ld.h, t0)
    nc.vector.tensor_copy(out=dtw_bf[0:DTR, :], in_=stage[0:DTR, :])
    cx.vpub()
    nc.sync.wait_ge(cx.sv.h, cx.sv.n)
    cx.fpool.release()

    h_tm = [cx.bpool.alloc(BF16, [NSUB, C]) for _ in range(2)]
    h_fm = cx.bpool.alloc(BF16, [DC, QT])
    xi_buf = cx.bpool.alloc(BF16, [DIC, QT + 3])
    hall = cx.bpool.alloc(BF16, [DS, QT])
    zs_fm = [cx.bpool.alloc(BF16, [DIC, QT]) for _ in range(2)]
    xc_fm = [cx.bpool.alloc(BF16, [DIC, QT]) for _ in range(2)]
    rows_bf = [cx.bpool.alloc(BF16, [QT]) for _ in range(2)]
    bcr = [cx.bpool.alloc(BF16, [2, QT]) for _ in range(2)]
    du_bf = cx.bpool.alloc(BF16, [QT])
    bc_all = cx.bpool.alloc(BF16, [DS, 2, QT])
    w_half = [cx.bpool.alloc(BF16, [DS // 2, QT]) for _ in range(2)]
    carry3 = cx.bpool.alloc(BF16, [DIC, 3])

    dn_fm = [cx.fpool.alloc(F32, [QT]) for _ in range(2)]
    e_half = [cx.fpool.alloc(F32, [DS // 2, QT]) for _ in range(2)]
    sgbuf = cx.fpool.alloc(F32, [QT])
    convacc = [cx.fpool.alloc(F32, [QT]) for _ in range(DIC)]
    cgtmp = cx.fpool.alloc(F32, [QT])
    ys_f = cx.fpool.alloc(F32, [QT])
    fix8 = cx.fpool.alloc(F32, [DS // 2])
    xdbl = cx.fpool.alloc(F32, [QT])
    p5_tm = cx.fpool.alloc(F32, [NSUB, C])
    state = cx.fpool.alloc(F32, [DIC, DS])
    pevac = [cx.fpool.alloc(F32, [QT]) for _ in range(2)]

    nc.vector.wait_ge(cx.ld.h, cx.ld.n)
    nc.vector.memset(state.rearrange("p a b -> p (a b)"), 0.0)
    nc.vector.drain()
    nc.vector.memset(carry3.rearrange("p a b -> p (a b)"), 0.0)
    tick0 = cx.vpub()
    nc.sync.wait_ge(cx.sv.h, tick0)
    nc.gpsimd.wait_ge(cx.sv.h, tick0)

    sub_map = list(range(NSUB - 1, -1, -1)) if rev else None
    perm = "rev" if rev else None
    cw = cc[f"convw_{dgn}"]

    def stream_w(dram_ap, slot_bufs, qi, key_pfx):
        slot = qi % 2
        key = (key_pfx, slot)
        cx.slot_wait(nc.sync, key)
        tl = cx.load(slot_bufs[slot], dram_ap)
        return slot_bufs[slot], tl, key

    def stage_a_front_gen(kk):
        k = NT - 1 - kk if rev else kk
        par = kk % 2
        cx.slot_wait(nc.sync, ("htm", par))
        tk = cx.load(h_tm[par], d["s_h"][k * QT:(k + 1) * QT, :]
                     .rearrange("(a p) c -> p a c", p=128))
        yield
        nc.tensor.wait_ge(cx.ld.h, tk)
        tfm = _transpose_to_fm(cx, h_tm[par], h_fm, NSUB, src_bf=True,
                               perm=perm, sub_map=sub_map)
        cx.slot_set(("htm", par), cx.stn, cx.stn.n)
        cx.slot_wait(nc.scalar, "xi")
        cx.slot_wait(nc.gpsimd, "xi")
        gp = nc.gpsimd.tensor_copy(out=xi_buf[:, :, 0:3], in_=carry3[:, :, :])
        tcar = cx.sg.inc(gp)
        nc.scalar.wait_ge(cx.sg.h, tcar)
        nc.tensor.wait_ge(cx.ss.h, tfm)
        conv_tick = {}
        yield
        for eg in range(8):
            wv, tl, wkey = stream_w(
                d[f"b_inw_{dgn}"][:, eg * 512:(eg + 1) * 512]
                .rearrange("(a p) c -> p a c", p=128), inw_sl, eg, "inw")
            nc.tensor.wait_ge(cx.ld.h, tl)
            for mi in range(4):
                mg = eg * 4 + mi
                pkey = f"b{mg % 2}"
                pslot = cx.ps[mg % 2][:, 0:QT]
                tt = _fm_matmul(cx, wv, h_fm, range(DC), mi, pslot, pkey)
                nc.scalar.wait_ge(cx.stn.h, tt)
                if mg < DIC:
                    cp = nc.scalar.copy(out=xi_buf[:, mg, 3:QT + 3], in_=pslot)
                    te = cx.ss.inc(cp)
                    cx.slot_set(pkey, cx.ss, te)
                    c = mg
                    ca = convacc[c]
                    nc.gpsimd.wait_ge(cx.ss.h, te)
                    if c in cx.mp_silu:
                        nc.gpsimd.wait_ge(cx.ss.h, cx.mp_silu[c])
                    nc.gpsimd.tensor_tensor(
                        out=ca, in0=xi_buf[:, c, 0:QT],
                        in1=cw[:, c, 0:1].broadcast_to([128, QT]), op=OP.mult)
                    nc.gpsimd.drain()
                    for j in range(1, 4):
                        nc.gpsimd.tensor_tensor(
                            out=cgtmp, in0=xi_buf[:, c, j:j + QT],
                            in1=cw[:, c, j:j + 1].broadcast_to([128, QT]),
                            op=OP.mult)
                        nc.gpsimd.drain()
                        nc.gpsimd.tensor_tensor(out=ca, in0=ca, in1=cgtmp,
                                                op=OP.add)
                        nc.gpsimd.drain()
                    conv_tick[c] = cx.sg.inc(nc.gpsimd.drain())
                else:
                    cp = nc.scalar.activation(out=zs_fm[par][:, mg - DIC],
                                              in_=pslot, func=AF.Silu)
                    cx.slot_set(pkey, cx.ss, cx.ss.inc(cp))
            cx.slot_set(wkey, cx.stn, cx.stn.n)
            yield
        gp = nc.gpsimd.tensor_copy(out=carry3[:, :, :],
                                   in_=xi_buf[:, :, QT:QT + 3])
        tcar2 = cx.sg.inc(gp)
        cx.slot_set("xi", cx.sg, tcar2)
        cx.mp_conv = conv_tick

    def stage_a_back(kk):
        par = kk % 2
        silu_tick = {}
        for c in range(DIC):
            nc.scalar.wait_ge(cx.sg.h, cx.mp_conv[c])
            cp2 = nc.scalar.activation(
                out=xc_fm[par][:, c], in_=convacc[c], func=AF.Silu,
                bias=cc[f"convb_c_{dgn}"][:, c:c + 1])
            silu_tick[c] = cx.ss.inc(cp2)
            cx.mp_silu[c] = silu_tick[c]
        cx.slot_wait(nc.tensor, "b2")
        mm = None
        for kc in range(DIC):
            nc.tensor.wait_ge(cx.ss.h, silu_tick[kc])
            mm = nc.tensor.matmul(cx.ps[2][:, 0:QT], xproj_bf[:, kc],
                                  xc_fm[par][:, kc], start=(kc == 0),
                                  stop=(kc == DIC - 1))
        tt = cx.stn.inc(mm)
        nc.scalar.wait_ge(cx.stn.h, tt)
        cx.slot_wait(nc.scalar, "xdbl")
        cp = nc.scalar.copy(out=xdbl[:, :], in_=cx.ps[2][:, 0:QT])
        tsd = cx.ss.inc(cp)
        cx.slot_set("b2", cx.ss, tsd)
        nc.gpsimd.wait_ge(cx.ss.h, tsd)
        cx.slot_wait(nc.gpsimd, ("rows", par))
        nc.gpsimd.tensor_copy(out=rows_bf[par][0:DTR, :], in_=xdbl[0:DTR, :])
        nc.gpsimd.drain()
        nc.gpsimd.tensor_scalar(out=bcr[par][0:DS, 0, :],
                                in0=xdbl[DTR:DTR + DS, :], scalar1=-1.0,
                                scalar2=None, op0=OP.mult)
        nc.gpsimd.drain()
        gp = nc.gpsimd.tensor_copy(out=bcr[par][0:DS, 1, :],
                                   in_=xdbl[96:96 + DS, :])
        trows = cx.sg.inc(gp)
        cx.slot_set("xdbl", cx.sg, trows)
        nc.tensor.wait_ge(cx.sg.h, trows)
        nc.scalar.wait_ge(cx.sv.h, cx.mp_scan_tick)
        bcflat = bcr[par].rearrange("p a b -> p (a b)")
        for s in range(DS):
            key = f"b{3 + (s % 2)}"
            cx.slot_wait(nc.tensor, key)
            mm = nc.tensor.matmul(cx.ps[3 + (s % 2)][:, 0:512],
                                  cc["sel_bf"][0:DS, s * 128:(s + 1) * 128],
                                  bcflat[0:DS, :], start=True, stop=True)
            tt = cx.stn.inc(mm)
            nc.scalar.wait_ge(cx.stn.h, tt)
            op = nc.scalar.copy(
                out=bc_all[:, s, :, :].rearrange("p a b -> p (a b)"),
                in_=cx.ps[3 + (s % 2)][:, 0:512])
            cx.slot_set(key, cx.ss, cx.ss.inc(op))
        cx.mp_trows[kk] = trows

    def stage_b1(kk, gen):
        k = NT - 1 - kk if rev else kk
        par = kk % 2

        def drive():
            if gen is not None:
                try:
                    next(gen)
                except StopIteration:
                    pass

        nc.sync.wait_ge(cx.st.h, cx.st.n)
        cx.mp_tkp = cx.load(p5_tm, d[src_p5][k * QT:(k + 1) * QT, :]
                            .rearrange("(a p) c -> p a c", p=128))
        drive()
        SH = DS // 2
        nc.tensor.wait_ge(cx.sg.h, cx.mp_trows[kk])
        for c in range(DIC):
            dnb = dn_fm[c % 2]
            pkey = f"b{c % 2}"
            pslot = cx.ps[c % 2][:, 0:QT]
            cx.slot_wait(nc.tensor, pkey)
            mm = nc.tensor.matmul(pslot, dtw_bf[0:DTR, c * 128:(c + 1) * 128],
                                  rows_bf[par][0:DTR, 0:QT], start=True,
                                  stop=True)
            tt = cx.stn.inc(mm)
            nc.scalar.wait_ge(cx.stn.h, tt)
            if (c % 2) in cx.mp_du:
                nc.scalar.wait_ge(cx.sv.h, cx.mp_du[c % 2])
            nc.scalar.activation(out=sgbuf, in_=pslot, func=AF.Sigmoid,
                                 scale=-1.0, bias=cc[f"ndtb_{dgn}"][:, c:c + 1])
            nc.scalar.drain()
            op = nc.scalar.activation(out=dnb, in_=sgbuf, func=AF.Ln)
            ts_dn = cx.ss.inc(op)
            cx.slot_set(pkey, cx.ss, ts_dn)
            nc.scalar.drain()
            nc.vector.wait_ge(cx.ss.h, ts_dn)
            du_op = nc.vector.tensor_tensor(out=du_bf, in0=dnb,
                                            in1=xc_fm[par][:, c], op=OP.mult)
            cx.mp_du[c % 2] = cx.sv.inc(du_op)
            nc.vector.drain()
            for hf in range(2):
                epar = cx.mp_half_i % 2
                eb = e_half[epar]
                wb = w_half[epar]
                s0 = hf * SH
                if epar in cx.mp_e_cons:
                    nc.scalar.wait_ge(cx.sv.h, cx.mp_e_cons[epar])
                ei = None
                for j in range(SH):
                    ei = nc.scalar.activation(
                        out=eb[:, j], in_=dnb, func=AF.Exp,
                        scale=cc[f"apos_{dgn}"][:, c, s0 + j:s0 + j + 1])
                tse = cx.ss.inc(ei)
                nc.vector.wait_ge(cx.ss.h, tse)
                nc.vector.tensor_tensor(
                    out=wb[:, :, :],
                    in0=du_bf[:].unsqueeze(1).broadcast_to([128, SH, QT]),
                    in1=bc_all[:, s0:s0 + SH, 0, :], op=OP.mult)
                nc.vector.drain()
                nc.vector.tensor_tensor(out=fix8, in0=eb[:, :, 0],
                                        in1=state[:, c, s0:s0 + SH], op=OP.mult)
                nc.vector.drain()
                nc.vector.tensor_tensor(out=wb[:, :, 0], in0=wb[:, :, 0],
                                        in1=fix8, op=OP.add)
                nc.vector.drain()
                nc.vector.memset(eb[:, :, 0:1], 0.0)
                nc.vector.drain()
                sc = nc.vector.tensor_tensor_scan(
                    out=hall[:, s0:s0 + SH, :].rearrange("p a b -> p (a b)"),
                    data0=eb[:, :, :].rearrange("p a b -> p (a b)"),
                    data1=wb[:, :, :].rearrange("p a b -> p (a b)"),
                    initial=0.0, op0=OP.mult, op1=OP.add)
                cx.mp_e_cons[epar] = cx.sv.inc(sc)
                nc.vector.drain()
                nc.vector.tensor_copy(out=state[:, c, s0:s0 + SH],
                                      in_=hall[:, s0:s0 + SH, QT - 1])
                nc.vector.drain()
                cx.mp_half_i += 1
            hc = nc.vector.tensor_tensor(
                out=hall[:, :, :], in0=hall[:, :, :],
                in1=bc_all[:, :, 1, :], op=OP.mult)
            nc.vector.drain()
            stride = 1
            while stride < DS // 2:
                nw = DS // (2 * stride)
                nc.vector.tensor_tensor(
                    out=hall[:, 0:nw, :].rearrange("p a b -> p (a b)"),
                    in0=hall[:, 0:nw, :].rearrange("p a b -> p (a b)"),
                    in1=hall[:, nw:2 * nw, :].rearrange("p a b -> p (a b)"),
                    op=OP.add)
                nc.vector.drain()
                stride *= 2
            nc.vector.tensor_tensor(out=ys_f, in0=hall[:, 0, :],
                                    in1=hall[:, 1, :], op=OP.add)
            nc.vector.drain()
            nc.vector.scalar_tensor_tensor(out=ys_f, in0=xc_fm[par][:, c],
                                           scalar=cc[f"dd_c_{dgn}"][:, c:c + 1],
                                           in1=ys_f, op0=OP.mult, op1=OP.add)
            nc.vector.drain()
            gi = nc.vector.tensor_tensor(out=zs_fm[par][:, c], in0=ys_f,
                                         in1=zs_fm[par][:, c], op=OP.mult)
            cx.sv.inc(gi)
            if 1 <= c <= 10:
                drive()
        cx.mp_scan_tick = cx.sv.n

    def stage_b2(kk):
        k = NT - 1 - kk if rev else kk
        par = kk % 2
        tick_scan = cx.mp_scan_tick
        nc.tensor.wait_ge(cx.sv.h, tick_scan)
        nc.vector.wait_ge(cx.ld.h, cx.mp_tkp)
        for qg in range(4):
            wv, tl, wkey = stream_w(
                d[f"b_weff_{dgn}"][:, qg * 256:(qg + 1) * 256]
                .rearrange("(a p) c -> p a c", p=128), weff_sl, qg, "weff")
            nc.tensor.wait_ge(cx.ld.h, tl)
            for mi in range(2):
                mg = qg * 2 + mi
                pkey = f"b{mg % 2}"
                pslot = cx.ps[mg % 2][:, 0:QT]
                tt = _fm_matmul(cx, wv, zs_fm[par], range(DIC), mi, pslot, pkey)
                nc.scalar.wait_ge(cx.stn.h, tt)
                cx.slot_wait(nc.scalar, ("pev", mg % 2))
                cp = nc.scalar.copy(out=pevac[mg % 2], in_=pslot)
                tse = cx.ss.inc(cp)
                cx.slot_set(pkey, cx.ss, tse)
                for i in range(NSUB):

                    def consume(psl, tick, i=i, mg=mg):
                        nc.vector.wait_ge(cx.stn.h, tick)
                        di = (sub_map[i] if sub_map else i)
                        dst = p5_tm[:, di, mg * 128:(mg + 1) * 128]
                        op = nc.vector.tensor_tensor(out=dst, in0=dst, in1=psl,
                                                     op=OP.add)
                        return cx.sv, cx.sv.inc(op)

                    nc.tensor.wait_ge(cx.ss.h, tse)
                    _tp128(cx, pevac[mg % 2][:, i * 128:(i + 1) * 128],
                           consume, bf=False, perm=perm, f2=True)
                cx.slot_set(("pev", mg % 2), cx.stn, cx.stn.n)
            cx.slot_set(wkey, cx.stn, cx.stn.n)
        tick_v = cx.vpub()
        nc.sync.wait_ge(cx.sv.h, tick_v)
        cx.store(d[dst_p5][k * QT:(k + 1) * QT, :]
                 .rearrange("(a p) c -> p a c", p=128), p5_tm)

    cx.mp_half_i = 0
    cx.mp_e_cons = {}
    cx.mp_scan_tick = 0
    cx.mp_silu = {}
    cx.mp_du = {}
    cx.mp_trows = {}
    for _ in stage_a_front_gen(0):
        pass
    stage_a_back(0)
    for kk in range(NT):
        gen = stage_a_front_gen(kk + 1) if kk + 1 < NT else None
        stage_b1(kk, gen)
        if kk + 1 < NT:
            stage_a_back(kk + 1)
        stage_b2(kk)
    cx.wpool.release()
    cx.fpool.release()
    cx.bpool.release()


def _pass5(cx):
    nc, d, cc = cx.nc, cx.dram, cx.cc
    NT = cx.NT
    cx.wpool.mark()
    cx.fpool.mark()
    cx.bpool.mark()
    weights = _rwkv_weights(cx)
    xa = cx.fpool.alloc(F32, [NSUB, C])
    dproj = cx.fpool.alloc(F32, [DC, QT])
    xn4 = cx.bpool.alloc(BF16, [NSUB, C])
    xf_buf = cx.bpool.alloc(BF16, [DC, QT + 1])
    xr = cx.bpool.alloc(BF16, [DC, QT])
    rbf = cx.bpool.alloc(BF16, [DC, QT])
    kbuf = cx.bpool.alloc(BF16, [DC, QT])
    carry = cx.bpool.alloc(BF16, [DC, 1])
    rw_bufs = (xn4, xf_buf, xr, rbf, kbuf, dproj)

    tile_end = {}
    for k in range(NT):
        if k >= 1:
            for sem, tick in tile_end[k - 1]:
                for eng in (nc.vector, nc.scalar, nc.tensor):
                    eng.wait_ge(sem.h, tick)
                nc.sync.wait_ge(sem.h, tick)
        tk = cx.load(xa, d["s_x5"][k * QT:(k + 1) * QT, :]
                     .rearrange("(a p) c -> p a c", p=128))
        for eng in (nc.vector, nc.scalar, nc.tensor):
            eng.wait_ge(cx.ld.h, tk)
        _rwkv_delta(cx, rw_bufs, xa, weights, carry, first=(k == 0))
        _residual_add_fm_delta(cx, dproj, xa, NSUB, 0.5)
        tick_v = cx.vpub()
        nc.sync.wait_ge(cx.sv.h, tick_v)
        cx.store(d["y"][k * QT:(k + 1) * QT, :]
                 .rearrange("(a p) c -> p a c", p=128), xa)
        tile_end[k] = [(cx.sv, cx.sv.n), (cx.ss, cx.ss.n), (cx.stn, cx.stn.n),
                       (cx.st, cx.st.n)]
    cx.wpool.release()
    cx.fpool.release()
    cx.bpool.release()
